# version 15
# baseline (speedup 1.0000x reference)
import numpy as np
import ml_dtypes as mld
from contextlib import ExitStack

import concourse.bass as bass
import concourse.mybir as mybir
from concourse.alu_op_type import AluOpType as OP
from concourse.bass_utils import run_bass_kernel_spmd

F32 = mybir.dt.float32
BF16 = mybir.dt.bfloat16
U32 = mybir.dt.uint32
AF = mybir.ActivationFunctionType
AXX = mybir.AxisListType.X
IOA = bass.IndirectOffsetOnAxis

NCORE = 8
D = 2048
T = 2048
S = 4096
NDT = D // 128
NTC = T // 512
ROT = 1024
NEG = -1e9
QSCALE = 192.0 ** -0.5



def _rope_perm(n):
    return np.concatenate([np.arange(0, ROT, 2), np.arange(1, ROT, 2),
                           np.arange(ROT, n)]).astype(np.int64)


def host_prep(inp):
    p = {}
    perm = _rope_perm(D)
    perm_all = np.concatenate([perm, perm + D, perm + 2 * D])
    W_in = np.asarray(inp["L0_attn_W_in"])
    p["winT"] = np.ascontiguousarray(W_in[perm_all].T)
    p["binp"] = np.ascontiguousarray(
        np.asarray(inp["L0_attn_b_in"])[perm_all].reshape(48, 128).T)
    W_out = np.asarray(inp["L0_attn_W_out"])
    p["woutT"] = np.ascontiguousarray(W_out[:, perm].T)
    p["bout"] = np.ascontiguousarray(
        np.asarray(inp["L0_attn_b_out"]).reshape(16, 128).T)
    p["wq"] = np.ascontiguousarray(np.asarray(inp["L0_ffn_Wq"]))
    keys = np.asarray(inp["L0_ffn_keys"])
    p["keysT"] = np.ascontiguousarray(
        keys.transpose(0, 2, 3, 1).reshape(8, 64, 128))
    p["e_cat"] = np.ascontiguousarray(
        np.concatenate([np.asarray(inp["L0_ffn_e_down"]),
                        np.asarray(inp["L0_ffn_e_up"])], axis=1))

    p["qprojT"] = np.ascontiguousarray(np.asarray(inp["L1_attn_q_proj_w"]).T)
    p["kvaT"] = np.ascontiguousarray(np.asarray(inp["L1_attn_kv_a_proj_w"]).T)
    p["lnw"] = np.ascontiguousarray(
        np.asarray(inp["L1_attn_kv_a_ln_w"]).reshape(4, 128).T)
    kvbT = np.asarray(inp["L1_attn_kv_b_proj_w"]).T
    nope_cols = (np.arange(16)[:, None] * 256 + np.arange(128)[None]).reshape(-1)
    p["kvbT_nope"] = np.ascontiguousarray(kvbT[:, nope_cols])
    p["kvbT_v"] = np.ascontiguousarray(kvbT[:, nope_cols + 128])
    p["oprojT"] = np.ascontiguousarray(np.asarray(inp["L1_attn_o_proj_w"]).T)
    p["wguT"] = np.ascontiguousarray(np.asarray(inp["L1_ffn_w_gate_up"]).T)
    p["wprT"] = np.ascontiguousarray(np.asarray(inp["L1_ffn_w_proj"]).T)
    selm = np.zeros((32, 2048), dtype=mld.bfloat16)
    jj = np.arange(2048)
    selm[jj // 64, jj] = 1
    p["selm"] = selm
    indm = np.zeros((128, 254), dtype=mld.bfloat16)
    kk = np.arange(128)
    indm[kk, kk // 64 + 126] = 1
    p["indm"] = indm

    X = np.asarray(inp["L0_attn_X"])
    p["xT"] = [np.ascontiguousarray(X[c // 2, (c % 2) * T:(c % 2 + 1) * T, :].T)
               for c in range(NCORE)]

    invf = 10000.0 ** (-(np.arange(512, dtype=np.float64) * 2.0 / ROT))
    ang = np.arange(S, dtype=np.float64)[None, :] * invf[:, None]
    p["cosT"] = np.cos(ang).astype(mld.bfloat16)
    p["sinT"] = np.sin(ang).astype(mld.bfloat16)
    p["ones1"] = np.ones((1, 128), dtype=mld.bfloat16)
    p["onescol"] = np.ones((128, 1), dtype=mld.bfloat16)
    p["ident"] = np.eye(128, dtype=np.float32).astype(mld.bfloat16)
    p["iota64"] = np.broadcast_to(np.arange(64, dtype=np.float32), (128, 64)).copy()
    masks = np.zeros((4, 128, 512), dtype=np.float32)
    for j in range(4):
        masks[j] = np.where(np.arange(128)[:, None] + 128 * j > np.arange(512)[None, :],
                            NEG, 0.0)
    p["masks"] = masks
    return p



class Sem:
    def __init__(self, nc, ctx, name):
        self.h = ctx.enter_context(nc.semaphore(name))
        self.v = 0

    def inc(self, n=1):
        self.v += n
        return self.v


class Prog:
    def __init__(self, name):
        self.nc = bass.Bass("TRN2", target_bir_lowering=False)
        self.name = name
        self.ctx = ExitStack()
        self.pctx = None
        self.prefix = ""
        self.block = None

    def sem(self, name):
        return Sem(self.nc, self.ctx, self.prefix + name)

    def sb(self, name, shape, dt, ctx=None):
        return (ctx or self.pctx or self.ctx).enter_context(
            self.nc.sbuf_tensor(self.prefix + name, list(shape), dt))

    def ps(self, name, shape, dt=F32, ctx=None):
        return (ctx or self.pctx or self.ctx).enter_context(
            self.nc.psum_tensor(self.prefix + name, list(shape), dt))

    def din(self, name, shape, dt):
        return self.nc.dram_tensor(name, list(shape), dt, kind="ExternalInput").ap()

    def dout(self, name, shape, dt):
        return self.nc.dram_tensor(name, list(shape), dt, kind="ExternalOutput").ap()

    def dtmp(self, name, shape, dt):
        return self.nc.dram_tensor(self.prefix + name, list(shape), dt,
                                   kind="Internal").ap()


def mark(eng, msem):
    eng.drain()
    eng.nop().then_inc(msem.h, 1)
    msem.inc()
    return msem.v


def tmark(t, msem, inst=None):
    (inst if inst is not None else t.nop()).then_inc(msem.h, 1)
    msem.inc()
    return msem.v


def get_io(pr, io, name, shape, dt, out=False):
    if io is not None and name in io:
        return io[name]
    return (pr.dout if out else pr.din)(name, list(shape), dt)


def gate_all(block, gate):
    if gate is None:
        return
    for emit in (block.sync, block.tensor, block.vector, block.scalar,
                 block.gpsimd):
        emit(lambda e, g=gate: e.wait_ge(g[0], g[1]))


class Stream:

    def __init__(self, pr, name, nslot):
        self.sems = [pr.sem(f"{name}{i}") for i in range(nslot)]
        self.n = nslot

    def issue(self, eng, slot, dmas):
        sem = self.sems[slot % self.n]
        for out, in_, kw in dmas:
            eng.dma_start(out=out, in_=in_, **kw).then_inc(sem.h, 16)
            sem.inc(16)
        return (sem.h, sem.v)

    def totals(self):
        return [(s.h, s.v) for s in self.sems]


def wait_all(eng, targets):
    for h, v in targets:
        if v > 0:
            eng.wait_ge(h, v)



def emit_ln(pr, b, src_dram, xnT, ms, eps=1e-5, tag="ln"):
    block = pr.block
    vm, sm, pe = ms["vm"], ms["sm"], ms["pe"]
    ctx = ExitStack()
    xf = [pr.sb(f"{tag}_xf{i}", [128, T], F32, ctx) for i in range(2)]
    xb = pr.sb(f"{tag}_xb", [128, T], BF16, ctx)
    xsq = pr.sb(f"{tag}_xsq", [128, T], BF16, ctx)
    m_row = pr.sb(f"{tag}_m", [1, T], F32, ctx)
    r_row = pr.sb(f"{tag}_r", [1, T], F32, ctx)
    sc_row = pr.sb(f"{tag}_sc", [1, T], F32, ctx)
    rb16 = pr.sb(f"{tag}_rb16", [1, T], BF16, ctx)
    mrb16 = pr.sb(f"{tag}_mrb16", [1, T], BF16, ctx)
    Rb = pr.sb(f"{tag}_Rb", [128, T], BF16, ctx)
    MRb = pr.sb(f"{tag}_MRb", [128, T], BF16, ctx)
    tmp = pr.sb(f"{tag}_tmp", [128, T], F32, ctx)
    psA = pr.ps(f"{tag}_psA", [128, 2048], F32, ctx)
    psB = pr.ps(f"{tag}_psB", [128, 2048], F32, ctx)
    s_x = Stream(pr, f"{tag}sx", 2)
    ocol, o1 = b["ocol"], b["o1"]

    cons = []
    for dt in range(NDT):
        def qf(q, dt=dt):
            if dt >= 2:
                q.wait_ge(vm.h, cons[dt - 2][0])
                q.wait_ge(sm.h, cons[dt - 2][1])
            s_x.issue(q, dt, [(xf[dt % 2][:],
                               src_dram[128 * dt:128 * (dt + 1), :], {})])
        block.sync(qf)
        lt = (s_x.sems[dt % 2].h, s_x.sems[dt % 2].v)

        def vf(v, dt=dt, lt=lt):
            v.wait_ge(*lt)
            if dt >= 1:
                v.wait_ge(pe.h, pe.v)
            v.tensor_copy(xb[:], xf[dt % 2][:])
            mark(v, vm)
        block.vector(vf)

        def sf(s, dt=dt, lt=lt):
            s.wait_ge(*lt)
            if dt >= 1:
                s.wait_ge(pe.h, pe.v)
            s.activation(xsq[:], xf[dt % 2][:], AF.Square)
            mark(s, sm)
        block.scalar(sf)
        cons.append((vm.v, sm.v))

        def tf(t, dt=dt, vv=vm.v, sv=sm.v):
            t.wait_ge(vm.h, vv)
            t.wait_ge(sm.h, sv)
            for tc in range(NTC):
                _mm = t.matmul(psA[0:1, 512 * tc:512 * (tc + 1)], ocol[:],
                         xb[:, 512 * tc:512 * (tc + 1)],
                         start=(dt == 0), stop=(dt == NDT - 1))
                _mm = t.matmul(psB[0:1, 512 * tc:512 * (tc + 1)], ocol[:],
                         xsq[:, 512 * tc:512 * (tc + 1)],
                         start=(dt == 0), stop=(dt == NDT - 1))
            tmark(t, pe, _mm)
        block.tensor(tf)

    def vrows(v, pv=pe.v):
        v.wait_ge(pe.h, pv)
        v.tensor_scalar(m_row[:], psA[0:1, :], 1.0 / D, None, OP.mult)
        v.tensor_scalar(sc_row[:], psB[0:1, :], 1.0 / D, None, OP.mult)
        v.drain()
        v.tensor_tensor(r_row[:], m_row[:], m_row[:], OP.mult)
        v.drain()
        v.tensor_tensor(sc_row[:], sc_row[:], r_row[:], OP.subtract)
        v.drain()
        v.tensor_scalar(sc_row[:], sc_row[:], eps, None, OP.add)
        mark(v, vm)
    block.vector(vrows)

    def srows(s, vv=vm.v):
        s.wait_ge(vm.h, vv)
        s.activation(r_row[:], sc_row[:], AF.Ln)
        s.drain()
        s.activation(r_row[:], r_row[:], AF.Exp, scale=-0.5)
        mark(s, sm)
    block.scalar(srows)

    def vmr(v, sv=sm.v):
        v.wait_ge(sm.h, sv)
        v.tensor_tensor(sc_row[:], m_row[:], r_row[:], OP.mult)
        v.drain()
        v.tensor_copy(rb16[:], r_row[:])
        v.tensor_copy(mrb16[:], sc_row[:])
        mark(v, vm)
    block.vector(vmr)

    def tbc(t, vv=vm.v):
        t.wait_ge(vm.h, vv)
        for tc in range(NTC):
            _mm = t.matmul(psA[:, 512 * tc:512 * (tc + 1)], o1[:],
                     rb16[:, 512 * tc:512 * (tc + 1)], start=True, stop=True)
            _mm = t.matmul(psB[:, 512 * tc:512 * (tc + 1)], o1[:],
                     mrb16[:, 512 * tc:512 * (tc + 1)], start=True, stop=True)
        tmark(t, pe, _mm)
    block.tensor(tbc)

    def vbc(v, pv=pe.v):
        v.wait_ge(pe.h, pv)
        v.tensor_copy(Rb[:], psA[:])
        v.tensor_copy(MRb[:], psB[:])
        mark(v, vm)
    block.vector(vbc)
    bc = vm.v

    marks = []
    for dt in range(NDT):
        def qf2(q, dt=dt):
            q.wait_ge(vm.h, bc if dt < 2 else marks[dt - 2])
            s_x.issue(q, dt, [(xf[dt % 2][:],
                               src_dram[128 * dt:128 * (dt + 1), :], {})])
        block.sync(qf2)
        lt = (s_x.sems[dt % 2].h, s_x.sems[dt % 2].v)

        def vf2(v, dt=dt, lt=lt):
            v.wait_ge(*lt)
            v.tensor_tensor(tmp[:], xf[dt % 2][:], Rb[:], OP.mult)
            v.drain()
            v.tensor_tensor(xnT[dt][:], tmp[:], MRb[:], OP.subtract)
            mark(v, vm)
        block.vector(vf2)
        marks.append(vm.v)

    ctx.close()
    return vm.v



def build_l0a(cut="all", pr=None, io=None, gate=None):
    if pr is None:
        pr = Prog("l0a")
    nc = pr.nc
    xT = get_io(pr, io, "xT", [D, T], F32)
    winT = get_io(pr, io, "winT", [D, 3 * D], F32)
    binp = get_io(pr, io, "binp", [128, 48], F32)
    cosT_d = get_io(pr, io, "cosT", [512, T], BF16)
    sinT_d = get_io(pr, io, "sinT", [512, T], BF16)
    wq_d = get_io(pr, io, "wq", [D, 512], F32)
    keysT = get_io(pr, io, "keysT", [8, 64, 128], F32)
    ones1 = get_io(pr, io, "ones1", [1, 128], BF16)
    onescol = get_io(pr, io, "onescol", [128, 1], BF16)
    iota64 = get_io(pr, io, "iota64", [128, 64], F32)

    xnT_o = get_io(pr, io, "xnT_o", [D, T], BF16, out=True)
    qT_o = get_io(pr, io, "qT_o", [D, T], BF16, out=True)
    apart_o = get_io(pr, io, "apart_o", [128, 16], F32, out=True)
    fi_o = get_io(pr, io, "fi_o", [16, 128, 32], U32, out=True)
    fs_o = get_io(pr, io, "fs_o", [16, 128, 32], F32, out=True)

    with pr.ctx:
        xnT = [pr.sb(f"xnT{i}", [128, T], BF16) for i in range(NDT)]
        o1 = pr.sb("o1", [1, 128], BF16)
        ocol = pr.sb("ocol", [128, 1], BF16)
        bcol = pr.sb("bcol", [128, 48], F32)
        acol = pr.sb("acol", [128, 16], F32)

        pe = pr.sem("pe")
        vm = pr.sem("vm")
        sm = pr.sem("sm")
        st = pr.sem("st")
        ms = {"pe": pe, "vm": vm, "sm": sm, "st": st}
        s_const = Stream(pr, "cst", 1)
        s_w = Stream(pr, "sw", 2)
        s_cs = Stream(pr, "scs", 2)
        s_wq = Stream(pr, "swq", 1)
        s_qst = Stream(pr, "sqst", 1)
        s_tk = Stream(pr, "stk", 1)

        if pr.block is None:
            pr.block = pr.ctx.enter_context(nc.Block())
        block = pr.block
        gate_all(block, gate)
        b = {"o1": o1, "ocol": ocol}

        def qconst(q):
            s_const.issue(q, 0, [
                (o1[:], ones1[:], {}),
                (ocol[:], onescol[:], {}),
                (bcol[:], binp[:], {}),
            ])
        block.sync(qconst)
        const_t = s_const.totals()[0]

        def vconst(v):
            v.wait_ge(*const_t)
            mark(v, vm)
        block.vector(vconst)

        emit_ln(pr, b, xT, xnT, ms)
        xn_done = vm.v

        def qxn(q):
            q.wait_ge(vm.h, xn_done)
            for dt in range(NDT):
                q.dma_start(out=xnT_o[128 * dt:128 * (dt + 1), :],
                            in_=xnT[dt][:]).then_inc(st.h, 16)
                st.inc(16)
        block.sync(qxn)

        if cut == "ln":
            def qfin0(q):
                q.wait_ge(st.h, st.v)
            block.sync(qfin0)
            pr.ctx.pop_all()
            return pr

        ctx2 = ExitStack()
        ktT = [pr.sb(f"ktT{i}", [128, T], BF16, ctx2) for i in range(NDT)]
        cosb = [pr.sb(f"cosb{i}", [128, T], BF16, ctx2) for i in range(2)]
        sinb = [pr.sb(f"sinb{i}", [128, T], BF16, ctx2) for i in range(2)]
        p1 = pr.sb("p1", [128, T], F32, ctx2)
        p2 = pr.sb("p2", [128, T], F32, ctx2)
        sqt = pr.sb("sqt", [128, T], BF16, ctx2)
        vtile = pr.sb("vtile", [128, T], BF16, ctx2)
        fkt = pr.sb("fkt", [128, T], BF16, ctx2)
        rrow = pr.sb("rrow", [1, T], F32, ctx2)
        rn16 = pr.sb("rn16", [1, T], BF16, ctx2)
        RNb = pr.sb("RNb", [128, T], BF16, ctx2)
        scr = pr.sb("scr", [128, T], BF16, ctx2)
        qtile = pr.sb("qtile", [128, T], BF16, ctx2)
        wbuf = [pr.sb(f"wb{i}", [128, NDT * 128], BF16, ctx2) for i in range(2)]
        psA = pr.ps("psA", [128, 2048], F32, ctx2)
        psB = pr.ps("psB", [128, 2048], F32, ctx2)

        cs_t = {}

        def csload(q, i, buf):
            cs_t[i] = s_cs.issue(q, buf, [
                (cosb[buf][:], cosT_d[128 * i:128 * (i + 1), :], {}),
                (sinb[buf][:], sinT_d[128 * i:128 * (i + 1), :], {}),
            ])

        def qcs(q, xv=xn_done):
            q.wait_ge(vm.h, xv)
            csload(q, 0, 0)
            csload(q, 1, 1)
        block.sync(qcs)

        w_t = {}

        def wload(g, e, buf):
            w_t[e] = s_w.issue(g, buf, [(
                wbuf[buf][:].rearrange("p (dt c) -> p dt c", c=128),
                winT[:, 128 * e:128 * (e + 1)].rearrange("(dt p) c -> p dt c", p=128),
                {})])

        def mm_echunk(t, e, psX, buf):
            t.wait_ge(*w_t[e])
            for dt in range(NDT):
                for tc in range(NTC):
                    _mm = t.matmul(psX[:, 512 * tc:512 * (tc + 1)],
                             wbuf[buf][:, dt * 128:dt * 128 + 128],
                             xnT[dt][:, 512 * tc:512 * (tc + 1)],
                             start=(dt == 0), stop=(dt == NDT - 1))
            return tmark(t, pe, _mm)

        def gk0(g, xv=xn_done):
            g.wait_ge(vm.h, xv)
            wload(g, 16, 0)
            wload(g, 20, 1)
        block.gpsimd(gk0)

        for i in range(4):
            def tk(t, i=i):
                if i > 0:
                    t.wait_ge(vm.h, vm.v)
                mm_echunk(t, 16 + i, psA, 0)
                mm_echunk(t, 20 + i, psB, 1)
            block.tensor(tk)
            pair_mm = pe.v

            if i < 3:
                def gk(g, i=i, pm=pair_mm):
                    g.wait_ge(pe.h, pm)
                    wload(g, 16 + i + 1, 0)
                    wload(g, 20 + i + 1, 1)
                block.gpsimd(gk)

            def vk(v, i=i, pm=pair_mm):
                v.wait_ge(pe.h, pm)
                v.wait_ge(*cs_t[i])
                cb, sb_ = cosb[i % 2], sinb[i % 2]
                v.scalar_tensor_tensor(p1[:], psA[:], bcol[:, 16 + i:17 + i], cb[:],
                                       OP.add, OP.mult)
                v.scalar_tensor_tensor(p2[:], psB[:], bcol[:, 20 + i:21 + i], sb_[:],
                                       OP.add, OP.mult)
                v.drain()
                v.tensor_tensor(ktT[i][:], p1[:], p2[:], OP.subtract)
                v.drain()
                v.scalar_tensor_tensor(p1[:], psB[:], bcol[:, 20 + i:21 + i], cb[:],
                                       OP.add, OP.mult)
                v.scalar_tensor_tensor(p2[:], psA[:], bcol[:, 16 + i:17 + i], sb_[:],
                                       OP.add, OP.mult)
                v.drain()
                v.tensor_tensor(ktT[4 + i][:], p1[:], p2[:], OP.add)
                mark(v, vm)
            block.vector(vk)

            if 0 < i < 3:
                def qcs2(q, i=i, vv=vm.v):
                    q.wait_ge(vm.h, vv)
                    csload(q, i + 1, (i + 1) % 2)
                block.sync(qcs2)

        if cut == "kpair":
            def qfin2(q, vv=vm.v):
                q.wait_ge(vm.h, vv)
                q.wait_ge(st.h, st.v)
            block.sync(qfin2)
            pr.ctx.pop_all()
            return pr

        def gkp(g, pv=pe.v):
            g.wait_ge(pe.h, pv)
            wload(g, 24, 0)
            wload(g, 25, 1)
        block.gpsimd(gkp)

        for j in range(8):
            def tkp(t, j=j, vv=vm.v):
                t.wait_ge(vm.h, vv if j == 0 else vm.v)
                mm_echunk(t, 24 + j, psA if j % 2 == 0 else psB, j % 2)
            block.tensor(tkp)
            mmv = pe.v
            if j < 6:
                def gkp2(g, j=j, mmv=mmv):
                    g.wait_ge(pe.h, mmv)
                    wload(g, 26 + j, j % 2)
                block.gpsimd(gkp2)

            def vkp(v, j=j, mmv=mmv):
                v.wait_ge(pe.h, mmv)
                v.tensor_scalar(ktT[8 + j][:], (psA if j % 2 == 0 else psB)[:],
                                bcol[:, 24 + j:25 + j], None, OP.add)
                mark(v, vm)
            block.vector(vkp)
        k_done = vm.v

        for dt in range(NDT):
            def sq(s, dt=dt, kd=k_done):
                s.wait_ge(vm.h, kd)
                if dt > 0:
                    s.wait_ge(pe.h, pe.v)
                s.activation(sqt[:], ktT[dt][:], AF.Square)
                mark(s, sm)
            block.scalar(sq)

            def tn(t, dt=dt, sv=sm.v):
                t.wait_ge(sm.h, sv)
                for tc in range(NTC):
                    _mm = t.matmul(psA[0:1, 512 * tc:512 * (tc + 1)], ocol[:],
                             sqt[:, 512 * tc:512 * (tc + 1)],
                             start=(dt == 0), stop=(dt == NDT - 1))
                tmark(t, pe, _mm)
            block.tensor(tn)

        def vnorm(v, pv=pe.v):
            v.wait_ge(pe.h, pv)
            v.tensor_scalar(rrow[:], psA[0:1, :], 1.0, 1e-24, OP.mult, OP.max)
            mark(v, vm)
        block.vector(vnorm)

        def snorm(s, vv=vm.v):
            s.wait_ge(vm.h, vv)
            s.activation(rrow[:], rrow[:], AF.Ln)
            s.drain()
            s.activation(rrow[:], rrow[:], AF.Exp, scale=-0.5)
            mark(s, sm)
        block.scalar(snorm)

        def vnorm2(v, sv=sm.v):
            v.wait_ge(sm.h, sv)
            v.tensor_scalar(rrow[:], rrow[:], 1e12, None, OP.min)
            v.drain()
            v.tensor_copy(rn16[:], rrow[:])
            mark(v, vm)
        block.vector(vnorm2)

        def trb(t, vv=vm.v):
            t.wait_ge(vm.h, vv)
            for tc in range(NTC):
                _mm = t.matmul(psA[:, 512 * tc:512 * (tc + 1)], o1[:],
                         rn16[:, 512 * tc:512 * (tc + 1)], start=True, stop=True)
            tmark(t, pe, _mm)
        block.tensor(trb)

        def vrb(v, pv=pe.v):
            v.wait_ge(pe.h, pv)
            v.tensor_copy(RNb[:], psA[:])
            v.memset(acol[:], 0.0)
            mark(v, vm)
        block.vector(vrb)
        rn_done = vm.v

        def gv0(g, pv=pe.v):
            g.wait_ge(pe.h, pv)
            wload(g, 32, 0)
            wload(g, 33, 1)
        block.gpsimd(gv0)

        for j in range(NDT):
            def tv(t, j=j, rd=rn_done):
                t.wait_ge(vm.h, rd if j == 0 else vm.v)
                mm_echunk(t, 32 + j, psB, j % 2)
            block.tensor(tv)
            mmv = pe.v
            if j < NDT - 2:
                def gv(g, j=j, mmv=mmv):
                    g.wait_ge(pe.h, mmv)
                    wload(g, 34 + j, j % 2)
                block.gpsimd(gv)

            def vv(v, j=j, mmv=mmv):
                v.wait_ge(pe.h, mmv)
                v.tensor_scalar(vtile[:], psB[:], bcol[:, 32 + j:33 + j], None, OP.add)
                v.drain()
                v.tensor_tensor(fkt[:], ktT[j][:], RNb[:], OP.mult)
                v.drain()
                v.scalar_tensor_tensor(scr[:], fkt[:], 1.0, vtile[:],
                                       OP.mult, OP.mult,
                                       accum_out=acol[:, j:j + 1])
                mark(v, vm)
            block.vector(vv)
        a_done = vm.v

        def qa(q):
            q.wait_ge(vm.h, a_done)
            q.dma_start(out=apart_o[:], in_=acol[:]).then_inc(st.h, 16)
            st.inc(16)
        block.sync(qa)

        def gq0(g, pv=pe.v):
            g.wait_ge(pe.h, pv)
            wload(g, 0, 0)
            wload(g, 4, 1)
        block.gpsimd(gq0)

        def qcsq(q, av=a_done):
            q.wait_ge(vm.h, av)
            csload(q, 0, 0)
            csload(q, 1, 1)
        block.sync(qcsq)

        for i in range(4):
            def tq(t, i=i, av=a_done):
                t.wait_ge(vm.h, av if i == 0 else vm.v)
                mm_echunk(t, i, psA, 0)
                mm_echunk(t, 4 + i, psB, 1)
            block.tensor(tq)
            pair_mm = pe.v
            if i < 3:
                def gq(g, i=i, pm=pair_mm):
                    g.wait_ge(pe.h, pm)
                    wload(g, i + 1, 0)
                    wload(g, 4 + i + 1, 1)
                block.gpsimd(gq)

            def vq(v, i=i, pm=pair_mm):
                v.wait_ge(pe.h, pm)
                v.wait_ge(*cs_t[i])
                cb, sb_ = cosb[i % 2], sinb[i % 2]
                v.scalar_tensor_tensor(p1[:], psA[:], bcol[:, i:i + 1], cb[:],
                                       OP.add, OP.mult)
                v.scalar_tensor_tensor(p2[:], psB[:], bcol[:, 4 + i:5 + i], sb_[:],
                                       OP.add, OP.mult)
                v.drain()
                v.tensor_tensor(qtile[:], p1[:], p2[:], OP.subtract)
                mark(v, vm)
            block.vector(vq)

            def qq(q, i=i, ev=vm.v):
                q.wait_ge(vm.h, ev)
                s_qst.issue(q, 0, [(qT_o[128 * i:128 * (i + 1), :], qtile[:], {})])
            block.sync(qq)
            qst_t = s_qst.totals()[0]

            def vq2(v, i=i, qt=qst_t):
                v.wait_ge(*qt)
                cb, sb_ = cosb[i % 2], sinb[i % 2]
                v.scalar_tensor_tensor(p1[:], psB[:], bcol[:, 4 + i:5 + i], cb[:],
                                       OP.add, OP.mult)
                v.scalar_tensor_tensor(p2[:], psA[:], bcol[:, i:i + 1], sb_[:],
                                       OP.add, OP.mult)
                v.drain()
                v.tensor_tensor(qtile[:], p1[:], p2[:], OP.add)
                mark(v, vm)
            block.vector(vq2)

            def qq2(q, i=i, ev=vm.v):
                q.wait_ge(vm.h, ev)
                s_qst.issue(q, 0, [(qT_o[128 * (4 + i):128 * (5 + i), :], qtile[:], {})])
            block.sync(qq2)
            qst_t2 = s_qst.totals()[0]

            def vq3(v, qt=qst_t2):
                v.wait_ge(*qt)
                mark(v, vm)
            block.vector(vq3)

            if 0 < i < 3:
                def qcsq2(q, i=i, vv=vm.v):
                    q.wait_ge(vm.h, vv)
                    csload(q, i + 1, (i + 1) % 2)
                block.sync(qcsq2)

        def gqp(g, pv=pe.v):
            g.wait_ge(pe.h, pv)
            wload(g, 8, 0)
            wload(g, 9, 1)
        block.gpsimd(gqp)

        for j in range(8):
            def tqp(t, j=j):
                t.wait_ge(vm.h, vm.v)
                mm_echunk(t, 8 + j, psA if j % 2 == 0 else psB, j % 2)
            block.tensor(tqp)
            mmv = pe.v
            if j < 6:
                def gqp2(g, j=j, mmv=mmv):
                    g.wait_ge(pe.h, mmv)
                    wload(g, 10 + j, j % 2)
                block.gpsimd(gqp2)

            def vqp(v, j=j, mmv=mmv):
                v.wait_ge(pe.h, mmv)
                v.tensor_scalar(qtile[:], (psA if j % 2 == 0 else psB)[:],
                                bcol[:, 8 + j:9 + j], None, OP.add)
                mark(v, vm)
            block.vector(vqp)

            def qqp(q, j=j, ev=vm.v):
                q.wait_ge(vm.h, ev)
                s_qst.issue(q, 0, [(qT_o[128 * (8 + j):128 * (9 + j), :], qtile[:], {})])
            block.sync(qqp)
            qst_t = s_qst.totals()[0]

            def vqp2(v, qt=qst_t):
                v.wait_ge(*qt)
                mark(v, vm)
            block.vector(vqp2)

        ctx2.close()
        qkv_end = vm.v

        if cut == "qkv":
            def qfin1(q):
                q.wait_ge(vm.h, qkv_end)
                q.wait_ge(st.h, st.v)
                wait_all(q, s_qst.totals())
            block.sync(qfin1)
            pr.ctx.pop_all()
            return pr

        ctx3 = ExitStack()
        qfT = [pr.sb(f"qfT{i}", [128, T], BF16, ctx3) for i in range(4)]
        wqb = pr.sb("wqb", [128, NDT * 512], BF16, ctx3)
        keysb = pr.sb("keysb", [128, 8 * 128], BF16, ctx3)
        iob = pr.sb("iob", [128, 64], F32, ctx3)
        s1 = pr.sb("s1", [128, 8], F32, ctx3)
        s2 = pr.sb("s2", [128, 8], F32, ctx3)
        i1 = pr.sb("i1", [128, 8], U32, ctx3)
        i2 = pr.sb("i2", [128, 8], U32, ctx3)
        i1f = pr.sb("i1f", [128, 8], F32, ctx3)
        i2f = pr.sb("i2f", [128, 8], F32, ctx3)
        i1m = pr.sb("i1m", [128, 8], F32, ctx3)
        asc = pr.sb("asc", [128, 64], F32, ctx3)
        aidx = pr.sb("aidx", [128, 64], F32, ctx3)
        fs_t = pr.sb("fs_t", [128, 32], F32, ctx3)
        fsh = pr.sb("fsh", [128, 8], F32, ctx3)
        pk_t = pr.sb("pk_t", [128, 8], U32, ctx3)
        pkf = pr.sb("pkf", [128, 8], F32, ctx3)
        fi_f = pr.sb("fi_f", [128, 32], F32, ctx3)
        fi_u = pr.sb("fi_u", [128, 32], U32, ctx3)
        eqm = pr.sb("eqm", [128, 64], F32, ctx3)
        prodm = pr.sb("prodm", [128, 64], F32, ctx3)
        psF = pr.ps("psF", [128, 2048], F32, ctx3)
        psS = pr.ps("psS", [128, 2048], F32, ctx3)

        def gw(g, vv=qkv_end):
            g.wait_ge(vm.h, vv)
            s_wq.issue(g, 0, [
                (wqb[:].rearrange("p (dt c) -> p dt c", c=512),
                 wq_d.rearrange("(dt p) c -> p dt c", p=128), {}),
                (keysb[0:64, :].rearrange("p (a c) -> p a c", c=128),
                 keysT.rearrange("a b c -> b a c"), {}),
                (keysb[64:128, :].rearrange("p (a c) -> p a c", c=128),
                 keysT.rearrange("a b c -> b a c"), {}),
                (iob[:], iota64[:], {}),
            ])
        block.gpsimd(gw)
        wq_t = s_wq.totals()[0]

        for cc in range(4):
            def tqf(t, cc=cc, wt=wq_t, vv=qkv_end):
                t.wait_ge(*wt)
                t.wait_ge(vm.h, vv)
                if cc > 0:
                    t.wait_ge(sm.h, sm.v)
                for dt in range(NDT):
                    for tc in range(NTC):
                        _mm = t.matmul(psF[:, 512 * tc:512 * (tc + 1)],
                                 wqb[:, dt * 512 + cc * 128:dt * 512 + cc * 128 + 128],
                                 xnT[dt][:, 512 * tc:512 * (tc + 1)],
                                 start=(dt == 0), stop=(dt == NDT - 1))
                tmark(t, pe, _mm)
            block.tensor(tqf)

            def sqf(s, cc=cc, pv=pe.v):
                s.wait_ge(pe.h, pv)
                s.copy(qfT[cc][:], psF[:])
                mark(s, sm)
            block.scalar(sqf)
        qf_done = sm.v

        if cut == "qf":
            def qfinq(q):
                q.wait_ge(sm.h, qf_done)
                q.wait_ge(st.h, st.v)
            block.sync(qfinq)
            pr.ctx.pop_all()
            return pr

        for tb in range(NDT):
            def tsim(t, tb=tb, qd=qf_done, wt=wq_t):
                t.wait_ge(sm.h, qd)
                t.wait_ge(*wt)
                if tb > 0:
                    t.wait_ge(vm.h, vm.v)
                for h in range(4):
                    for w in range(2):
                        b_ = 2 * h + w
                        psX = psF if b_ < 4 else psS
                        _mm = t.matmul(psX[:, 512 * (b_ % 4):512 * (b_ % 4) + 128],
                                 qfT[h][64 * w:64 * (w + 1), 128 * tb:128 * (tb + 1)],
                                 keysb[64 * w:64 * (w + 1),
                                       (2 * h + w) * 128:(2 * h + w + 1) * 128],
                                 start=True, stop=True)
                tmark(t, pe, _mm)
            block.tensor(tsim)

            def vtk(v, tb=tb, pv=pe.v):
                v.wait_ge(pe.h, pv)
                for h in range(4):
                    b1, b2 = 2 * h, 2 * h + 1
                    sl1 = (psF if b1 < 4 else psS)[:, 512 * (b1 % 4):512 * (b1 % 4) + 128]
                    sl2 = (psF if b2 < 4 else psS)[:, 512 * (b2 % 4):512 * (b2 % 4) + 128]
                    v.max(s1[:], sl1)
                    v.max(s2[:], sl2)
                    v.drain()
                    v.max_index(i1[:], s1[:], sl1)
                    v.max_index(i2[:], s2[:], sl2)
                    v.drain()
                    v.tensor_copy(i1f[:], i1[:])
                    v.tensor_copy(i2f[:], i2[:])
                    v.drain()
                    v.tensor_scalar(i1m[:], i1f[:], 128.0, None, OP.mult)
                    v.drain()
                    for j in range(8):
                        v.tensor_scalar(asc[:, 8 * j:8 * j + 8], s2[:],
                                        s1[:, j:j + 1], None, OP.add)
                        v.tensor_scalar(aidx[:, 8 * j:8 * j + 8], i2f[:],
                                        i1m[:, j:j + 1], None, OP.add)
                    v.drain()
                    v.max(fsh[:], asc[:])
                    v.drain()
                    v.max_index(pk_t[:], fsh[:], asc[:])
                    v.tensor_copy(fs_t[:, 8 * h:8 * h + 8], fsh[:])
                    v.drain()
                    v.tensor_copy(pkf[:], pk_t[:])
                    v.drain()
                    for j in range(8):
                        v.tensor_scalar(eqm[:], iob[:], pkf[:, j:j + 1],
                                        None, OP.is_equal)
                        v.drain()
                        v.tensor_tensor(prodm[:], eqm[:], aidx[:], OP.mult)
                        v.drain()
                        v.tensor_reduce(fi_f[:, 8 * h + j:8 * h + j + 1], prodm[:],
                                        axis=AXX, op=OP.add)
                        v.drain()
                v.tensor_copy(fi_u[:], fi_f[:])
                mark(v, vm)
            block.vector(vtk)

            def qtk(q, tb=tb, ev=vm.v):
                q.wait_ge(vm.h, ev)
                s_tk.issue(q, 0, [(fi_o[tb], fi_u[:], {}),
                                  (fs_o[tb], fs_t[:], {})])
            block.sync(qtk)
            tk_t = s_tk.totals()[0]

            def vtk2(v, tt=tk_t):
                v.wait_ge(*tt)
                mark(v, vm)
            block.vector(vtk2)

        ctx3.close()

        def qfin(q):
            q.wait_ge(st.h, st.v)
            wait_all(q, s_qst.totals())
            wait_all(q, s_tk.totals())
        block.sync(qfin)
        pr.finish = [(pe.h, pe.v), (vm.h, vm.v), (sm.h, sm.v), (st.h, st.v)]
        pr.ctx.pop_all()

    return pr



def l0a_inmaps(prep):
    cos, sin = prep["cosT"], prep["sinT"]
    maps = []
    for c in range(NCORE):
        r = c % 2
        maps.append({
            "xT": prep["xT"][c],
            "winT": prep["winT"],
            "binp": prep["binp"],
            "cosT": np.ascontiguousarray(cos[:, r * T:(r + 1) * T]),
            "sinT": np.ascontiguousarray(sin[:, r * T:(r + 1) * T]),
            "wq": prep["wq"],
            "keysT": prep["keysT"],
            "ones1": prep["ones1"],
            "onescol": prep["onescol"],
            "iota64": prep["iota64"],
        })
    return maps





def build_l0b(pr=None, io=None, gate=None):
    if pr is None:
        pr = Prog("l0b")
    nc = pr.nc
    summed = io is not None and "ap1" not in io
    xT = get_io(pr, io, "xT", [D, T], F32)
    xnT_i = get_io(pr, io, "xnT_i", [D, T], BF16)
    qT_i = get_io(pr, io, "qT_i", [D, T], BF16)
    ap0 = get_io(pr, io, "ap0", [128, 16], F32)
    ap1 = None if summed else get_io(pr, io, "ap1", [128, 16], F32)
    fi_i = get_io(pr, io, "fi_i", [16, 128, 32], U32)
    fs_i = get_io(pr, io, "fs_i", [16, 128, 32], F32)
    ecat = get_io(pr, io, "ecat", [16384, 4096], F32)
    woutT = get_io(pr, io, "woutT", [D, D], F32)
    bout_d = get_io(pr, io, "bout_d", [128, 16], F32)
    ident_d = get_io(pr, io, "ident_d", [128, 128], BF16)
    ones1 = get_io(pr, io, "ones1", [1, 128], BF16)
    onescol = get_io(pr, io, "onescol", [128, 1], BF16)

    x1T_o = get_io(pr, io, "x1T_o", [D, T], F32, out=True)
    xn2T_o = get_io(pr, io, "xn2T_o", [D, T], BF16, out=True)
    x1s = io["x1s"] if (io is not None and "x1s" in io) else pr.dtmp("x1s", [D, T], F32)

    with pr.ctx:
        xnT = [pr.sb(f"xnT{i}", [128, T], BF16) for i in range(NDT)]
        fT = [pr.sb(f"fT{i}", [128, T], BF16) for i in range(NDT)]
        o1 = pr.sb("o1", [1, 128], BF16)
        ocol = pr.sb("ocol", [128, 1], BF16)
        ident = pr.sb("ident", [128, 128], BF16)
        bout = pr.sb("bout", [128, 16], F32)
        acol = pr.sb("acol", [128, 16], F32)
        aps = [pr.sb(f"aps{i}", [128, 16], F32) for i in range(2)]

        pe = pr.sem("pe")
        vm = pr.sem("vm")
        sm = pr.sem("sm")
        st = pr.sem("st")
        ms = {"pe": pe, "vm": vm, "sm": sm, "st": st}
        s_const = Stream(pr, "cst", 1)
        s_xn = Stream(pr, "sxn", 1)
        s_g = Stream(pr, "sg", 4)
        s_tk = Stream(pr, "stk", 2)
        s_w = Stream(pr, "sw", 2)
        s_q = Stream(pr, "sq", 2)
        s_xf = Stream(pr, "sxf", 2)

        if pr.block is None:
            pr.block = pr.ctx.enter_context(nc.Block())
        block = pr.block
        gate_all(block, gate)
        b = {"o1": o1, "ocol": ocol}

        def qc(q):
            dmas = [
                (o1[:], ones1[:], {}),
                (ocol[:], onescol[:], {}),
                (ident[:], ident_d[:], {}),
                (bout[:], bout_d[:], {}),
                (aps[0][:], ap0[:], {}),
            ]
            if not summed:
                dmas.append((aps[1][:], ap1[:], {}))
            s_const.issue(q, 0, dmas)
            for dt in range(NDT):
                s_xn.issue(q, 0, [(xnT[dt][:], xnT_i[128 * dt:128 * (dt + 1), :], {})])
        block.sync(qc)
        const_t = s_const.totals()[0]
        xn_t = s_xn.totals()[0]

        def va(v):
            v.wait_ge(*const_t)
            if summed:
                v.tensor_copy(acol[:], aps[0][:])
            else:
                v.tensor_tensor(acol[:], aps[0][:], aps[1][:], OP.add)
            mark(v, vm)
        block.vector(va)
        a_done = vm.v

        ctx1 = ExitStack()
        xnat = pr.sb("xnat", [128, D], BF16, ctx1)
        gbuf = [pr.sb(f"gbuf{i}", [128, 4096], BF16, ctx1) for i in range(4)]
        dscr = [pr.sb(f"dscr{i}", [128, D], BF16, ctx1) for i in range(2)]
        diag = [pr.sb(f"diag{i}", [128, 128], BF16, ctx1) for i in range(2)]
        idxb = pr.sb("idxb", [128, 32], U32, ctx1)
        fsb = pr.sb("fsb", [128, 32], F32, ctx1)
        dcol = pr.sb("dcol", [128, 32], F32, ctx1)
        sil = pr.sb("sil", [128, 32], F32, ctx1)
        hcol = pr.sb("hcol", [128, 32], F32, ctx1)
        fnat = pr.sb("fnat", [128, D], BF16, ctx1)
        psU = pr.ps("psU", [128, 2048], F32, ctx1)
        psT = pr.ps("psT", [128, 128], BF16, ctx1)

        vmF, smF, peF = vm.v, sm.v, pe.v
        pvN, svN, vvN = 64, 65, 96

        def dpos(k):
            return 1 if k == 0 else 3 * k - 1

        def hpos(j):
            return 3 * j + 3 if j <= 30 else 95

        def gpos(j):
            return 3 * j + 4 if j <= 30 else 96

        def qtkb(q):
            for tb in range(NDT):
                if tb >= 1:
                    q.wait_ge(sm.h, smF + svN * (tb - 1) + 49)
                    q.wait_ge(vm.h, vmF + vvN * (tb - 1) + 96)
                s_tk.issue(q, tb, [(idxb[:], fi_i[tb], {}),
                                   (fsb[:], fs_i[tb], {})])
        block.sync(qtkb)

        def gkb(g):
            for tb in range(NDT):
                tkt = (s_tk.sems[tb % 2].h, 32 * (tb // 2 + 1))
                for k in range(32):
                    gi = 32 * tb + k
                    if k == 0:
                        g.wait_ge(*tkt)
                    if gi >= 4:
                        tbp, kp = (gi - 4) // 32, (gi - 4) % 32
                        g.wait_ge(pe.h, peF + pvN * tbp + 16 + kp + 1)
                    g.indirect_dma_start(
                        out=gbuf[k % 4][:], out_offset=None,
                        in_=ecat[:, :],
                        in_offset=IOA(ap=idxb[:, k:k + 1], axis=0),
                    ).then_inc(s_g.sems[k % 4].h, 16)
                    s_g.sems[k % 4].inc(16)
        block.gpsimd(gkb)

        def txb(t, xv=xn_t, cv=const_t):
            t.wait_ge(*xv)
            t.wait_ge(*cv)
            for tb in range(NDT):
                peb, smb, vmb = peF + pvN * tb, smF + svN * tb, vmF + vvN * tb
                if tb >= 1:
                    t.wait_ge(vm.h, vmb - 4)
                    t.wait_ge(sm.h, smb)
                for dt in range(NDT):
                    if dt >= 1:
                        t.wait_ge(sm.h, smb + dt)
                    t.transpose(psT[:], xnT[dt][:, 128 * tb:128 * (tb + 1)],
                                ident[:]).then_inc(pe.h, 1)
                    pe.inc()
                for k in range(32):
                    t.wait_ge(vm.h, vmb + gpos(k))
                    t.wait_ge(s_g.sems[k % 4].h, 16 * (8 * tb + k // 4 + 1))
                    for tc in range(4):
                        _mm = t.matmul(psU[:, 512 * tc:512 * (tc + 1)],
                                       diag[k % 2][:],
                                       gbuf[k % 4][:, 2048 + 512 * tc:
                                                   2048 + 512 * (tc + 1)],
                                       start=(k == 0), stop=(k == 31))
                    tmark(t, pe, _mm)
                t.wait_ge(sm.h, smb + 49)
                for dt in range(NDT):
                    if dt >= 1:
                        t.wait_ge(sm.h, smb + 49 + dt)
                    t.transpose(psT[:], fnat[:, 128 * dt:128 * (dt + 1)],
                                ident[:]).then_inc(pe.h, 1)
                    pe.inc()
        block.tensor(txb)

        def sxb(s):
            for tb in range(NDT):
                peb, smb, vmb = peF + pvN * tb, smF + svN * tb, vmF + vvN * tb
                for dt in range(NDT):
                    s.wait_ge(pe.h, peb + dt + 1)
                    _c = s.copy(xnat[:, 128 * dt:128 * (dt + 1)], psT[:])
                    _c.then_inc(sm.h, 1)
                    sm.inc()
                for k in range(32):
                    s.wait_ge(vm.h, vmb + dpos(k))
                    _a = s.activation(sil[:, k:k + 1], dcol[:, k:k + 1],
                                      AF.Silu)
                    _a.then_inc(sm.h, 1)
                    sm.inc()
                s.wait_ge(pe.h, peb + 48)
                _c = s.copy(fnat[:], psU[:])
                _c.then_inc(sm.h, 1)
                sm.inc()
                for dt in range(NDT):
                    s.wait_ge(pe.h, peb + 48 + dt + 1)
                    _c = s.copy(fT[dt][:, 128 * tb:128 * (tb + 1)], psT[:])
                    _c.then_inc(sm.h, 1)
                    sm.inc()
        block.scalar(sxb)

        def vdb(v):
            def dot(v, tb, k, smb, vmb):
                v.wait_ge(s_g.sems[k % 4].h, 16 * (8 * tb + k // 4 + 1))
                if k == 0:
                    v.wait_ge(sm.h, smb + 16)
                    if tb >= 1:
                        v.wait_ge(sm.h, smb - svN + 48)
                if k >= 2:
                    v.wait_ge(vm.h, vmb + dpos(k - 2))
                elif tb >= 1:
                    v.wait_ge(vm.h, vmb - 4)
                _d = v.scalar_tensor_tensor(dscr[k % 2][:],
                                            gbuf[k % 4][:, 0:2048],
                                            1.0, xnat[:], OP.mult, OP.mult,
                                            accum_out=dcol[:, k:k + 1])
                _d.then_inc(vm.h, 1)
                vm.inc()

            def hd(v, j, peb, smb, vmb):
                v.wait_ge(sm.h, smb + 16 + j + 1)
                _h = v.scalar_tensor_tensor(hcol[:, j:j + 1], fsb[:, j:j + 1],
                                            0.0, sil[:, j:j + 1],
                                            OP.max, OP.mult)
                _h.then_inc(vm.h, 1)
                hv = vm.inc()
                if j >= 2:
                    v.wait_ge(pe.h, peb + 16 + j - 1)
                v.wait_ge(vm.h, hv)
                _g = v.tensor_scalar(diag[j % 2][:], ident[:],
                                     hcol[:, j:j + 1], None, OP.mult)
                _g.then_inc(vm.h, 1)
                vm.inc()

            for tb in range(NDT):
                peb, smb, vmb = peF + pvN * tb, smF + svN * tb, vmF + vvN * tb
                for k in range(32):
                    dot(v, tb, k, smb, vmb)
                    if k >= 1:
                        hd(v, k - 1, peb, smb, vmb)
                hd(v, 31, peb, smb, vmb)
        block.vector(vdb)

        ctx1.close()
        ffn_done = vm.v
        ffn_sm = sm.v

        ctx2 = ExitStack()
        ytile = [pr.sb(f"yt{i}", [128, 1024], BF16, ctx2) for i in range(NDT)]
        qbuf = [pr.sb(f"qbuf{i}", [128, 1024], BF16, ctx2) for i in range(2)]
        wbuf = [pr.sb(f"wb{i}", [128, NDT * 128], BF16, ctx2) for i in range(2)]
        xf = [pr.sb(f"xf{i}", [128, 1024], F32, ctx2) for i in range(2)]
        t1 = pr.sb("t1", [128, 1024], F32, ctx2)
        t2 = pr.sb("t2", [128, 1024], F32, ctx2)
        psW = pr.ps("psW", [128, 1024], F32, ctx2)

        w_t = {}

        def wload(g, e, buf):
            w_t[e] = s_w.issue(g, buf, [(
                wbuf[buf][:].rearrange("p (dt c) -> p dt c", c=128),
                woutT[:, 128 * e:128 * (e + 1)].rearrange("(dt p) c -> p dt c", p=128),
                {})])

        for half in range(2):
            hs = slice(1024 * half, 1024 * (half + 1))
            for dt in range(NDT):
                def qy(q, dt=dt, half=half, fs_=ffn_sm, vv=vm.v):
                    if half == 0 and dt < 2:
                        q.wait_ge(sm.h, fs_)
                    else:
                        q.wait_ge(vm.h, vv)
                    s_q.issue(q, dt, [(qbuf[dt % 2][:], qT_i[128 * dt:128 * (dt + 1), hs], {})])
                block.sync(qy)
                q_t = (s_q.sems[dt % 2].h, s_q.sems[dt % 2].v)

                def vy(v, dt=dt, qt=q_t, av=a_done):
                    v.wait_ge(*qt)
                    v.wait_ge(vm.h, av)
                    v.tensor_scalar(ytile[dt][:], qbuf[dt % 2][:], acol[:, dt:dt + 1],
                                    None, OP.mult)
                    mark(v, vm)
                block.vector(vy)
            y_done = vm.v

            def g0(g, half=half, yv=y_done):
                g.wait_ge(vm.h, yv)
                wload(g, 0, 0)
                wload(g, 1, 1)
            block.gpsimd(g0)

            for d2 in range(NDT):
                def tw(t, d2=d2, yv=y_done):
                    t.wait_ge(vm.h, yv)
                    t.wait_ge(*w_t[d2])
                    if d2 > 0:
                        t.wait_ge(vm.h, vm.v)
                    for d1 in range(NDT):
                        for tc in range(2):
                            _mm = t.matmul(psW[:, 512 * tc:512 * (tc + 1)],
                                     wbuf[d2 % 2][:, d1 * 128:d1 * 128 + 128],
                                     ytile[d1][:, 512 * tc:512 * (tc + 1)],
                                     start=(d1 == 0), stop=(d1 == NDT - 1))
                    tmark(t, pe, _mm)
                block.tensor(tw)
                mmv = pe.v
                if d2 < NDT - 2:
                    def g2(g, d2=d2, mmv=mmv):
                        g.wait_ge(pe.h, mmv)
                        wload(g, d2 + 2, d2 % 2)
                    block.gpsimd(g2)

                def qx(q, d2=d2, half=half, vv=vm.v):
                    q.wait_ge(vm.h, vv)
                    s_xf.issue(q, d2, [(xf[d2 % 2][:],
                                        xT[128 * d2:128 * (d2 + 1), hs], {})])
                block.sync(qx)
                xf_t = (s_xf.sems[d2 % 2].h, s_xf.sems[d2 % 2].v)

                def vx(v, d2=d2, half=half, mmv=mmv, xt_=xf_t, fs_=ffn_sm):
                    v.wait_ge(pe.h, mmv)
                    v.wait_ge(*xt_)
                    if half == 0:
                        v.wait_ge(sm.h, fs_)
                    v.tensor_scalar(t1[:], psW[:], bout[:, d2:d2 + 1], None, OP.add)
                    v.drain()
                    v.tensor_tensor(t2[:], t1[:], xnT[d2][:, hs], OP.add)
                    v.drain()
                    v.tensor_tensor(t1[:], t2[:], fT[d2][:, hs], OP.add)
                    v.drain()
                    v.tensor_tensor(t2[:], t1[:], xf[d2 % 2][:], OP.add)
                    mark(v, vm)
                block.vector(vx)

                def qo(q, d2=d2, half=half, vv=vm.v):
                    q.wait_ge(vm.h, vv)
                    q.dma_start(out=x1T_o[128 * d2:128 * (d2 + 1), hs],
                                in_=t2[:]).then_inc(st.h, 16)
                    st.inc(16)
                    q.dma_start(out=x1s[128 * d2:128 * (d2 + 1), hs],
                                in_=t2[:]).then_inc(st.h, 16)
                    st.inc(16)
                block.sync(qo)
                stv = st.v

                def vnx(v, sv=stv):
                    v.wait_ge(st.h, sv)
                    mark(v, vm)
                block.vector(vnx)

        ctx2.close()

        emit_ln(pr, b, x1s, xnT, ms, eps=1e-5, tag="ln2")
        xn2_done = vm.v

        def qxn2(q):
            q.wait_ge(vm.h, xn2_done)
            for dt in range(NDT):
                q.dma_start(out=xn2T_o[128 * dt:128 * (dt + 1), :],
                            in_=xnT[dt][:]).then_inc(st.h, 16)
                st.inc(16)
        block.sync(qxn2)

        def qfin(q):
            q.wait_ge(st.h, st.v)
        block.sync(qfin)
        pr.finish = [(pe.h, pe.v), (vm.h, vm.v), (sm.h, sm.v), (st.h, st.v)]
        pr.ctx.pop_all()

    return pr


def l0b_inmaps(prep, l0a_res):
    maps = []
    for c in range(NCORE):
        part = c + 1 if c % 2 == 0 else c - 1
        maps.append({
            "xT": prep["xT"][c],
            "xnT_i": l0a_res[c]["xnT_o"],
            "qT_i": l0a_res[c]["qT_o"],
            "ap0": l0a_res[c]["apart_o"],
            "ap1": l0a_res[part]["apart_o"],
            "fi_i": l0a_res[c]["fi_o"],
            "fs_i": l0a_res[c]["fs_o"],
            "ecat": prep["e_cat"],
            "woutT": prep["woutT"],
            "bout_d": prep["bout"],
            "ident_d": prep["ident"],
            "ones1": prep["ones1"],
            "onescol": prep["onescol"],
        })
    return maps





def build_l1a(pr=None, io=None, gate=None):
    if pr is None:
        pr = Prog("l1a")
    nc = pr.nc
    ag = io is not None and "xn2f_ag" in io
    xn2f_d = io["xn2f_ag"] if ag else get_io(pr, io, "xn2f", [D, S], BF16)
    kvaT_d = get_io(pr, io, "kvaT", [D, 576], F32)
    lnw_d = get_io(pr, io, "lnw_d", [128, 4], F32)
    kvbn_d = get_io(pr, io, "kvbn", [512, 2048], F32)
    kvbv_d = get_io(pr, io, "kvbv", [512, 2048], F32)
    qpT_d = get_io(pr, io, "qpT", [D, 1536], F32)
    ones1 = get_io(pr, io, "ones1", [1, 128], BF16)
    onescol = get_io(pr, io, "onescol", [128, 1], BF16)

    kT_o = get_io(pr, io, "kT_o", [2048, S], BF16, out=True)
    krotT_o = get_io(pr, io, "krotT_o", [64, S], BF16, out=True)
    v_o = get_io(pr, io, "v_o", [S, 2048], BF16, out=True)
    q2T_o = get_io(pr, io, "q2T_o", [1536, S], BF16, out=True)

    with pr.ctx:
        o1 = pr.sb("o1", [1, 128], BF16)
        ocol = pr.sb("ocol", [128, 1], BF16)
        lnw = pr.sb("lnw", [128, 4], F32)
        cn = [pr.sb(f"cn{i}", [128, S], BF16) for i in range(4)]
        RNb = pr.sb("RNb", [128, S], BF16)
        ssq = pr.sb("ssq", [1, S], F32)
        rn16 = pr.sb("rn16", [1, S], BF16)

        pe = pr.sem("pe")
        vm = pr.sem("vm")
        sm = pr.sem("sm")
        st = pr.sem("st")
        s_const = Stream(pr, "cst", 1)
        s_xn = Stream(pr, "sxn", 1)
        s_w = Stream(pr, "sw", 2)

        if pr.block is None:
            pr.block = pr.ctx.enter_context(nc.Block())
        block = pr.block
        gate_all(block, gate)

        def qc(q):
            s_const.issue(q, 0, [
                (o1[:], ones1[:], {}),
                (ocol[:], onescol[:], {}),
                (lnw[:], lnw_d[:], {}),
            ])
        block.sync(qc)
        const_t = s_const.totals()[0]

        ctxA = ExitStack()
        xnf = [pr.sb(f"xnf{i}", [128, S], BF16, ctxA) for i in range(NDT)]
        wbuf = [pr.sb(f"wb{i}", [128, NDT * 128], BF16, ctxA) for i in range(2)]
        evb = pr.sb("evb", [128, 2048], BF16, ctxA)
        csq = evb
        psA = pr.ps("psA", [128, 2048], F32, ctxA)
        psR = pr.ps("psR", [128, 512], F32, ctxA)

        def qxn(q):
            for dt in range(NDT):
                if ag:
                    s_xn.issue(q, 0, [
                        (xnf[dt][:, 0:T], xn2f_d[128 * dt:128 * (dt + 1), :], {}),
                        (xnf[dt][:, T:S],
                         xn2f_d[D + 128 * dt:D + 128 * (dt + 1), :], {}),
                    ])
                else:
                    s_xn.issue(q, 0, [(xnf[dt][:],
                                       xn2f_d[128 * dt:128 * (dt + 1), :], {})])
        block.sync(qxn)
        xn_t = s_xn.totals()[0]

        w_t = {}

        def wload(g, dram, e, buf, ncol=128):
            w_t[(dram is qpT_d, e)] = s_w.issue(g, buf, [(
                wbuf[buf][:, 0:NDT * ncol].rearrange("p (dt c) -> p dt c", c=ncol),
                dram[:, ncol * e:ncol * (e + 1)].rearrange("(dt p) c -> p dt c", p=128),
                {})])

        def gq0(g):
            wload(g, qpT_d, 0, 0)
            wload(g, qpT_d, 1, 1)
        block.gpsimd(gq0)

        for mc in range(12):
            def tq(t, mc=mc, xt=xn_t, ct=const_t):
                t.wait_ge(*xt)
                t.wait_ge(*ct)
                t.wait_ge(*w_t[(True, mc)])
                if mc > 0:
                    t.wait_ge(vm.h, vm.v)
                for th in range(2):
                    for dt in range(NDT):
                        for tc in range(4):
                            _mm = t.matmul(psA[:, 512 * tc:512 * (tc + 1)],
                                     wbuf[mc % 2][:, dt * 128:dt * 128 + 128],
                                     xnf[dt][:, 2048 * th + 512 * tc:
                                              2048 * th + 512 * (tc + 1)],
                                     start=(dt == 0), stop=(dt == NDT - 1))
                    _mm.then_inc(pe.h, 1)
                    pe.inc()
                    t.wait_ge(vm.h, vm.v + th + 1)
            block.tensor(tq)
            if mc < 10:
                def gq(g, mc=mc, pv=pe.v):
                    g.wait_ge(pe.h, pv)
                    wload(g, qpT_d, mc + 2, mc % 2)
                block.gpsimd(gq)

            for th in range(2):
                def vq(v, mc=mc, th=th, pv=pe.v - 1 + th):
                    v.wait_ge(pe.h, pv)
                    v.tensor_scalar(evb[:], psA[:], QSCALE, None, OP.mult)
                    mark(v, vm)
                block.vector(vq)

                def qq(q, mc=mc, th=th, vv=vm.v):
                    q.wait_ge(vm.h, vv)
                    q.dma_start(out=q2T_o[128 * mc:128 * (mc + 1),
                                          2048 * th:2048 * (th + 1)],
                                in_=evb[:]).then_inc(st.h, 16)
                    st.inc(16)
                block.sync(qq)
                stv = st.v

                def vq2(v, sv=stv):
                    v.wait_ge(st.h, sv)
                    mark(v, vm)
                block.vector(vq2)
        q_done = vm.v

        def gc0(g, pv=pe.v):
            g.wait_ge(pe.h, pv)
            wload(g, kvaT_d, 0, 0)
            wload(g, kvaT_d, 1, 1)
        block.gpsimd(gc0)

        def vz(v, qd=q_done):
            v.wait_ge(vm.h, qd)
            v.memset(ssq[:], 0.0)
            mark(v, vm)
        block.vector(vz)

        for cc in range(5):
            ncol = 128 if cc < 4 else 64
            if cc >= 2 and cc < 4:
                pass
            for th in range(2):
                def tc_(t, cc=cc, th=th, ncol=ncol, qd=q_done):
                    t.wait_ge(vm.h, vm.v)
                    t.wait_ge(*w_t[(False, cc)])
                    for dt in range(NDT):
                        for tc in range(4):
                            _mm = t.matmul(psA[0:ncol, 512 * tc:512 * (tc + 1)],
                                     wbuf[cc % 2][:, dt * 128:dt * 128 + ncol],
                                     xnf[dt][:, 2048 * th + 512 * tc:
                                              2048 * th + 512 * (tc + 1)],
                                     start=(dt == 0), stop=(dt == NDT - 1))
                    tmark(t, pe, _mm)
                block.tensor(tc_)
                mmv = pe.v
                if th == 1 and cc < 3:
                    def gc(g, cc=cc, mmv=mmv):
                        g.wait_ge(pe.h, mmv)
                        if cc + 2 < 4:
                            wload(g, kvaT_d, cc + 2, cc % 2)
                        else:
                            w_t[(False, 4)] = s_w.issue(g, cc % 2, [(
                                wbuf[cc % 2][:].rearrange(
                                    "p (dt c) -> p dt c", c=128)[:, :, 0:64],
                                kvaT_d[:, 512:576].rearrange(
                                    "(dt p) c -> p dt c", p=128), {})])
                    block.gpsimd(gc)

                if cc < 4:
                    def sc(s, cc=cc, th=th, mmv=mmv):
                        s.wait_ge(pe.h, mmv)
                        s.copy(cn[cc][:, 2048 * th:2048 * (th + 1)], psA[:])
                        s.drain()
                        s.activation(csq[:], psA[:], AF.Square)
                        mark(s, sm)
                    block.scalar(sc)

                    def tsq(t, sv=sm.v):
                        t.wait_ge(sm.h, sv)
                        for tc in range(4):
                            _mm = t.matmul(psR[0:1, 0:512], ocol[:],
                                     csq[:, 512 * tc:512 * (tc + 1)],
                                     start=True, stop=True)
                            _mm.then_inc(pe.h, 1)
                            pe.inc()
                            t.wait_ge(vm.h, vm.v + tc + 1)
                    block.tensor(tsq)

                    for tc in range(4):
                        def vsq(v, th=th, tc=tc, pv=pe.v - 3 + tc):
                            v.wait_ge(pe.h, pv)
                            v.tensor_tensor(ssq[0:1, 2048 * th + 512 * tc:
                                                2048 * th + 512 * (tc + 1)],
                                            ssq[0:1, 2048 * th + 512 * tc:
                                                2048 * th + 512 * (tc + 1)],
                                            psR[0:1, 0:512], OP.add)
                            mark(v, vm)
                        block.vector(vsq)

                    def vcn(v, sv=sm.v):
                        v.wait_ge(sm.h, sv)
                        mark(v, vm)
                    block.vector(vcn)
                else:
                    def vrot(v, th=th, mmv=mmv):
                        v.wait_ge(pe.h, mmv)
                        v.tensor_copy(evb[0:64, 0:2048], psA[0:64, :])
                        mark(v, vm)
                    block.vector(vrot)

                    def qrot(q, th=th, vv=vm.v):
                        q.wait_ge(vm.h, vv)
                        q.dma_start(out=krotT_o[:, 2048 * th:2048 * (th + 1)],
                                    in_=evb[0:64, 0:2048]).then_inc(st.h, 16)
                        st.inc(16)
                    block.sync(qrot)
                    stv = st.v

                    def vrot2(v, sv=stv):
                        v.wait_ge(st.h, sv)
                        mark(v, vm)
                    block.vector(vrot2)
        c_done = vm.v

        def vrr(v, cd=c_done):
            v.wait_ge(vm.h, cd)
            v.tensor_scalar(ssq[:], ssq[:], 1.0 / KVR_, 1e-6, OP.mult, OP.add)
            mark(v, vm)
        block.vector(vrr)

        def srr(s, vv=vm.v):
            s.wait_ge(vm.h, vv)
            s.activation(ssq[:], ssq[:], AF.Ln)
            s.drain()
            s.activation(ssq[:], ssq[:], AF.Exp, scale=-0.5)
            mark(s, sm)
        block.scalar(srr)

        def vrr2(v, sv=sm.v):
            v.wait_ge(sm.h, sv)
            v.tensor_copy(rn16[:], ssq[:])
            mark(v, vm)
        block.vector(vrr2)

        def trb(t, vv=vm.v):
            t.wait_ge(vm.h, vv)
            for tc in range(8):
                _mm = t.matmul(psA[:, 512 * (tc % 4):512 * (tc % 4 + 1)], o1[:],
                         rn16[:, 512 * tc:512 * (tc + 1)], start=True, stop=True)
                _mm.then_inc(pe.h, 1)
                pe.inc()
                t.wait_ge(sm.h, sm.v + tc + 1)
        block.tensor(trb)

        for tc in range(8):
            def srb(s, tc=tc, pv=pe.v - 7 + tc):
                s.wait_ge(pe.h, pv)
                s.copy(RNb[:, 512 * tc:512 * (tc + 1)],
                       psA[:, 512 * (tc % 4):512 * (tc % 4 + 1)])
                s.drain()
                s.nop().then_inc(sm.h, 1)
                sm.inc()
            block.scalar(srb)
        rb_done = sm.v

        for cc in range(4):
            def vcs(v, cc=cc, rb=rb_done):
                v.wait_ge(sm.h, rb)
                v.tensor_tensor(cn[cc][:], cn[cc][:], RNb[:], OP.mult)
                v.drain()
                v.tensor_scalar(cn[cc][:], cn[cc][:], lnw[:, cc:cc + 1], None,
                                OP.mult)
                mark(v, vm)
            block.vector(vcs)
        cn_done = vm.v

        ctxA.close()

        ctxB = ExitStack()
        kwb = pr.sb("kwb", [128, 4 * 128], BF16, ctxB)
        vwb = pr.sb("vwb", [128, 4 * 2048], BF16, ctxB)
        ev2 = pr.sb("ev2", [128, 2048], BF16, ctxB)
        psK = pr.ps("psK", [128, 2048], F32, ctxB)
        s_kw = Stream(pr, "skw", 2)

        kw_t = {}

        def kwload(g, e, buf):
            kw_t[e] = s_kw.issue(g, buf, [(
                kwb[:, 0:4 * 128].rearrange("p (rt c) -> p rt c", c=128),
                kvbn_d[:, 128 * e:128 * (e + 1)].rearrange("(rt p) c -> p rt c", p=128),
                {})])

        def gk0(g, cd=cn_done):
            g.wait_ge(vm.h, cd)
            kwload(g, 0, 0)
            g.dma_start(out=vwb[:].rearrange("p (rt c) -> p rt c", c=2048),
                        in_=kvbv_d.rearrange("(rt p) c -> p rt c", p=128),
                        ).then_inc(s_kw.sems[1].h, 16)
            s_kw.sems[1].inc(16)
        block.gpsimd(gk0)
        vw_t = (s_kw.sems[1].h, s_kw.sems[1].v)

        for mc in range(16):
            for th in range(2):
                def tk(t, mc=mc, th=th, cd=cn_done):
                    t.wait_ge(vm.h, vm.v if (mc or th) else cd)
                    t.wait_ge(*kw_t[mc])
                    for rt in range(4):
                        for tc in range(4):
                            _mm = t.matmul(psK[:, 512 * tc:512 * (tc + 1)],
                                     kwb[:, rt * 128:rt * 128 + 128],
                                     cn[rt][:, 2048 * th + 512 * tc:
                                             2048 * th + 512 * (tc + 1)],
                                     start=(rt == 0), stop=(rt == 3))
                    tmark(t, pe, _mm)
                block.tensor(tk)
                mmv = pe.v

                def vk(v, mmv=mmv):
                    v.wait_ge(pe.h, mmv)
                    v.tensor_copy(ev2[:], psK[:])
                    mark(v, vm)
                block.vector(vk)

                def qk(q, mc=mc, th=th, vv=vm.v):
                    q.wait_ge(vm.h, vv)
                    q.dma_start(out=kT_o[128 * mc:128 * (mc + 1),
                                         2048 * th:2048 * (th + 1)],
                                in_=ev2[:]).then_inc(st.h, 16)
                    st.inc(16)
                block.sync(qk)
                stv = st.v

                def vk2(v, sv=stv):
                    v.wait_ge(st.h, sv)
                    mark(v, vm)
                block.vector(vk2)
            if mc < 15:
                def gkn(g, mc=mc, pv=pe.v):
                    g.wait_ge(pe.h, pv)
                    kwload(g, mc + 1, 0)
                block.gpsimd(gkn)

        for kc in range(32):
            def tv(t, kc=kc, vt=vw_t):
                t.wait_ge(*vt)
                t.wait_ge(vm.h, vm.v)
                for rt in range(4):
                    for tc in range(4):
                        _mm = t.matmul(psK[:, 512 * tc:512 * (tc + 1)],
                                 cn[rt][:, 128 * kc:128 * (kc + 1)],
                                 vwb[:, rt * 2048 + 512 * tc:
                                     rt * 2048 + 512 * (tc + 1)],
                                 start=(rt == 0), stop=(rt == 3))
                tmark(t, pe, _mm)
            block.tensor(tv)
            mmv = pe.v

            def vv2(v, mmv=mmv):
                v.wait_ge(pe.h, mmv)
                v.tensor_copy(ev2[:], psK[:])
                mark(v, vm)
            block.vector(vv2)

            def qv(q, kc=kc, vv=vm.v):
                q.wait_ge(vm.h, vv)
                q.dma_start(out=v_o[128 * kc:128 * (kc + 1), :],
                            in_=ev2[:]).then_inc(st.h, 16)
                st.inc(16)
            block.sync(qv)
            stv = st.v

            def vv3(v, sv=stv):
                v.wait_ge(st.h, sv)
                mark(v, vm)
            block.vector(vv3)

        ctxB.close()

        def qfin(q):
            q.wait_ge(st.h, st.v)
        block.sync(qfin)
        pr.finish = [(pe.h, pe.v), (vm.h, vm.v), (sm.h, sm.v), (st.h, st.v)]
        pr.ctx.pop_all()

    return pr


KVR_ = 512.0



def build_l1b(pr=None, io=None, gate=None):
    if pr is None:
        pr = Prog("l1b")
    nc = pr.nc
    kT_d = get_io(pr, io, "kT", [1024, S], BF16)
    krotT_d = get_io(pr, io, "krotT", [64, S], BF16)
    v_d = get_io(pr, io, "v", [S, 1024], BF16)
    qT_d = get_io(pr, io, "qT", [1536, S], BF16)
    opT_d = get_io(pr, io, "opT", [1024, D], F32)
    masks_d = get_io(pr, io, "masks", [4, 128, 512], F32)
    ones1 = get_io(pr, io, "ones1", [1, 128], BF16)
    onescol = get_io(pr, io, "onescol", [128, 1], BF16)

    op_o = get_io(pr, io, "op_o", [S, D], BF16, out=True)

    with pr.ctx:
        o1 = pr.sb("o1", [1, 128], BF16)
        ocol = pr.sb("ocol", [128, 1], BF16)
        maskb = pr.sb("maskb", [128, 4 * 512], F32)
        ob = pr.sb("ob", [128, 8 * 2048], BF16)
        krot = pr.sb("krot", [64, S], BF16)
        pvt = [pr.sb(f"pvt{i}", [128, 2048], BF16) for i in range(8)]

        pe = pr.sem("pe")
        vm = pr.sem("vm")
        sm = pr.sem("sm")
        st = pr.sem("st")
        s_const = Stream(pr, "cst", 1)
        s_kh = Stream(pr, "skh", 2)
        s_vh = Stream(pr, "svh", 2)
        s_qh = Stream(pr, "sqh", 2)

        if pr.block is None:
            pr.block = pr.ctx.enter_context(nc.Block())
        block = pr.block
        gate_all(block, gate)

        def qc(q):
            s_const.issue(q, 0, [
                (o1[:], ones1[:], {}),
                (ocol[:], onescol[:], {}),
                (maskb[:].rearrange("p (a c) -> p a c", c=512),
                 masks_d.rearrange("a b c -> b a c"), {}),
                (krot[:], krotT_d[:], {}),
            ])
        block.sync(qc)
        const_t = s_const.totals()[0]

        s_ob = Stream(pr, "sob", 1)

        def gob(g):
            s_ob.issue(g, 0, [(ob[:].rearrange("p (a c) -> p a c", c=2048),
                               opT_d.rearrange("(a p) c -> p a c", p=128), {})])
        block.gpsimd(gob)
        ob_t = s_ob.totals()[0]

        ctx1 = ExitStack()
        kh = [pr.sb(f"kh{i}", [128, S], BF16, ctx1) for i in range(2)]
        vh = [pr.sb(f"vh{i}", [128, 32 * 128], BF16, ctx1) for i in range(2)]
        qhn = [pr.sb(f"qhn{i}", [128, 2048], BF16, ctx1) for i in range(2)]
        qhr = [pr.sb(f"qhr{i}", [64, 2048], BF16, ctx1) for i in range(2)]
        pbuf = [pr.sb(f"pbuf{i}", [128, 512], BF16, ctx1) for i in range(4)]
        zrow = pr.sb("zrow", [1, 512], F32, ctx1)
        rz16 = pr.sb("rz16", [1, 512], BF16, ctx1)
        rzb = pr.sb("rzb", [128, 512], BF16, ctx1)
        onat = pr.sb("onat", [128, 2048], BF16, ctx1)
        psSb = [pr.ps(f"psS{i}", [128, 512], F32, ctx1) for i in range(4)]
        psZ = pr.ps("psZ", [128, 512], F32, ctx1)
        psP = pr.ps("psP", [128, 512], F32, ctx1)
        psO = pr.ps("psO", [128, 1024], F32, ctx1)

        for qhalf in range(2):
            for hl in range(8):
                vm0, sm0, pe0 = vm.v, sm.v, pe.v
                nkts = [4 * (qhalf * 4 + g_) + 4 for g_ in range(4)]
                SC = [0] * 5
                PC = [0] * 5
                for g_ in range(4):
                    SC[g_ + 1] = SC[g_] + nkts[g_] + 1
                    PC[g_ + 1] = PC[g_] + 2 * nkts[g_] + 1

                def expv(g_, kt, sm0=sm0, SC=SC):
                    return sm0 + SC[g_] + kt + 1

                def srzv(g_, sm0=sm0, SC=SC, nkts=nkts):
                    return sm0 + SC[g_] + nkts[g_] + 1

                def tsv(g_, kt, pe0=pe0, PC=PC):
                    return pe0 + PC[g_] + (2 * kt if kt >= 1 else 1)

                def tzv(g_, kt, pe0=pe0, PC=PC, nkts=nkts):
                    nkt = nkts[g_]
                    return pe0 + PC[g_] + (2 * kt + 3 if kt <= nkt - 2
                                           else 2 * nkt)

                def trzv(g_, pe0=pe0, PC=PC, nkts=nkts):
                    return pe0 + PC[g_] + 2 * nkts[g_] + 1

                def qh_(q, hl=hl, qhalf=qhalf, vv=vm0):
                    q.wait_ge(vm.h, vv)
                    s_kh.issue(q, hl, [(kh[hl % 2][:],
                                        kT_d[128 * hl:128 * (hl + 1), :], {})])
                    s_vh.issue(q, hl, [(
                        vh[hl % 2][:].rearrange("p (a c) -> p a c", c=128),
                        v_d[:, 128 * hl:128 * (hl + 1)].rearrange(
                            "(a p) c -> p a c", p=128), {})])
                    s_qh.issue(q, hl, [
                        (qhn[hl % 2][:],
                         qT_d[192 * hl:192 * hl + 128,
                              2048 * qhalf:2048 * (qhalf + 1)], {}),
                        (qhr[hl % 2][0:64, :],
                         qT_d[192 * hl + 128:192 * hl + 192,
                              2048 * qhalf:2048 * (qhalf + 1)], {}),
                    ])
                block.sync(qh_)
                kh_t = (s_kh.sems[hl % 2].h, s_kh.sems[hl % 2].v)
                vh_t = (s_vh.sems[hl % 2].h, s_vh.sems[hl % 2].v)
                qh_t = (s_qh.sems[hl % 2].h, s_qh.sems[hl % 2].v)

                def tb(t, hl=hl, qhalf=qhalf, vm0=vm0, nkts=nkts,
                       kh_t=kh_t, vh_t=vh_t, qh_t=qh_t,
                       expv=expv, tzv=tzv, trzv=trzv):
                    t.wait_ge(*kh_t)
                    t.wait_ge(*qh_t)
                    t.wait_ge(*vh_t)
                    t.wait_ge(*const_t)
                    for g_ in range(4):
                        nkt = nkts[g_]

                        def ts_(kt):
                            psS = psSb[kt % 4]
                            if kt >= 4:
                                t.wait_ge(sm.h, expv(g_, kt - 4))
                            elif kt == 0:
                                t.wait_ge(vm.h, vm0 + 6 * g_)
                            t.matmul(psS[:],
                                     kh[hl % 2][:, 128 * kt:128 * (kt + 1)],
                                     qhn[hl % 2][:, 512 * g_:512 * (g_ + 1)],
                                     start=True, stop=False)
                            _mm = t.matmul(psS[:],
                                           krot[:, 128 * kt:128 * (kt + 1)],
                                           qhr[hl % 2][0:64,
                                               512 * g_:512 * (g_ + 1)],
                                           start=False, stop=True)
                            tmark(t, pe, _mm)

                        def tz_(kt):
                            t.wait_ge(sm.h, expv(g_, kt))
                            t.matmul(psZ[0:1, :], ocol[:], pbuf[kt % 4][:],
                                     start=(kt == 0), stop=(kt == nkt - 1))
                            _mm = t.matmul(psP[:],
                                           vh[hl % 2][:, 128 * kt:128 * (kt + 1)],
                                           pbuf[kt % 4][:],
                                           start=(kt == 0), stop=(kt == nkt - 1))
                            tmark(t, pe, _mm)

                        for kt in range(nkt):
                            ts_(kt)
                            if kt >= 1:
                                tz_(kt - 1)
                        tz_(nkt - 1)
                        t.wait_ge(vm.h, vm0 + 6 * g_ + 5)
                        _mm = t.matmul(psSb[0][:], o1[:], rz16[:],
                                       start=True, stop=True)
                        tmark(t, pe, _mm)
                block.tensor(tb)

                def sb_(s, qhalf=qhalf, vm0=vm0, nkts=nkts,
                        tsv=tsv, tzv=tzv, trzv=trzv):
                    for g_ in range(4):
                        nkt = nkts[g_]
                        qg = qhalf * 4 + g_
                        for kt in range(nkt):
                            j = kt - 4 * qg
                            if j >= 0:
                                s.wait_ge(vm.h, vm0 + 6 * g_ + j + 1)
                            else:
                                s.wait_ge(pe.h, tsv(g_, kt))
                            if kt >= 4:
                                s.wait_ge(pe.h, tzv(g_, kt - 4))
                            _a = s.activation(pbuf[kt % 4][:],
                                              psSb[kt % 4][:], AF.Exp)
                            _a.then_inc(sm.h, 1)
                            sm.inc()
                        s.wait_ge(pe.h, trzv(g_))
                        _c = s.copy(rzb[:], psSb[0][:])
                        _c.then_inc(sm.h, 1)
                        sm.inc()
                block.scalar(sb_)

                def vb_(v, hl=hl, qhalf=qhalf, nkts=nkts,
                        tsv=tsv, tzv=tzv, srzv=srzv):
                    for g_ in range(4):
                        nkt = nkts[g_]
                        qg = qhalf * 4 + g_
                        for j in range(4):
                            kt = 4 * qg + j
                            v.wait_ge(pe.h, tsv(g_, kt))
                            _m = v.tensor_tensor(psSb[kt % 4][:],
                                                 psSb[kt % 4][:],
                                                 maskb[:, 512 * j:512 * (j + 1)],
                                                 OP.add)
                            _m.then_inc(vm.h, 1)
                            vm.inc()
                        v.wait_ge(pe.h, tzv(g_, nkt - 1))
                        v.tensor_copy(zrow[:], psZ[0:1, :])
                        v.drain()
                        v.reciprocal(zrow[:], zrow[:])
                        v.drain()
                        _c = v.tensor_copy(rz16[:], zrow[:])
                        _c.then_inc(vm.h, 1)
                        vm.inc()
                        v.wait_ge(sm.h, srzv(g_))
                        _p = v.tensor_tensor(pvt[hl][:, 512 * g_:512 * (g_ + 1)],
                                             psP[:], rzb[:], OP.mult)
                        _p.then_inc(vm.h, 1)
                        vm.inc()
                block.vector(vb_)

            pv_done = vm.v
            pe1, vm1, st1 = pe.v, vm.v, st.v

            def to_(t, qhalf=qhalf, pv=pv_done, obt=ob_t, vm1=vm1):
                t.wait_ge(vm.h, pv)
                t.wait_ge(*obt)
                for qc_ in range(16):
                    for oh in range(2):
                        i2 = 2 * qc_ + oh
                        if i2 >= 1:
                            t.wait_ge(vm.h, vm1 + i2)
                        for hl in range(8):
                            for tc in range(2):
                                c0 = 1024 * oh + 512 * tc
                                _mm = t.matmul(psO[:, 512 * tc:512 * (tc + 1)],
                                               pvt[hl][:, 128 * qc_:
                                                        128 * (qc_ + 1)],
                                               ob[:, 2048 * hl + c0:
                                                  2048 * hl + c0 + 512],
                                               start=(hl == 0), stop=(hl == 7))
                        tmark(t, pe, _mm)
            block.tensor(to_)

            def vo_(v, pe1=pe1, st1=st1):
                for qc_ in range(16):
                    for oh in range(2):
                        i2 = 2 * qc_ + oh
                        v.wait_ge(pe.h, pe1 + i2 + 1)
                        if oh == 0 and st1 + 16 * qc_ > 0:
                            v.wait_ge(st.h, st1 + 16 * qc_)
                        _c = v.tensor_copy(onat[:, 1024 * oh:1024 * (oh + 1)],
                                           psO[:])
                        _c.then_inc(vm.h, 1)
                        vm.inc()
            block.vector(vo_)

            def qo_(q, qhalf=qhalf, vm1=vm1):
                for qc_ in range(16):
                    q.wait_ge(vm.h, vm1 + 2 * qc_ + 2)
                    q.dma_start(out=op_o[2048 * qhalf + 128 * qc_:
                                         2048 * qhalf + 128 * (qc_ + 1), :],
                                in_=onat[:]).then_inc(st.h, 16)
                    st.inc(16)
            block.sync(qo_)

        ctx1.close()

        def qfin(q):
            q.wait_ge(st.h, st.v)
        block.sync(qfin)
        pr.finish = [(pe.h, pe.v), (vm.h, vm.v), (sm.h, sm.v), (st.h, st.v)]
        pr.ctx.pop_all()

    return pr



def build_l1c(pr=None, io=None, gate=None):
    if pr is None:
        pr = Prog("l1c")
    nc = pr.nc
    xn2T_d = get_io(pr, io, "xn2T", [D, T], BF16)
    x1T_d = get_io(pr, io, "x1T", [D, T], F32)
    opa_d = get_io(pr, io, "opa", [T, D], BF16)
    wgut_d = get_io(pr, io, "wgut16", [D, 8 * D], BF16)
    wprt_d = get_io(pr, io, "wprt16", [4 * D, D], BF16)
    selm_d = get_io(pr, io, "selm_d", [32, 2048], BF16)
    indm_d = get_io(pr, io, "indm_d", [128, 254], BF16)
    ident_d = get_io(pr, io, "ident_d", [128, 128], BF16)
    ones1 = get_io(pr, io, "ones1", [1, 128], BF16)
    onescol = get_io(pr, io, "onescol", [128, 1], BF16)
    w16rdy = (io or {}).get("w16_ready", [])

    yT_o = get_io(pr, io, "yT_o", [D, T], F32, out=True)
    h_hbm = pr.dtmp("h_hbm", [4 * D, T], BF16)
    wqpT_hbm = pr.dtmp("wqpT_hbm", [4 * D, D], BF16)

    NH = 4 * D
    with pr.ctx:
        o1 = pr.sb("o1", [1, 128], BF16)
        ocol = pr.sb("ocol", [128, 1], BF16)
        ident = pr.sb("ident", [128, 128], BF16)
        selm = pr.sb("selm", [32, 2048], BF16)
        indm = pr.sb("indm", [128, 254], BF16)
        ssq = pr.sb("ssq", [1, T], F32)
        rn16 = pr.sb("rn16", [1, T], BF16)
        RNb = pr.sb("RNb", [128, T], BF16)
        ctxX = ExitStack()
        xn2T = [pr.sb(f"xn2T{i}", [128, T], BF16, ctxX) for i in range(NDT)]

        s_const = Stream(pr, "cst", 1)
        vq = pr.sem("vq")
        vqn = [0]

        def vchain(inst):
            inst.then_inc(vq.h, 1)
            vqn[0] += 1
            vq.inc()
            return vqn[0]

        if pr.block is None:
            pr.block = pr.ctx.enter_context(nc.Block())
        block = pr.block
        gate_all(block, gate)

        def qc(q):
            s_const.issue(q, 0, [
                (o1[:], ones1[:], {}),
                (ocol[:], onescol[:], {}),
                (ident[:], ident_d[:], {}),
                (selm[:], selm_d[:], {}),
                (indm[:], indm_d[:], {}),
            ])
            for dt in range(NDT):
                s_const.issue(q, 0, [(xn2T[dt][:],
                                      xn2T_d[128 * dt:128 * (dt + 1), :], {})])
        block.sync(qc)
        const_t = s_const.totals()[0]
        xn_t = const_t

        ctxB = ExitStack()
        rawB = [pr.sb(f"rawB{i}", [128, D], BF16, ctxB) for i in range(8)]
        absb = [pr.sb(f"absb{i}", [128, D], BF16, ctxB) for i in range(8)]
        thr16B = [pr.sb(f"thr16B{i}", [8, 1024], BF16, ctxB) for i in range(2)]
        thrBB = [pr.sb(f"thrBB{i}", [128, D], BF16, ctxB) for i in range(2)]
        gtb = pr.sb("gtb", [128, 1024], BF16, ctxB)
        ltb = pr.sb("ltb", [128, 1024], BF16, ctxB)
        subb = pr.sb("subb", [128, 1024], BF16, ctxB)
        wqbB = [pr.sb(f"wqbB{i}", [128, D], BF16, ctxB) for i in range(2)]
        psSum = pr.ps("psSum", [8, 1024], F32, ctxB)
        psThr = [pr.ps(f"psThr{i}", [128, 1024], F32, ctxB) for i in range(2)]

        bA = pr.sem("bA")
        bS = pr.sem("bS")
        bRw = pr.sem("bRw")
        bB = pr.sem("bB")
        bM = pr.sem("bM")
        bC = pr.sem("bC")
        s_wp = Stream(pr, "swp", 2)
        s_sq = Stream(pr, "ssq2", 2)
        wq_done = [(s.h, 16 * 32) for s in s_sq.sems]

        def _wp_slot(ip):
            return s_wp.sems[ip % 2].h, 16 * (ip // 2 + 1)

        def gB(g):
            wait_all(g, w16rdy)
            for ip in range(64):
                if ip >= 2:
                    g.wait_ge(bA.h, ip - 1)
                if ip >= 8:
                    g.wait_ge(bC.h, ip - 7)
                s_wp.issue(g, ip, [(rawB[ip % 8][:],
                           wprt_d[128 * ip:128 * (ip + 1), :], {})])
        block.gpsimd(gB)

        def tB(t):
            t.wait_ge(*const_t)
            for Q in range(16):
                for hc in range(2):
                    t.wait_ge(bA.h, 4 * Q + 4)
                    if 2 * Q + hc >= 1:
                        t.wait_ge(bRw.h, 2 * Q + hc)
                    for r in range(4):
                        for c in range(2):
                            _mm = t.matmul(
                                psSum[:, 512 * c:512 * (c + 1)],
                                indm[:, 126 - 2 * r:134 - 2 * r],
                                absb[(4 * Q + r) % 8][:, 1024 * hc + 512 * c:
                                                      1024 * hc + 512 * (c + 1)],
                                start=(r == 0), stop=(r == 3))
                    tmark(t, bS, _mm)
                for r in range(4):
                    ip = 4 * Q + r
                    for hc in range(2):
                        t.wait_ge(bRw.h, 2 * Q + hc + 1)
                        pm = 2 * ip + 1 if hc == 1 else 2 * ip - 2
                        if pm >= 1:
                            t.wait_ge(bM.h, pm)
                        for c in range(2):
                            _mm = t.matmul(
                                psThr[ip % 2][:, 512 * c:512 * (c + 1)],
                                selm[0:8, 128 * r:128 * (r + 1)],
                                thr16B[hc][:, 512 * c:512 * (c + 1)],
                                start=True, stop=True)
                        tmark(t, bB, _mm)
        block.tensor(tB)

        def sB(s):
            for ip in range(64):
                if ip >= 1:
                    s.wait_ge(bC.h, ip)
                    s_sq.issue(s, ip - 1, [(
                        wqpT_hbm[128 * (ip - 1):128 * ip, :],
                        wqbB[(ip - 1) % 2][:], {})])
                for hc in range(2):
                    s.wait_ge(bB.h, 2 * ip + hc + 1)
                    if ip >= 2:
                        s.wait_ge(bC.h, ip - 1)
                    _cp = s.copy(thrBB[ip % 2][:, 1024 * hc:1024 * (hc + 1)],
                                 psThr[ip % 2][:])
                    _cp.then_inc(bM.h, 1)
                    bM.inc()
            s.wait_ge(bC.h, 64)
            s_sq.issue(s, 63, [(wqpT_hbm[128 * 63:128 * 64, :],
                                wqbB[63 % 2][:], {})])
        block.scalar(sB)

        def vB(v):
            for ip in range(64):
                Q, r = ip // 4, ip % 4
                sh, sv = _wp_slot(ip)
                v.wait_ge(sh, sv)
                if ip >= 8:
                    v.wait_ge(bS.h, 2 * ((ip - 8) // 4) + 2)
                _a = v.scalar_tensor_tensor(absb[ip % 8][:], rawB[ip % 8][:],
                                            -1.0, rawB[ip % 8][:],
                                            OP.mult, OP.max)
                _a.then_inc(bA.h, 1)
                bA.inc()
                if r != 3:
                    continue
                for hc in range(2):
                    v.wait_ge(bS.h, 2 * Q + hc + 1)
                    if Q >= 1:
                        v.wait_ge(bB.h, 2 * (4 * Q - 1) + hc + 1)
                    _r1 = v.tensor_scalar(thr16B[hc][:], psSum[:],
                                          1.0 / 64, None, OP.mult)
                    v.wait_ge(vq.h, vchain(_r1))
                    _r2 = v.tensor_scalar(thr16B[hc][:], thr16B[hc][:],
                                          0.5, 5e-9, OP.mult, OP.max)
                    _r2.then_inc(bRw.h, 1)
                    bRw.inc()
                for r2 in range(4):
                    ip2 = 4 * Q + r2
                    for hc in range(2):
                        v.wait_ge(bM.h, 2 * ip2 + hc + 1)
                        if hc == 0 and ip2 >= 2:
                            sl = s_sq.sems[(ip2 - 2) % 2]
                            v.wait_ge(sl.h, 16 * ((ip2 - 2) // 2 + 1))
                        csl = slice(1024 * hc, 1024 * (hc + 1))
                        v.tensor_tensor(gtb[:], rawB[ip2 % 8][:, csl],
                                        thrBB[ip2 % 2][:, csl], OP.is_gt)
                        _l = v.scalar_tensor_tensor(
                            ltb[:], rawB[ip2 % 8][:, csl], -1.0,
                            thrBB[ip2 % 2][:, csl], OP.mult, OP.is_gt)
                        v.wait_ge(vq.h, vchain(_l))
                        _s = v.tensor_tensor(subb[:], gtb[:], ltb[:],
                                             OP.subtract)
                        v.wait_ge(vq.h, vchain(_s))
                        _w = v.scalar_tensor_tensor(
                            wqbB[ip2 % 2][:, csl], subb[:], 2.0,
                            thrBB[ip2 % 2][:, csl], OP.mult, OP.mult)
                        if hc == 1:
                            _w.then_inc(bC.h, 1)
                            bC.inc()
        block.vector(vB)
        ctxB.close()

        ctxC = ExitStack()
        rawC = [[pr.sb(f"rawC{z}_{i}", [128, 4096], BF16, ctxC)
                 for i in range(2)] for z in range(2)]
        absC = [pr.sb(f"absC{z}", [128, 4096], BF16, ctxC) for z in range(2)]
        thr16C = [pr.sb(f"thr16C{z}", [32, 256], BF16, ctxC) for z in range(2)]
        thrBC = [pr.sb(f"thrBC{z}", [128, 4096], BF16, ctxC) for z in range(2)]
        gtc = pr.sb("gtc", [128, 4096], BF16, ctxC)
        ltc = pr.sb("ltc", [128, 4096], BF16, ctxC)
        wqbC = [[pr.sb(f"wqbC{z}_{i}", [128, 4096], BF16, ctxC)
                 for i in range(2)] for z in range(2)]
        sgb = [pr.sb(f"sgb{i}", [128, 512], F32, ctxC) for i in range(2)]
        hjb = [pr.sb(f"hjb{i}", [128, 512], BF16, ctxC) for i in range(2)]
        hsqb = [pr.sb(f"hsqb{i}", [128, 512], BF16, ctxC) for i in range(2)]
        cSum = pr.ps("cSum", [128, 512], F32, ctxC)
        cThr = [pr.ps(f"cThr{i}", [128, 512], F32, ctxC) for i in range(2)]
        psG = [pr.ps(f"psG{i}", [128, 512], F32, ctxC) for i in range(2)]
        psU = [pr.ps(f"psU{i}", [128, 512], F32, ctxC) for i in range(2)]
        psR = pr.ps("psR", [128, 512], F32, ctxC)

        cA = pr.sem("cA")
        cT = pr.sem("cT")
        cQ = pr.sem("cQ")
        cS = pr.sem("cS")
        cB = pr.sem("cB")
        cM = pr.sem("cM")
        cG = pr.sem("cG")
        cL = pr.sem("cL")
        cH = pr.sem("cH")
        cR = pr.sem("cR")
        cZ = pr.sem("cZ")
        s_wgg = Stream(pr, "swgg", 1)
        s_wgq = Stream(pr, "swgq", 1)
        s_hh = Stream(pr, "shh", 2)

        NJP = 32

        def _wg_slot(jp, z):
            st = s_wgg if z == 0 else s_wgq
            return st.sems[0].h, 16 * (jp + 1)

        def _craw(jp, z):
            c0 = 256 * jp + (NH if z == 1 else 0)
            return wgut_d[:, c0:c0 + 256].rearrange("(a p) c -> p a c", p=128)

        def gC(g):
            wait_all(g, w16rdy)
            g.wait_ge(bC.h, 64)
            wait_all(g, wq_done)
            for jp in range(NJP):
                if jp >= 1:
                    g.wait_ge(cA.h, 2 * (jp - 1) + 0 + 1)
                if jp >= 2:
                    g.wait_ge(cQ.h, 2 * (jp - 2) + 0 + 1)
                s_wgg.issue(g, jp, [(
                    rawC[0][jp % 2][:].rearrange("p (a c) -> p a c", c=256),
                    _craw(jp, 0), {})])
        block.gpsimd(gC)

        def qC(q):
            wait_all(q, w16rdy)
            q.wait_ge(bC.h, 64)
            wait_all(q, wq_done)
            for jp in range(NJP):
                if jp >= 1:
                    q.wait_ge(cA.h, 2 * (jp - 1) + 1 + 1)
                if jp >= 2:
                    q.wait_ge(cQ.h, 2 * (jp - 2) + 1 + 1)
                s_wgq.issue(q, jp, [(
                    rawC[1][jp % 2][:].rearrange("p (a c) -> p a c", c=256),
                    _craw(jp, 1), {})])
        block.sync(qC)

        def tC(t):
            t.wait_ge(*const_t)
            t.wait_ge(*xn_t)
            for jp in range(NJP):
                for z in range(2):
                    q2 = 2 * jp + z
                    t.wait_ge(cA.h, q2 + 1)
                    if q2 >= 1:
                        t.wait_ge(cT.h, q2)
                    for a in range(16):
                        _mm = t.matmul(cSum[0:32, 0:256],
                                       indm[:, 126 - 2 * a:158 - 2 * a],
                                       absC[z][:, 256 * a:256 * (a + 1)],
                                       start=(a == 0), stop=(a == 15))
                    tmark(t, cS, _mm)
                    t.wait_ge(cT.h, q2 + 1)
                    for a in range(16):
                        e = 16 * q2 + a
                        if e >= 2:
                            t.wait_ge(cM.h, e - 1)
                        _mm = t.matmul(cThr[a % 2][:, 0:256],
                                       selm[0:32, 128 * a:128 * (a + 1)],
                                       thr16C[z][:], start=True, stop=True)
                        tmark(t, cB, _mm)
                t.wait_ge(cQ.h, 2 * jp + 2)
                for tc in range(4):
                    for mc in range(2):
                        hh = 8 * jp + 2 * tc + mc
                        if hh >= 2:
                            t.wait_ge(cH.h, hh - 1)
                        for a in range(16):
                            t.matmul(psG[mc][:],
                                     wqbC[0][jp % 2][:, 256 * a + 128 * mc:
                                                     256 * a + 128 * (mc + 1)],
                                     xn2T[a][:, 512 * tc:512 * (tc + 1)],
                                     start=(a == 0), stop=(a == 15))
                        for a in range(16):
                            _mm = t.matmul(psU[mc][:],
                                           wqbC[1][jp % 2][:, 256 * a + 128 * mc:
                                                           256 * a + 128 * (mc + 1)],
                                           xn2T[a][:, 512 * tc:512 * (tc + 1)],
                                           start=(a == 0), stop=(a == 15))
                        tmark(t, cG, _mm)
                    rr = 4 * jp + tc
                    if rr >= 1:
                        t.wait_ge(cZ.h, rr)
                    for mc in range(2):
                        t.wait_ge(cH.h, 8 * jp + 2 * tc + mc + 1)
                        _mm = t.matmul(psR[0:1, :], ocol[:], hsqb[mc][:],
                                       start=(mc == 0), stop=(mc == 1))
                    tmark(t, cR, _mm)
        block.tensor(tC)

        def sC(s):
            def hstore(ph):
                s.wait_ge(cH.h, ph + 1)
                jb, tb2, mb = ph // 8, (ph % 8) // 2, ph % 2
                s_hh.issue(s, ph, [(
                    h_hbm[256 * jb + 128 * mb:256 * jb + 128 * (mb + 1),
                          512 * tb2:512 * (tb2 + 1)], hjb[ph % 2][:], {})])

            for jp in range(NJP):
                for z in range(2):
                    q2 = 2 * jp + z
                    for a in range(16):
                        e = 16 * q2 + a
                        s.wait_ge(cB.h, e + 1)
                        if a == 0 and jp >= 1:
                            s.wait_ge(cQ.h, 2 * (jp - 1) + z + 1)
                        _cp = s.copy(thrBC[z][:, 256 * a:256 * (a + 1)],
                                     cThr[a % 2][:, 0:256])
                        _cp.then_inc(cM.h, 1)
                        cM.inc()
                for tc in range(4):
                    for mc in range(2):
                        hh = 8 * jp + 2 * tc + mc
                        s.wait_ge(cG.h, hh + 1)
                        if hh >= 2:
                            s.wait_ge(cH.h, hh - 1)
                        _ac = s.activation(sgb[hh % 2][:], psG[mc][:], AF.Silu)
                        _ac.then_inc(cL.h, 1)
                        cL.inc()
                        if hh >= 1:
                            hstore(hh - 1)
            hstore(8 * NJP - 1)
        block.scalar(sC)

        def vC(v):
            for jp in range(NJP):
                for z in range(2):
                    q2 = 2 * jp + z
                    sh, sv = _wg_slot(jp, z)
                    v.wait_ge(sh, sv)
                    if jp >= 1:
                        v.wait_ge(cS.h, 2 * (jp - 1) + z + 1)
                    _a = v.scalar_tensor_tensor(absC[z][:], rawC[z][jp % 2][:],
                                                -1.0, rawC[z][jp % 2][:],
                                                OP.mult, OP.max)
                    _a.then_inc(cA.h, 1)
                    cA.inc()
                    v.wait_ge(cS.h, q2 + 1)
                    if jp >= 1:
                        v.wait_ge(cB.h, 16 * (2 * (jp - 1) + z) + 16)
                    _r1 = v.tensor_scalar(thr16C[z][:], cSum[0:32, 0:256],
                                          1.0 / 64, None, OP.mult)
                    v.wait_ge(vq.h, vchain(_r1))
                    _r2 = v.tensor_scalar(thr16C[z][:], thr16C[z][:],
                                          0.5, 5e-9, OP.mult, OP.max)
                    _r2.then_inc(cT.h, 1)
                    cT.inc()
                    v.wait_ge(cM.h, 16 * q2 + 16)
                    if jp >= 2:
                        v.wait_ge(cG.h, 8 * (jp - 2) + 8)
                    v.tensor_tensor(gtc[:], rawC[z][jp % 2][:], thrBC[z][:],
                                    OP.is_gt)
                    _l = v.scalar_tensor_tensor(ltc[:], rawC[z][jp % 2][:],
                                                -1.0, thrBC[z][:],
                                                OP.mult, OP.is_gt)
                    v.wait_ge(vq.h, vchain(_l))
                    _s = v.tensor_tensor(gtc[:], gtc[:], ltc[:], OP.subtract)
                    v.wait_ge(vq.h, vchain(_s))
                    _w = v.scalar_tensor_tensor(wqbC[z][jp % 2][:], gtc[:],
                                                2.0, thrBC[z][:],
                                                OP.mult, OP.mult)
                    _w.then_inc(cQ.h, 1)
                    cQ.inc()
                for tc in range(4):
                    for mc in range(2):
                        hh = 8 * jp + 2 * tc + mc
                        v.wait_ge(cL.h, hh + 1)
                        if hh >= 2:
                            sl = s_hh.sems[hh % 2]
                            v.wait_ge(sl.h, 16 * (hh // 2))
                        _h = v.tensor_tensor(hjb[hh % 2][:], sgb[hh % 2][:],
                                             psU[mc][:], OP.mult)
                        hv = vchain(_h)
                        if 4 * jp + tc >= 1:
                            v.wait_ge(cR.h, 4 * jp + tc)
                        v.wait_ge(vq.h, hv)
                        _q = v.tensor_tensor(hsqb[mc][:], hjb[hh % 2][:],
                                             hjb[hh % 2][:], OP.mult)
                        _q.then_inc(cH.h, 1)
                        cH.inc()
                    rr = 4 * jp + tc
                    v.wait_ge(cR.h, rr + 1)
                    if jp >= 1:
                        v.wait_ge(cZ.h, 4 * (jp - 1) + tc + 1)
                    tsl = slice(512 * tc, 512 * (tc + 1))
                    if jp == 0:
                        _z = v.tensor_copy(ssq[0:1, tsl], psR[0:1, :])
                    else:
                        _z = v.tensor_tensor(ssq[0:1, tsl], ssq[0:1, tsl],
                                             psR[0:1, :], OP.add)
                    _z.then_inc(cZ.h, 1)
                    cZ.inc()
        block.vector(vC)
        ctxC.close()
        ctxX.close()

        ctxR = ExitStack()
        psB2 = [pr.ps(f"psB2{i}", [128, 512], F32, ctxR) for i in range(2)]
        CT0, CL0, CR0, CM0 = 2 * NJP, 8 * NJP, 4 * NJP, 32 * NJP

        def vR(v):
            v.wait_ge(cZ.h, 4 * NJP)
            _r = v.tensor_scalar(ssq[:], ssq[:], 1.0 / NH, 1.1920929e-7,
                                 OP.mult, OP.add)
            _r.then_inc(cT.h, 1)
            cT.inc()
            v.wait_ge(cL.h, CL0 + 1)
            _c = v.tensor_copy(rn16[:], ssq[:])
            _c.then_inc(cT.h, 1)
            cT.inc()
        block.vector(vR)

        def sR(s):
            s.wait_ge(cT.h, CT0 + 1)
            s.activation(ssq[:], ssq[:], AF.Ln)
            s.drain()
            _e = s.activation(ssq[:], ssq[:], AF.Exp, scale=-0.5)
            _e.then_inc(cL.h, 1)
            cL.inc()
            for tcc in range(4):
                s.wait_ge(cR.h, CR0 + tcc + 1)
                _cp = s.copy(RNb[:, 512 * tcc:512 * (tcc + 1)],
                             psB2[tcc % 2][:])
                _cp.then_inc(cM.h, 1)
                cM.inc()
        block.scalar(sR)

        def tR(t):
            t.wait_ge(cT.h, CT0 + 2)
            for tcc in range(4):
                if tcc >= 2:
                    t.wait_ge(cM.h, CM0 + tcc - 1)
                _mm = t.matmul(psB2[tcc % 2][:], o1[:],
                               rn16[:, 512 * tcc:512 * (tcc + 1)],
                               start=True, stop=True)
                tmark(t, cR, _mm)
        block.tensor(tR)
        ctxR.close()

        ctxD = ExitStack()
        TD = 1024
        hn = [pr.sb(f"hn{i}", [128, TD], BF16, ctxD) for i in range(64)]
        wpb = [pr.sb(f"wpb{i}", [128, NH], BF16, ctxD) for i in range(2)]
        x1f = [pr.sb(f"x1f{i}", [128, TD], F32, ctxD) for i in range(2)]
        anb = [pr.sb(f"anb{i}", [128, TD], BF16, ctxD) for i in range(2)]
        fo = [pr.sb(f"fo{i}", [128, TD], F32, ctxD) for i in range(2)]
        psO = [pr.ps(f"psO{i}", [128, TD], F32, ctxD) for i in range(2)]
        dH = pr.sem("dH")
        dT = pr.sem("dT")
        dV = pr.sem("dV")
        s_hn = Stream(pr, "shn", 2)
        s_wpbg = Stream(pr, "swbg", 1)
        s_wpbq = Stream(pr, "swbq", 1)
        s_xa = Stream(pr, "sxa", 2)
        s_y = Stream(pr, "sy", 2)

        def sD(s):
            for tc2 in range(2):
                ts_ = slice(TD * tc2, TD * (tc2 + 1))
                for ht in range(64):
                    g2 = 64 * tc2 + ht
                    if ht == 0:
                        s.wait_ge(cM.h, CM0 + 4)
                        if tc2 == 0:
                            wait_all(s, s_hh.totals())
                        else:
                            s.wait_ge(dT.h, 16 * tc2)
                    if g2 >= 2:
                        s.wait_ge(dH.h, g2 - 1)
                    s_hn.issue(s, g2, [(hn[ht][:],
                                h_hbm[128 * ht:128 * (ht + 1), ts_], {})])
                for dc in range(16):
                    pp = 16 * tc2 + dc
                    s.wait_ge(dV.h, pp + 1)
                    s_y.issue(s, pp, [(yT_o[128 * dc:128 * (dc + 1), ts_],
                                       fo[dc % 2][:], {})])
        block.scalar(sD)

        def vD(v):
            for tc2 in range(2):
                ts_ = slice(TD * tc2, TD * (tc2 + 1))
                for ht in range(64):
                    g2 = 64 * tc2 + ht
                    sl = s_hn.sems[g2 % 2]
                    v.wait_ge(sl.h, 16 * (g2 // 2 + 1))
                    if g2 == 0:
                        v.wait_ge(cM.h, CM0 + 4)
                    _m = v.tensor_tensor(hn[ht][:], hn[ht][:], RNb[:, ts_],
                                         OP.mult)
                    _m.then_inc(dH.h, 1)
                    dH.inc()
                for dc in range(16):
                    pp = 16 * tc2 + dc
                    v.wait_ge(dT.h, pp + 1)
                    sl = s_xa.sems[dc % 2]
                    v.wait_ge(sl.h, 32 * (pp // 2 + 1))
                    if pp >= 2:
                        sy = s_y.sems[(pp - 2) % 2]
                        v.wait_ge(sy.h, 16 * ((pp - 2) // 2 + 1))
                    _f = v.tensor_tensor(fo[dc % 2][:], psO[dc % 2][:],
                                         x1f[dc % 2][:], OP.add)
                    _f.then_inc(dV.h, 1)
                    dV.inc()
        block.vector(vD)

        def gD(g):
            g.wait_ge(cM.h, CM0 + 4)
            wait_all(g, wq_done)
            for tc2 in range(2):
                ts_ = slice(TD * tc2, TD * (tc2 + 1))
                for dc in range(16):
                    pp = 16 * tc2 + dc
                    if dc % 2 == 0:
                        if pp >= 2:
                            g.wait_ge(dT.h, pp - 1)
                        s_wpbg.issue(g, 0, [(
                            wpb[0][:].rearrange("p (a c) -> p a c", c=128),
                            wqpT_hbm[:, 128 * dc:128 * (dc + 1)].rearrange(
                                "(a p) c -> p a c", p=128), {})])
                    if pp >= 2:
                        g.wait_ge(dV.h, pp - 1)
                    s_xa.issue(g, dc, [
                        (x1f[dc % 2][:],
                         x1T_d[128 * dc:128 * (dc + 1), ts_], {}),
                        (anb[dc % 2][:].rearrange("p (a c) -> p a c", c=128),
                         opa_d[ts_, 128 * dc:128 * (dc + 1)].rearrange(
                             "(a p) c -> p a c", p=128), {})])
        block.gpsimd(gD)

        def qD(q):
            q.wait_ge(cM.h, CM0 + 4)
            wait_all(q, wq_done)
            for tc2 in range(2):
                for dc in range(1, 16, 2):
                    pp = 16 * tc2 + dc
                    if pp >= 2:
                        q.wait_ge(dT.h, pp - 1)
                    s_wpbq.issue(q, 0, [(
                        wpb[1][:].rearrange("p (a c) -> p a c", c=128),
                        wqpT_hbm[:, 128 * dc:128 * (dc + 1)].rearrange(
                            "(a p) c -> p a c", p=128), {})])
        block.sync(qD)

        def tD(t):
            t.wait_ge(*const_t)
            for tc2 in range(2):
                for dc in range(16):
                    pp = 16 * tc2 + dc
                    if dc == 0:
                        t.wait_ge(dH.h, 64 * tc2 + 64)
                    stw = s_wpbg if dc % 2 == 0 else s_wpbq
                    t.wait_ge(stw.sems[0].h, 16 * (pp // 2 + 1))
                    sa = s_xa.sems[dc % 2]
                    t.wait_ge(sa.h, 32 * (pp // 2 + 1))
                    if pp >= 2:
                        t.wait_ge(dV.h, pp - 1)
                    for ht in range(64):
                        for c in range(2):
                            t.matmul(psO[dc % 2][:, 512 * c:512 * (c + 1)],
                                     wpb[dc % 2][:, 128 * ht:128 * (ht + 1)],
                                     hn[ht][:, 512 * c:512 * (c + 1)],
                                     start=(ht == 0), stop=False)
                    for j in range(8):
                        _mm = t.matmul(psO[dc % 2][:, 128 * j:128 * (j + 1)],
                                       anb[dc % 2][:, 128 * j:128 * (j + 1)],
                                       ident[:], start=False,
                                       stop=(j in (3, 7)))
                    tmark(t, dT, _mm)
        block.tensor(tD)

        def qfin(q):
            wait_all(q, s_y.totals())
        block.sync(qfin)
        pr.finish = (s_y.totals() + s_hh.totals() +
                     [(dT.h, dT.v), (dV.h, dV.v), (dH.h, dH.v),
                      (cM.h, cM.v), (cZ.h, cZ.v), (bC.h, bC.v)])
        ctxD.close()
        pr.ctx.pop_all()

    return pr



RG_PAIR = [[0, 1], [2, 3], [4, 5], [6, 7]]


def build_fused(nphase=5, nocoll=False):
    pr = Prog("fz")
    nc = pr.nc
    io_sh = {
        "ones1": pr.din("ones1", [1, 128], BF16),
        "onescol": pr.din("onescol", [128, 1], BF16),
        "ident_d": pr.din("ident_d", [128, 128], BF16),
    }
    ins = {
        "xT": pr.din("xT", [D, T], F32),
        "winT": pr.din("winT", [D, 3 * D], F32),
        "binp": pr.din("binp", [128, 48], F32),
        "cosT": pr.din("cosT", [512, T], BF16),
        "sinT": pr.din("sinT", [512, T], BF16),
        "wq": pr.din("wq", [D, 512], F32),
        "keysT": pr.din("keysT", [8, 64, 128], F32),
        "iota64": pr.din("iota64", [128, 64], F32),
        "ecat": pr.din("ecat", [16384, 4096], F32),
        "woutT": pr.din("woutT", [D, D], F32),
        "bout_d": pr.din("bout_d", [128, 16], F32),
        "kvaT": pr.din("kvaT", [D, 576], F32),
        "lnw_d": pr.din("lnw_d", [128, 4], F32),
        "kvbn": pr.din("kvbn", [512, 2048], F32),
        "kvbv": pr.din("kvbv", [512, 2048], F32),
        "qpT": pr.din("qpT", [D, 1536], F32),
        "opT": pr.din("opT", [1024, D], F32),
        "masks": pr.din("masks", [4, 128, 512], F32),
        "wguT": pr.din("wguT", [D, 8 * D], F32),
        "wprT": pr.din("wprT", [4 * D, D], F32),
        "selm_d": pr.din("selm_d", [32, 2048], BF16),
        "indm_d": pr.din("indm_d", [128, 254], BF16),
    }
    yT_o = pr.dout("yT_o", [D, T], F32)

    scr = {
        "xnT_s": pr.dtmp("xnT_s", [D, T], BF16),
        "qT_s": pr.dtmp("qT_s", [D, T], BF16),
        "apart_s": pr.dtmp("apart_s", [128, 16], F32),
        "apart_sum": pr.dtmp("apart_sum", [128, 16], F32),
        "fi_s": pr.dtmp("fi_s", [16, 128, 32], U32),
        "fs_s": pr.dtmp("fs_s", [16, 128, 32], F32),
        "x1T_s": pr.dtmp("x1T_s", [D, T], F32),
        "x1s2": pr.dtmp("x1s2", [D, T], F32),
        "xn2T_s": pr.dtmp("xn2T_s", [D, T], BF16),
        "xn2f_ag": pr.dtmp("xn2f_ag", [2 * D, T], BF16),
        "kT_s": pr.dtmp("kT_s", [2048, S], BF16),
        "krot_s": pr.dtmp("krot_s", [64, S], BF16),
        "v_s": pr.dtmp("v_s", [S, 2048], BF16),
        "q2T_s": pr.dtmp("q2T_s", [1536, S], BF16),
        "op_s": pr.dtmp("op_s", [S, D], BF16),
        "opsum_s": pr.dtmp("opsum_s", [T, D], BF16),
    }

    scr["ecat16"] = pr.dtmp("ecat16", [16384, 4096], BF16)
    scr["wguT16"] = pr.dtmp("wguT16", [D, 8 * D], BF16)
    scr["wprT16"] = pr.dtmp("wprT16", [4 * D, D], BF16)
    gbar = pr.sem("gbar")
    ccA = pr.sem("ccA")
    ccG = pr.sem("ccG")
    ccR = pr.sem("ccR")
    cast_s = pr.sem("cast_s")
    cast_w = pr.sem("cast_w")

    pr.block = pr.ctx.enter_context(pr.nc.Block())

    def gcast(g):
        for ch in range(16):
            g.dma_start(out=scr["ecat16"][1024 * ch:1024 * (ch + 1), :],
                        in_=ins["ecat"][1024 * ch:1024 * (ch + 1), :]
                        ).then_inc(cast_s.h, 16)
            cast_s.inc(16)
    pr.block.gpsimd(gcast)
    cast_t = (cast_s.h, cast_s.v)

    def wcast(g):
        for ch in range(4):
            g.dma_start(out=scr["wprT16"][2048 * ch:2048 * (ch + 1), :],
                        in_=ins["wprT"][2048 * ch:2048 * (ch + 1), :]
                        ).then_inc(cast_w.h, 16)
            cast_w.inc(16)
        for ch in range(4):
            g.dma_start(out=scr["wguT16"][512 * ch:512 * (ch + 1), :],
                        in_=ins["wguT"][512 * ch:512 * (ch + 1), :]
                        ).then_inc(cast_w.h, 16)
            cast_w.inc(16)
    cast_wt = (cast_w.h, 16 * 8)

    def phase(fn, io, coll=None, extra_fin=None):
        pr.pctx = ExitStack()
        fn(pr=pr, io=io, gate=(gbar.h, gbar.v) if gbar.v else None)
        pr.pctx.close()
        pr.pctx = None
        block = pr.block

        def qbar(q, fin=pr.finish + (extra_fin or [])):
            wait_all(q, fin)
            q.nop().then_inc(gbar.h, 1)
        block.sync(qbar)
        gbar.inc()

        if coll is not None and nocoll:
            coll = None
        if coll is not None:
            kind, op_, cin, cout, csem = coll

            def gcc(g, gv=gbar.v):
                g.wait_ge(gbar.h, gv)
                g.collective_compute(kind, op_, ins=[cin.opt()], outs=[cout.opt()],
                                     replica_groups=RG_PAIR).then_inc(csem.h, 1)
            block.gpsimd(gcc)
            csem.inc(1)

            def qcc(q, cs=(csem.h, csem.v)):
                q.wait_ge(*cs)
                q.nop().then_inc(gbar.h, 1)
            block.sync(qcc)
            gbar.inc()

    pr.prefix = "a_"
    phase(build_l0a,
          {**io_sh, **{k: ins[k] for k in ["xT", "winT", "binp", "cosT", "sinT",
                                           "wq", "keysT", "iota64"]},
           "xnT_o": scr["xnT_s"], "qT_o": scr["qT_s"], "apart_o": scr["apart_s"],
           "fi_o": scr["fi_s"], "fs_o": scr["fs_s"]},
          coll=("AllReduce", OP.add, scr["apart_s"], scr["apart_sum"], ccA),
          extra_fin=[cast_t])

    if nphase <= 1:
        return pr
    pr.prefix = "b_"
    phase(build_l0b,
          {**io_sh, "xT": ins["xT"], "xnT_i": scr["xnT_s"], "qT_i": scr["qT_s"],
           "ap0": scr["apart_sum"], "fi_i": scr["fi_s"], "fs_i": scr["fs_s"],
           "ecat": scr["ecat16"], "woutT": ins["woutT"], "bout_d": ins["bout_d"],
           "x1T_o": scr["x1T_s"], "xn2T_o": scr["xn2T_s"], "x1s": scr["x1s2"]},
          coll=("AllGather", OP.bypass, scr["xn2T_s"], scr["xn2f_ag"], ccG))
    pr.block.gpsimd(wcast)

    if nphase <= 2:
        return pr
    pr.prefix = "c_"
    phase(build_l1a,
          {**io_sh, "xn2f_ag": scr["xn2f_ag"],
           **{k: ins[k] for k in ["kvaT", "lnw_d", "kvbn", "kvbv", "qpT"]},
           "kT_o": scr["kT_s"], "krotT_o": scr["krot_s"], "v_o": scr["v_s"],
           "q2T_o": scr["q2T_s"]})

    if nphase <= 3:
        return pr
    pr.prefix = "d_"
    phase(build_l1b,
          {**io_sh, "kT": scr["kT_s"][0:1024, :], "krotT": scr["krot_s"],
           "v": scr["v_s"][:, 0:1024], "qT": scr["q2T_s"], "opT": ins["opT"],
           "masks": ins["masks"], "op_o": scr["op_s"]},
          coll=("ReduceScatter", OP.add, scr["op_s"], scr["opsum_s"], ccR))

    if nphase <= 4:
        return pr
    pr.prefix = "e_"
    phase(build_l1c,
          {**io_sh, "xn2T": scr["xn2T_s"], "x1T": scr["x1T_s"],
           "opa": scr["opsum_s"], "wgut16": scr["wguT16"],
           "wprt16": scr["wprT16"], "selm_d": ins["selm_d"],
           "indm_d": ins["indm_d"], "w16_ready": [cast_wt],
           "yT_o": yT_o})

    return pr


def fused_inmaps(prep):
    cos, sin = prep["cosT"], prep["sinT"]
    maps = []
    for c in range(NCORE):
        r = c % 2
        horder = list(range(r * 8, r * 8 + 8)) + list(range((1 - r) * 8,
                                                            (1 - r) * 8 + 8))
        cols = np.concatenate([np.arange(128) + 128 * h for h in horder])
        maps.append({
            "xT": prep["xT"][c],
            "winT": prep["winT"], "binp": prep["binp"],
            "cosT": np.ascontiguousarray(cos[:, r * T:(r + 1) * T]),
            "sinT": np.ascontiguousarray(sin[:, r * T:(r + 1) * T]),
            "wq": prep["wq"], "keysT": prep["keysT"], "iota64": prep["iota64"],
            "ecat": prep["e_cat"], "woutT": prep["woutT"], "bout_d": prep["bout"],
            "kvaT": prep["kvaT"], "lnw_d": prep["lnw"],
            "kvbn": np.ascontiguousarray(prep["kvbT_nope"][:, cols]),
            "kvbv": np.ascontiguousarray(prep["kvbT_v"][:, cols]),
            "qpT": np.ascontiguousarray(prep["qprojT"][:, r * 1536:(r + 1) * 1536]),
            "opT": np.ascontiguousarray(prep["oprojT"][r * 1024:(r + 1) * 1024]),
            "masks": prep["masks"],
            "wguT": prep["wguT"], "wprT": prep["wprT"],
            "selm_d": prep["selm"], "indm_d": prep["indm"],
            "ones1": prep["ones1"], "onescol": prep["onescol"],
            "ident_d": prep["ident"],
        })
    return maps


_PROGS = {}


def _prog(name):
    if name not in _PROGS:
        _PROGS[name] = {"l0a": build_l0a, "l0b": build_l0b, "l1a": build_l1a,
                        "l1b": build_l1b, "l1c": build_l1c}[name]()
    return _PROGS[name]


def _run(name, maps):
    pr = _prog(name)
    res = run_bass_kernel_spmd(pr.nc, maps, core_ids=list(range(NCORE)))
    return res.results


MODE = "fused"


def kernel(**inputs):
    prep = host_prep(inputs)
    if MODE == "fused":
        if "fused" not in _PROGS:
            _PROGS["fused"] = build_fused()
        pr = _PROGS["fused"]
        res = run_bass_kernel_spmd(pr.nc, fused_inmaps(prep),
                                   core_ids=list(range(NCORE)))
        out = np.empty((4, S, D), np.float32)
        for c in range(NCORE):
            b, r = c // 2, c % 2
            out[b, r * T:(r + 1) * T, :] = res.results[c]["yT_o"].T
        return out
    raise RuntimeError("only MODE='fused' is supported")


# version 16 (final)
# speedup vs baseline: 1.0342x; 1.0342x over previous
import numpy as np
import ml_dtypes as mld
from contextlib import ExitStack

import concourse.bass as bass
import concourse.mybir as mybir
from concourse.alu_op_type import AluOpType as OP
from concourse.bass_utils import run_bass_kernel_spmd

F32 = mybir.dt.float32
BF16 = mybir.dt.bfloat16
U32 = mybir.dt.uint32
AF = mybir.ActivationFunctionType
AXX = mybir.AxisListType.X
IOA = bass.IndirectOffsetOnAxis

NCORE = 8
D = 2048
T = 2048
S = 4096
NDT = D // 128
NTC = T // 512
ROT = 1024
NEG = -1e9
QSCALE = 192.0 ** -0.5



def _rope_perm(n):
    return np.concatenate([np.arange(0, ROT, 2), np.arange(1, ROT, 2),
                           np.arange(ROT, n)]).astype(np.int64)


def host_prep(inp):
    p = {}
    perm = _rope_perm(D)
    perm_all = np.concatenate([perm, perm + D, perm + 2 * D])
    W_in = np.asarray(inp["L0_attn_W_in"])
    p["winT"] = np.ascontiguousarray(W_in[perm_all].T)
    p["binp"] = np.ascontiguousarray(
        np.asarray(inp["L0_attn_b_in"])[perm_all].reshape(48, 128).T)
    W_out = np.asarray(inp["L0_attn_W_out"])
    p["woutT"] = np.ascontiguousarray(W_out[:, perm].T)
    p["bout"] = np.ascontiguousarray(
        np.asarray(inp["L0_attn_b_out"]).reshape(16, 128).T)
    p["wq"] = np.ascontiguousarray(np.asarray(inp["L0_ffn_Wq"]))
    keys = np.asarray(inp["L0_ffn_keys"])
    p["keysT"] = np.ascontiguousarray(
        keys.transpose(0, 2, 3, 1).reshape(8, 64, 128))
    p["e_cat"] = np.ascontiguousarray(
        np.concatenate([np.asarray(inp["L0_ffn_e_down"]),
                        np.asarray(inp["L0_ffn_e_up"])], axis=1))

    p["qprojT"] = np.ascontiguousarray(np.asarray(inp["L1_attn_q_proj_w"]).T)
    p["kvaT"] = np.ascontiguousarray(np.asarray(inp["L1_attn_kv_a_proj_w"]).T)
    p["lnw"] = np.ascontiguousarray(
        np.asarray(inp["L1_attn_kv_a_ln_w"]).reshape(4, 128).T)
    kvbT = np.asarray(inp["L1_attn_kv_b_proj_w"]).T
    nope_cols = (np.arange(16)[:, None] * 256 + np.arange(128)[None]).reshape(-1)
    p["kvbT_nope"] = np.ascontiguousarray(kvbT[:, nope_cols])
    p["kvbT_v"] = np.ascontiguousarray(kvbT[:, nope_cols + 128])
    p["oprojT"] = np.ascontiguousarray(np.asarray(inp["L1_attn_o_proj_w"]).T)
    p["wguT"] = np.ascontiguousarray(np.asarray(inp["L1_ffn_w_gate_up"]).T)
    p["wprT"] = np.ascontiguousarray(np.asarray(inp["L1_ffn_w_proj"]).T)
    selm = np.zeros((32, 2048), dtype=mld.bfloat16)
    jj = np.arange(2048)
    selm[jj // 64, jj] = 1
    p["selm"] = selm
    indm = np.zeros((128, 254), dtype=mld.bfloat16)
    kk = np.arange(128)
    indm[kk, kk // 64 + 126] = 1
    p["indm"] = indm

    X = np.asarray(inp["L0_attn_X"])
    p["xT"] = [np.ascontiguousarray(X[c // 2, (c % 2) * T:(c % 2 + 1) * T, :].T)
               for c in range(NCORE)]

    invf = 10000.0 ** (-(np.arange(512, dtype=np.float64) * 2.0 / ROT))
    ang = np.arange(S, dtype=np.float64)[None, :] * invf[:, None]
    p["cosT"] = np.cos(ang).astype(mld.bfloat16)
    p["sinT"] = np.sin(ang).astype(mld.bfloat16)
    p["ones1"] = np.ones((1, 128), dtype=mld.bfloat16)
    p["onescol"] = np.ones((128, 1), dtype=mld.bfloat16)
    p["ident"] = np.eye(128, dtype=np.float32).astype(mld.bfloat16)
    p["iota64"] = np.broadcast_to(np.arange(64, dtype=np.float32), (128, 64)).copy()
    masks = np.zeros((4, 128, 512), dtype=np.float32)
    for j in range(4):
        masks[j] = np.where(np.arange(128)[:, None] + 128 * j > np.arange(512)[None, :],
                            NEG, 0.0)
    p["masks"] = masks
    return p



class Sem:
    def __init__(self, nc, ctx, name):
        self.h = ctx.enter_context(nc.semaphore(name))
        self.v = 0

    def inc(self, n=1):
        self.v += n
        return self.v


class Prog:
    def __init__(self, name):
        self.nc = bass.Bass("TRN2", target_bir_lowering=False)
        self.name = name
        self.ctx = ExitStack()
        self.pctx = None
        self.prefix = ""
        self.block = None

    def sem(self, name):
        return Sem(self.nc, self.ctx, self.prefix + name)

    def sb(self, name, shape, dt, ctx=None):
        return (ctx or self.pctx or self.ctx).enter_context(
            self.nc.sbuf_tensor(self.prefix + name, list(shape), dt))

    def ps(self, name, shape, dt=F32, ctx=None):
        return (ctx or self.pctx or self.ctx).enter_context(
            self.nc.psum_tensor(self.prefix + name, list(shape), dt))

    def din(self, name, shape, dt):
        return self.nc.dram_tensor(name, list(shape), dt, kind="ExternalInput").ap()

    def dout(self, name, shape, dt):
        return self.nc.dram_tensor(name, list(shape), dt, kind="ExternalOutput").ap()

    def dtmp(self, name, shape, dt):
        return self.nc.dram_tensor(self.prefix + name, list(shape), dt,
                                   kind="Internal").ap()


def mark(eng, msem):
    eng.drain()
    eng.nop().then_inc(msem.h, 1)
    msem.inc()
    return msem.v


def tmark(t, msem, inst=None):
    (inst if inst is not None else t.nop()).then_inc(msem.h, 1)
    msem.inc()
    return msem.v


def get_io(pr, io, name, shape, dt, out=False):
    if io is not None and name in io:
        return io[name]
    return (pr.dout if out else pr.din)(name, list(shape), dt)


def gate_all(block, gate):
    if gate is None:
        return
    for emit in (block.sync, block.tensor, block.vector, block.scalar,
                 block.gpsimd):
        emit(lambda e, g=gate: e.wait_ge(g[0], g[1]))


class Stream:

    def __init__(self, pr, name, nslot):
        self.sems = [pr.sem(f"{name}{i}") for i in range(nslot)]
        self.n = nslot

    def issue(self, eng, slot, dmas):
        sem = self.sems[slot % self.n]
        for out, in_, kw in dmas:
            eng.dma_start(out=out, in_=in_, **kw).then_inc(sem.h, 16)
            sem.inc(16)
        return (sem.h, sem.v)

    def totals(self):
        return [(s.h, s.v) for s in self.sems]


def wait_all(eng, targets):
    for h, v in targets:
        if v > 0:
            eng.wait_ge(h, v)



def emit_ln(pr, b, src_dram, xnT, ms, eps=1e-5, tag="ln"):
    block = pr.block
    vm, sm, pe = ms["vm"], ms["sm"], ms["pe"]
    ctx = ExitStack()
    xf = [pr.sb(f"{tag}_xf{i}", [128, T], F32, ctx) for i in range(2)]
    xb = pr.sb(f"{tag}_xb", [128, T], BF16, ctx)
    xsq = pr.sb(f"{tag}_xsq", [128, T], BF16, ctx)
    m_row = pr.sb(f"{tag}_m", [1, T], F32, ctx)
    r_row = pr.sb(f"{tag}_r", [1, T], F32, ctx)
    sc_row = pr.sb(f"{tag}_sc", [1, T], F32, ctx)
    rb16 = pr.sb(f"{tag}_rb16", [1, T], BF16, ctx)
    mrb16 = pr.sb(f"{tag}_mrb16", [1, T], BF16, ctx)
    Rb = pr.sb(f"{tag}_Rb", [128, T], BF16, ctx)
    MRb = pr.sb(f"{tag}_MRb", [128, T], BF16, ctx)
    tmp = pr.sb(f"{tag}_tmp", [128, T], F32, ctx)
    psA = pr.ps(f"{tag}_psA", [128, 2048], F32, ctx)
    psB = pr.ps(f"{tag}_psB", [128, 2048], F32, ctx)
    s_x = Stream(pr, f"{tag}sx", 2)
    ocol, o1 = b["ocol"], b["o1"]

    cons = []
    for dt in range(NDT):
        def qf(q, dt=dt):
            if dt >= 2:
                q.wait_ge(vm.h, cons[dt - 2][0])
                q.wait_ge(sm.h, cons[dt - 2][1])
            s_x.issue(q, dt, [(xf[dt % 2][:],
                               src_dram[128 * dt:128 * (dt + 1), :], {})])
        block.sync(qf)
        lt = (s_x.sems[dt % 2].h, s_x.sems[dt % 2].v)

        def vf(v, dt=dt, lt=lt):
            v.wait_ge(*lt)
            if dt >= 1:
                v.wait_ge(pe.h, pe.v)
            v.tensor_copy(xb[:], xf[dt % 2][:])
            mark(v, vm)
        block.vector(vf)

        def sf(s, dt=dt, lt=lt):
            s.wait_ge(*lt)
            if dt >= 1:
                s.wait_ge(pe.h, pe.v)
            s.activation(xsq[:], xf[dt % 2][:], AF.Square)
            mark(s, sm)
        block.scalar(sf)
        cons.append((vm.v, sm.v))

        def tf(t, dt=dt, vv=vm.v, sv=sm.v):
            t.wait_ge(vm.h, vv)
            t.wait_ge(sm.h, sv)
            for tc in range(NTC):
                _mm = t.matmul(psA[0:1, 512 * tc:512 * (tc + 1)], ocol[:],
                         xb[:, 512 * tc:512 * (tc + 1)],
                         start=(dt == 0), stop=(dt == NDT - 1))
                _mm = t.matmul(psB[0:1, 512 * tc:512 * (tc + 1)], ocol[:],
                         xsq[:, 512 * tc:512 * (tc + 1)],
                         start=(dt == 0), stop=(dt == NDT - 1))
            tmark(t, pe, _mm)
        block.tensor(tf)

    def vrows(v, pv=pe.v):
        v.wait_ge(pe.h, pv)
        v.tensor_scalar(m_row[:], psA[0:1, :], 1.0 / D, None, OP.mult)
        v.tensor_scalar(sc_row[:], psB[0:1, :], 1.0 / D, None, OP.mult)
        v.drain()
        v.tensor_tensor(r_row[:], m_row[:], m_row[:], OP.mult)
        v.drain()
        v.tensor_tensor(sc_row[:], sc_row[:], r_row[:], OP.subtract)
        v.drain()
        v.tensor_scalar(sc_row[:], sc_row[:], eps, None, OP.add)
        mark(v, vm)
    block.vector(vrows)

    def srows(s, vv=vm.v):
        s.wait_ge(vm.h, vv)
        s.activation(r_row[:], sc_row[:], AF.Ln)
        s.drain()
        s.activation(r_row[:], r_row[:], AF.Exp, scale=-0.5)
        mark(s, sm)
    block.scalar(srows)

    def vmr(v, sv=sm.v):
        v.wait_ge(sm.h, sv)
        v.tensor_tensor(sc_row[:], m_row[:], r_row[:], OP.mult)
        v.drain()
        v.tensor_copy(rb16[:], r_row[:])
        v.tensor_copy(mrb16[:], sc_row[:])
        mark(v, vm)
    block.vector(vmr)

    def tbc(t, vv=vm.v):
        t.wait_ge(vm.h, vv)
        for tc in range(NTC):
            _mm = t.matmul(psA[:, 512 * tc:512 * (tc + 1)], o1[:],
                     rb16[:, 512 * tc:512 * (tc + 1)], start=True, stop=True)
            _mm = t.matmul(psB[:, 512 * tc:512 * (tc + 1)], o1[:],
                     mrb16[:, 512 * tc:512 * (tc + 1)], start=True, stop=True)
        tmark(t, pe, _mm)
    block.tensor(tbc)

    def vbc(v, pv=pe.v):
        v.wait_ge(pe.h, pv)
        v.tensor_copy(Rb[:], psA[:])
        v.tensor_copy(MRb[:], psB[:])
        mark(v, vm)
    block.vector(vbc)
    bc = vm.v

    marks = []
    for dt in range(NDT):
        def qf2(q, dt=dt):
            q.wait_ge(vm.h, bc if dt < 2 else marks[dt - 2])
            s_x.issue(q, dt, [(xf[dt % 2][:],
                               src_dram[128 * dt:128 * (dt + 1), :], {})])
        block.sync(qf2)
        lt = (s_x.sems[dt % 2].h, s_x.sems[dt % 2].v)

        def vf2(v, dt=dt, lt=lt):
            v.wait_ge(*lt)
            v.tensor_tensor(tmp[:], xf[dt % 2][:], Rb[:], OP.mult)
            v.drain()
            v.tensor_tensor(xnT[dt][:], tmp[:], MRb[:], OP.subtract)
            mark(v, vm)
        block.vector(vf2)
        marks.append(vm.v)

    ctx.close()
    return vm.v



def build_l0a(cut="all", pr=None, io=None, gate=None):
    if pr is None:
        pr = Prog("l0a")
    nc = pr.nc
    xT = get_io(pr, io, "xT", [D, T], F32)
    winT = get_io(pr, io, "winT", [D, 3 * D], F32)
    binp = get_io(pr, io, "binp", [128, 48], F32)
    cosT_d = get_io(pr, io, "cosT", [512, T], BF16)
    sinT_d = get_io(pr, io, "sinT", [512, T], BF16)
    wq_d = get_io(pr, io, "wq", [D, 512], F32)
    keysT = get_io(pr, io, "keysT", [8, 64, 128], F32)
    ones1 = get_io(pr, io, "ones1", [1, 128], BF16)
    onescol = get_io(pr, io, "onescol", [128, 1], BF16)
    iota64 = get_io(pr, io, "iota64", [128, 64], F32)

    xnT_o = get_io(pr, io, "xnT_o", [D, T], BF16, out=True)
    qT_o = get_io(pr, io, "qT_o", [D, T], BF16, out=True)
    apart_o = get_io(pr, io, "apart_o", [128, 16], F32, out=True)
    fi_o = get_io(pr, io, "fi_o", [16, 128, 32], U32, out=True)
    fs_o = get_io(pr, io, "fs_o", [16, 128, 32], F32, out=True)

    with pr.ctx:
        xnT = [pr.sb(f"xnT{i}", [128, T], BF16) for i in range(NDT)]
        o1 = pr.sb("o1", [1, 128], BF16)
        ocol = pr.sb("ocol", [128, 1], BF16)
        bcol = pr.sb("bcol", [128, 48], F32)
        acol = pr.sb("acol", [128, 16], F32)

        pe = pr.sem("pe")
        vm = pr.sem("vm")
        sm = pr.sem("sm")
        st = pr.sem("st")
        ms = {"pe": pe, "vm": vm, "sm": sm, "st": st}
        s_const = Stream(pr, "cst", 1)
        s_w = Stream(pr, "sw", 2)
        s_cs = Stream(pr, "scs", 2)
        s_wq = Stream(pr, "swq", 1)
        s_qst = Stream(pr, "sqst", 1)
        s_tk = Stream(pr, "stk", 1)

        if pr.block is None:
            pr.block = pr.ctx.enter_context(nc.Block())
        block = pr.block
        gate_all(block, gate)
        b = {"o1": o1, "ocol": ocol}

        def qconst(q):
            s_const.issue(q, 0, [
                (o1[:], ones1[:], {}),
                (ocol[:], onescol[:], {}),
                (bcol[:], binp[:], {}),
            ])
        block.sync(qconst)
        const_t = s_const.totals()[0]

        def vconst(v):
            v.wait_ge(*const_t)
            mark(v, vm)
        block.vector(vconst)

        emit_ln(pr, b, xT, xnT, ms)
        xn_done = vm.v

        def qxn(q):
            q.wait_ge(vm.h, xn_done)
            for dt in range(NDT):
                q.dma_start(out=xnT_o[128 * dt:128 * (dt + 1), :],
                            in_=xnT[dt][:]).then_inc(st.h, 16)
                st.inc(16)
        block.sync(qxn)

        if cut == "ln":
            def qfin0(q):
                q.wait_ge(st.h, st.v)
            block.sync(qfin0)
            pr.ctx.pop_all()
            return pr

        ctx2 = ExitStack()
        ktT = [pr.sb(f"ktT{i}", [128, T], BF16, ctx2) for i in range(NDT)]
        cosb = [pr.sb(f"cosb{i}", [128, T], BF16, ctx2) for i in range(2)]
        sinb = [pr.sb(f"sinb{i}", [128, T], BF16, ctx2) for i in range(2)]
        p1 = pr.sb("p1", [128, T], F32, ctx2)
        p2 = pr.sb("p2", [128, T], F32, ctx2)
        sqt = pr.sb("sqt", [128, T], BF16, ctx2)
        vtile = pr.sb("vtile", [128, T], BF16, ctx2)
        fkt = pr.sb("fkt", [128, T], BF16, ctx2)
        rrow = pr.sb("rrow", [1, T], F32, ctx2)
        rn16 = pr.sb("rn16", [1, T], BF16, ctx2)
        RNb = pr.sb("RNb", [128, T], BF16, ctx2)
        scr = pr.sb("scr", [128, T], BF16, ctx2)
        qtile = pr.sb("qtile", [128, T], BF16, ctx2)
        wbuf = [pr.sb(f"wb{i}", [128, NDT * 128], BF16, ctx2) for i in range(2)]
        psA = pr.ps("psA", [128, 2048], F32, ctx2)
        psB = pr.ps("psB", [128, 2048], F32, ctx2)

        cs_t = {}

        def csload(q, i, buf):
            cs_t[i] = s_cs.issue(q, buf, [
                (cosb[buf][:], cosT_d[128 * i:128 * (i + 1), :], {}),
                (sinb[buf][:], sinT_d[128 * i:128 * (i + 1), :], {}),
            ])

        def qcs(q, xv=xn_done):
            q.wait_ge(vm.h, xv)
            csload(q, 0, 0)
            csload(q, 1, 1)
        block.sync(qcs)

        w_t = {}

        def wload(g, e, buf):
            w_t[e] = s_w.issue(g, buf, [(
                wbuf[buf][:].rearrange("p (dt c) -> p dt c", c=128),
                winT[:, 128 * e:128 * (e + 1)].rearrange("(dt p) c -> p dt c", p=128),
                {})])

        def mm_echunk(t, e, psX, buf):
            t.wait_ge(*w_t[e])
            for dt in range(NDT):
                for tc in range(NTC):
                    _mm = t.matmul(psX[:, 512 * tc:512 * (tc + 1)],
                             wbuf[buf][:, dt * 128:dt * 128 + 128],
                             xnT[dt][:, 512 * tc:512 * (tc + 1)],
                             start=(dt == 0), stop=(dt == NDT - 1))
            return tmark(t, pe, _mm)

        def gk0(g, xv=xn_done):
            g.wait_ge(vm.h, xv)
            wload(g, 16, 0)
            wload(g, 20, 1)
        block.gpsimd(gk0)

        for i in range(4):
            def tk(t, i=i):
                if i > 0:
                    t.wait_ge(vm.h, vm.v)
                mm_echunk(t, 16 + i, psA, 0)
                mm_echunk(t, 20 + i, psB, 1)
            block.tensor(tk)
            pair_mm = pe.v

            if i < 3:
                def gk(g, i=i, pm=pair_mm):
                    g.wait_ge(pe.h, pm)
                    wload(g, 16 + i + 1, 0)
                    wload(g, 20 + i + 1, 1)
                block.gpsimd(gk)

            def vk(v, i=i, pm=pair_mm):
                v.wait_ge(pe.h, pm)
                v.wait_ge(*cs_t[i])
                cb, sb_ = cosb[i % 2], sinb[i % 2]
                v.scalar_tensor_tensor(p1[:], psA[:], bcol[:, 16 + i:17 + i], cb[:],
                                       OP.add, OP.mult)
                v.scalar_tensor_tensor(p2[:], psB[:], bcol[:, 20 + i:21 + i], sb_[:],
                                       OP.add, OP.mult)
                v.drain()
                v.tensor_tensor(ktT[i][:], p1[:], p2[:], OP.subtract)
                v.drain()
                v.scalar_tensor_tensor(p1[:], psB[:], bcol[:, 20 + i:21 + i], cb[:],
                                       OP.add, OP.mult)
                v.scalar_tensor_tensor(p2[:], psA[:], bcol[:, 16 + i:17 + i], sb_[:],
                                       OP.add, OP.mult)
                v.drain()
                v.tensor_tensor(ktT[4 + i][:], p1[:], p2[:], OP.add)
                mark(v, vm)
            block.vector(vk)

            if 0 < i < 3:
                def qcs2(q, i=i, vv=vm.v):
                    q.wait_ge(vm.h, vv)
                    csload(q, i + 1, (i + 1) % 2)
                block.sync(qcs2)

        if cut == "kpair":
            def qfin2(q, vv=vm.v):
                q.wait_ge(vm.h, vv)
                q.wait_ge(st.h, st.v)
            block.sync(qfin2)
            pr.ctx.pop_all()
            return pr

        def gkp(g, pv=pe.v):
            g.wait_ge(pe.h, pv)
            wload(g, 24, 0)
            wload(g, 25, 1)
        block.gpsimd(gkp)

        for j in range(8):
            def tkp(t, j=j, vv=vm.v):
                t.wait_ge(vm.h, vv if j == 0 else vm.v)
                mm_echunk(t, 24 + j, psA if j % 2 == 0 else psB, j % 2)
            block.tensor(tkp)
            mmv = pe.v
            if j < 6:
                def gkp2(g, j=j, mmv=mmv):
                    g.wait_ge(pe.h, mmv)
                    wload(g, 26 + j, j % 2)
                block.gpsimd(gkp2)

            def vkp(v, j=j, mmv=mmv):
                v.wait_ge(pe.h, mmv)
                v.tensor_scalar(ktT[8 + j][:], (psA if j % 2 == 0 else psB)[:],
                                bcol[:, 24 + j:25 + j], None, OP.add)
                mark(v, vm)
            block.vector(vkp)
        k_done = vm.v

        for dt in range(NDT):
            def sq(s, dt=dt, kd=k_done):
                s.wait_ge(vm.h, kd)
                if dt > 0:
                    s.wait_ge(pe.h, pe.v)
                s.activation(sqt[:], ktT[dt][:], AF.Square)
                mark(s, sm)
            block.scalar(sq)

            def tn(t, dt=dt, sv=sm.v):
                t.wait_ge(sm.h, sv)
                for tc in range(NTC):
                    _mm = t.matmul(psA[0:1, 512 * tc:512 * (tc + 1)], ocol[:],
                             sqt[:, 512 * tc:512 * (tc + 1)],
                             start=(dt == 0), stop=(dt == NDT - 1))
                tmark(t, pe, _mm)
            block.tensor(tn)

        def vnorm(v, pv=pe.v):
            v.wait_ge(pe.h, pv)
            v.tensor_scalar(rrow[:], psA[0:1, :], 1.0, 1e-24, OP.mult, OP.max)
            mark(v, vm)
        block.vector(vnorm)

        def snorm(s, vv=vm.v):
            s.wait_ge(vm.h, vv)
            s.activation(rrow[:], rrow[:], AF.Ln)
            s.drain()
            s.activation(rrow[:], rrow[:], AF.Exp, scale=-0.5)
            mark(s, sm)
        block.scalar(snorm)

        def vnorm2(v, sv=sm.v):
            v.wait_ge(sm.h, sv)
            v.tensor_scalar(rrow[:], rrow[:], 1e12, None, OP.min)
            v.drain()
            v.tensor_copy(rn16[:], rrow[:])
            mark(v, vm)
        block.vector(vnorm2)

        def trb(t, vv=vm.v):
            t.wait_ge(vm.h, vv)
            for tc in range(NTC):
                _mm = t.matmul(psA[:, 512 * tc:512 * (tc + 1)], o1[:],
                         rn16[:, 512 * tc:512 * (tc + 1)], start=True, stop=True)
            tmark(t, pe, _mm)
        block.tensor(trb)

        def vrb(v, pv=pe.v):
            v.wait_ge(pe.h, pv)
            v.tensor_copy(RNb[:], psA[:])
            v.memset(acol[:], 0.0)
            mark(v, vm)
        block.vector(vrb)
        rn_done = vm.v

        def gv0(g, pv=pe.v):
            g.wait_ge(pe.h, pv)
            wload(g, 32, 0)
            wload(g, 33, 1)
        block.gpsimd(gv0)

        for j in range(NDT):
            def tv(t, j=j, rd=rn_done):
                t.wait_ge(vm.h, rd if j == 0 else vm.v)
                mm_echunk(t, 32 + j, psB, j % 2)
            block.tensor(tv)
            mmv = pe.v
            if j < NDT - 2:
                def gv(g, j=j, mmv=mmv):
                    g.wait_ge(pe.h, mmv)
                    wload(g, 34 + j, j % 2)
                block.gpsimd(gv)

            def vv(v, j=j, mmv=mmv):
                v.wait_ge(pe.h, mmv)
                v.tensor_scalar(vtile[:], psB[:], bcol[:, 32 + j:33 + j], None, OP.add)
                v.drain()
                v.tensor_tensor(fkt[:], ktT[j][:], RNb[:], OP.mult)
                v.drain()
                v.scalar_tensor_tensor(scr[:], fkt[:], 1.0, vtile[:],
                                       OP.mult, OP.mult,
                                       accum_out=acol[:, j:j + 1])
                mark(v, vm)
            block.vector(vv)
        a_done = vm.v

        def qa(q):
            q.wait_ge(vm.h, a_done)
            q.dma_start(out=apart_o[:], in_=acol[:]).then_inc(st.h, 16)
            st.inc(16)
        block.sync(qa)

        def gq0(g, pv=pe.v):
            g.wait_ge(pe.h, pv)
            wload(g, 0, 0)
            wload(g, 4, 1)
        block.gpsimd(gq0)

        def qcsq(q, av=a_done):
            q.wait_ge(vm.h, av)
            csload(q, 0, 0)
            csload(q, 1, 1)
        block.sync(qcsq)

        for i in range(4):
            def tq(t, i=i, av=a_done):
                t.wait_ge(vm.h, av if i == 0 else vm.v)
                mm_echunk(t, i, psA, 0)
                mm_echunk(t, 4 + i, psB, 1)
            block.tensor(tq)
            pair_mm = pe.v
            if i < 3:
                def gq(g, i=i, pm=pair_mm):
                    g.wait_ge(pe.h, pm)
                    wload(g, i + 1, 0)
                    wload(g, 4 + i + 1, 1)
                block.gpsimd(gq)

            def vq(v, i=i, pm=pair_mm):
                v.wait_ge(pe.h, pm)
                v.wait_ge(*cs_t[i])
                cb, sb_ = cosb[i % 2], sinb[i % 2]
                v.scalar_tensor_tensor(p1[:], psA[:], bcol[:, i:i + 1], cb[:],
                                       OP.add, OP.mult)
                v.scalar_tensor_tensor(p2[:], psB[:], bcol[:, 4 + i:5 + i], sb_[:],
                                       OP.add, OP.mult)
                v.drain()
                v.tensor_tensor(qtile[:], p1[:], p2[:], OP.subtract)
                mark(v, vm)
            block.vector(vq)

            def qq(q, i=i, ev=vm.v):
                q.wait_ge(vm.h, ev)
                s_qst.issue(q, 0, [(qT_o[128 * i:128 * (i + 1), :], qtile[:], {})])
            block.sync(qq)
            qst_t = s_qst.totals()[0]

            def vq2(v, i=i, qt=qst_t):
                v.wait_ge(*qt)
                cb, sb_ = cosb[i % 2], sinb[i % 2]
                v.scalar_tensor_tensor(p1[:], psB[:], bcol[:, 4 + i:5 + i], cb[:],
                                       OP.add, OP.mult)
                v.scalar_tensor_tensor(p2[:], psA[:], bcol[:, i:i + 1], sb_[:],
                                       OP.add, OP.mult)
                v.drain()
                v.tensor_tensor(qtile[:], p1[:], p2[:], OP.add)
                mark(v, vm)
            block.vector(vq2)

            def qq2(q, i=i, ev=vm.v):
                q.wait_ge(vm.h, ev)
                s_qst.issue(q, 0, [(qT_o[128 * (4 + i):128 * (5 + i), :], qtile[:], {})])
            block.sync(qq2)
            qst_t2 = s_qst.totals()[0]

            def vq3(v, qt=qst_t2):
                v.wait_ge(*qt)
                mark(v, vm)
            block.vector(vq3)

            if 0 < i < 3:
                def qcsq2(q, i=i, vv=vm.v):
                    q.wait_ge(vm.h, vv)
                    csload(q, i + 1, (i + 1) % 2)
                block.sync(qcsq2)

        def gqp(g, pv=pe.v):
            g.wait_ge(pe.h, pv)
            wload(g, 8, 0)
            wload(g, 9, 1)
        block.gpsimd(gqp)

        for j in range(8):
            def tqp(t, j=j):
                t.wait_ge(vm.h, vm.v)
                mm_echunk(t, 8 + j, psA if j % 2 == 0 else psB, j % 2)
            block.tensor(tqp)
            mmv = pe.v
            if j < 6:
                def gqp2(g, j=j, mmv=mmv):
                    g.wait_ge(pe.h, mmv)
                    wload(g, 10 + j, j % 2)
                block.gpsimd(gqp2)

            def vqp(v, j=j, mmv=mmv):
                v.wait_ge(pe.h, mmv)
                v.tensor_scalar(qtile[:], (psA if j % 2 == 0 else psB)[:],
                                bcol[:, 8 + j:9 + j], None, OP.add)
                mark(v, vm)
            block.vector(vqp)

            def qqp(q, j=j, ev=vm.v):
                q.wait_ge(vm.h, ev)
                s_qst.issue(q, 0, [(qT_o[128 * (8 + j):128 * (9 + j), :], qtile[:], {})])
            block.sync(qqp)
            qst_t = s_qst.totals()[0]

            def vqp2(v, qt=qst_t):
                v.wait_ge(*qt)
                mark(v, vm)
            block.vector(vqp2)

        ctx2.close()
        qkv_end = vm.v

        if cut == "qkv":
            def qfin1(q):
                q.wait_ge(vm.h, qkv_end)
                q.wait_ge(st.h, st.v)
                wait_all(q, s_qst.totals())
            block.sync(qfin1)
            pr.ctx.pop_all()
            return pr

        ctx3 = ExitStack()
        qfT = [pr.sb(f"qfT{i}", [128, T], BF16, ctx3) for i in range(4)]
        wqb = pr.sb("wqb", [128, NDT * 512], BF16, ctx3)
        keysb = pr.sb("keysb", [128, 8 * 128], BF16, ctx3)
        iob = pr.sb("iob", [128, 64], F32, ctx3)
        s1 = pr.sb("s1", [128, 8], F32, ctx3)
        s2 = pr.sb("s2", [128, 8], F32, ctx3)
        i1 = pr.sb("i1", [128, 8], U32, ctx3)
        i2 = pr.sb("i2", [128, 8], U32, ctx3)
        i1f = pr.sb("i1f", [128, 8], F32, ctx3)
        i2f = pr.sb("i2f", [128, 8], F32, ctx3)
        i1m = pr.sb("i1m", [128, 8], F32, ctx3)
        asc = pr.sb("asc", [128, 64], F32, ctx3)
        aidx = pr.sb("aidx", [128, 64], F32, ctx3)
        fs_t = pr.sb("fs_t", [128, 32], F32, ctx3)
        fsh = pr.sb("fsh", [128, 8], F32, ctx3)
        pk_t = pr.sb("pk_t", [128, 8], U32, ctx3)
        pkf = pr.sb("pkf", [128, 8], F32, ctx3)
        fi_f = pr.sb("fi_f", [128, 32], F32, ctx3)
        fi_u = pr.sb("fi_u", [128, 32], U32, ctx3)
        eqm = pr.sb("eqm", [128, 64], F32, ctx3)
        prodm = pr.sb("prodm", [128, 64], F32, ctx3)
        psF = pr.ps("psF", [128, 2048], F32, ctx3)
        psS = pr.ps("psS", [128, 2048], F32, ctx3)

        def gw(g, vv=qkv_end):
            g.wait_ge(vm.h, vv)
            s_wq.issue(g, 0, [
                (wqb[:].rearrange("p (dt c) -> p dt c", c=512),
                 wq_d.rearrange("(dt p) c -> p dt c", p=128), {}),
                (keysb[0:64, :].rearrange("p (a c) -> p a c", c=128),
                 keysT.rearrange("a b c -> b a c"), {}),
                (keysb[64:128, :].rearrange("p (a c) -> p a c", c=128),
                 keysT.rearrange("a b c -> b a c"), {}),
                (iob[:], iota64[:], {}),
            ])
        block.gpsimd(gw)
        wq_t = s_wq.totals()[0]

        for cc in range(4):
            def tqf(t, cc=cc, wt=wq_t, vv=qkv_end):
                t.wait_ge(*wt)
                t.wait_ge(vm.h, vv)
                if cc > 0:
                    t.wait_ge(sm.h, sm.v)
                for dt in range(NDT):
                    for tc in range(NTC):
                        _mm = t.matmul(psF[:, 512 * tc:512 * (tc + 1)],
                                 wqb[:, dt * 512 + cc * 128:dt * 512 + cc * 128 + 128],
                                 xnT[dt][:, 512 * tc:512 * (tc + 1)],
                                 start=(dt == 0), stop=(dt == NDT - 1))
                tmark(t, pe, _mm)
            block.tensor(tqf)

            def sqf(s, cc=cc, pv=pe.v):
                s.wait_ge(pe.h, pv)
                s.copy(qfT[cc][:], psF[:])
                mark(s, sm)
            block.scalar(sqf)
        qf_done = sm.v

        if cut == "qf":
            def qfinq(q):
                q.wait_ge(sm.h, qf_done)
                q.wait_ge(st.h, st.v)
            block.sync(qfinq)
            pr.ctx.pop_all()
            return pr

        for tb in range(NDT):
            def tsim(t, tb=tb, qd=qf_done, wt=wq_t):
                t.wait_ge(sm.h, qd)
                t.wait_ge(*wt)
                if tb > 0:
                    t.wait_ge(vm.h, vm.v)
                for h in range(4):
                    for w in range(2):
                        b_ = 2 * h + w
                        psX = psF if b_ < 4 else psS
                        _mm = t.matmul(psX[:, 512 * (b_ % 4):512 * (b_ % 4) + 128],
                                 qfT[h][64 * w:64 * (w + 1), 128 * tb:128 * (tb + 1)],
                                 keysb[64 * w:64 * (w + 1),
                                       (2 * h + w) * 128:(2 * h + w + 1) * 128],
                                 start=True, stop=True)
                tmark(t, pe, _mm)
            block.tensor(tsim)

            def vtk(v, tb=tb, pv=pe.v):
                v.wait_ge(pe.h, pv)
                for h in range(4):
                    b1, b2 = 2 * h, 2 * h + 1
                    sl1 = (psF if b1 < 4 else psS)[:, 512 * (b1 % 4):512 * (b1 % 4) + 128]
                    sl2 = (psF if b2 < 4 else psS)[:, 512 * (b2 % 4):512 * (b2 % 4) + 128]
                    v.max(s1[:], sl1)
                    v.max(s2[:], sl2)
                    v.drain()
                    v.max_index(i1[:], s1[:], sl1)
                    v.max_index(i2[:], s2[:], sl2)
                    v.drain()
                    v.tensor_copy(i1f[:], i1[:])
                    v.tensor_copy(i2f[:], i2[:])
                    v.drain()
                    v.tensor_scalar(i1m[:], i1f[:], 128.0, None, OP.mult)
                    v.drain()
                    for j in range(8):
                        v.tensor_scalar(asc[:, 8 * j:8 * j + 8], s2[:],
                                        s1[:, j:j + 1], None, OP.add)
                        v.tensor_scalar(aidx[:, 8 * j:8 * j + 8], i2f[:],
                                        i1m[:, j:j + 1], None, OP.add)
                    v.drain()
                    v.max(fsh[:], asc[:])
                    v.drain()
                    v.max_index(pk_t[:], fsh[:], asc[:])
                    v.tensor_copy(fs_t[:, 8 * h:8 * h + 8], fsh[:])
                    v.drain()
                    v.tensor_copy(pkf[:], pk_t[:])
                    v.drain()
                    for j in range(8):
                        v.tensor_scalar(eqm[:], iob[:], pkf[:, j:j + 1],
                                        None, OP.is_equal)
                        v.drain()
                        v.tensor_tensor(prodm[:], eqm[:], aidx[:], OP.mult)
                        v.drain()
                        v.tensor_reduce(fi_f[:, 8 * h + j:8 * h + j + 1], prodm[:],
                                        axis=AXX, op=OP.add)
                        v.drain()
                v.tensor_copy(fi_u[:], fi_f[:])
                mark(v, vm)
            block.vector(vtk)

            def qtk(q, tb=tb, ev=vm.v):
                q.wait_ge(vm.h, ev)
                s_tk.issue(q, 0, [(fi_o[tb], fi_u[:], {}),
                                  (fs_o[tb], fs_t[:], {})])
            block.sync(qtk)
            tk_t = s_tk.totals()[0]

            def vtk2(v, tt=tk_t):
                v.wait_ge(*tt)
                mark(v, vm)
            block.vector(vtk2)

        ctx3.close()

        def qfin(q):
            q.wait_ge(st.h, st.v)
            wait_all(q, s_qst.totals())
            wait_all(q, s_tk.totals())
        block.sync(qfin)
        pr.finish = [(pe.h, pe.v), (vm.h, vm.v), (sm.h, sm.v), (st.h, st.v)]
        pr.ctx.pop_all()

    return pr



def l0a_inmaps(prep):
    cos, sin = prep["cosT"], prep["sinT"]
    maps = []
    for c in range(NCORE):
        r = c % 2
        maps.append({
            "xT": prep["xT"][c],
            "winT": prep["winT"],
            "binp": prep["binp"],
            "cosT": np.ascontiguousarray(cos[:, r * T:(r + 1) * T]),
            "sinT": np.ascontiguousarray(sin[:, r * T:(r + 1) * T]),
            "wq": prep["wq"],
            "keysT": prep["keysT"],
            "ones1": prep["ones1"],
            "onescol": prep["onescol"],
            "iota64": prep["iota64"],
        })
    return maps





def build_l0b(pr=None, io=None, gate=None):
    if pr is None:
        pr = Prog("l0b")
    nc = pr.nc
    summed = io is not None and "ap1" not in io
    xT = get_io(pr, io, "xT", [D, T], F32)
    xnT_i = get_io(pr, io, "xnT_i", [D, T], BF16)
    qT_i = get_io(pr, io, "qT_i", [D, T], BF16)
    ap0 = get_io(pr, io, "ap0", [128, 16], F32)
    ap1 = None if summed else get_io(pr, io, "ap1", [128, 16], F32)
    fi_i = get_io(pr, io, "fi_i", [16, 128, 32], U32)
    fs_i = get_io(pr, io, "fs_i", [16, 128, 32], F32)
    ecat = get_io(pr, io, "ecat", [16384, 4096], F32)
    woutT = get_io(pr, io, "woutT", [D, D], F32)
    bout_d = get_io(pr, io, "bout_d", [128, 16], F32)
    ident_d = get_io(pr, io, "ident_d", [128, 128], BF16)
    ones1 = get_io(pr, io, "ones1", [1, 128], BF16)
    onescol = get_io(pr, io, "onescol", [128, 1], BF16)

    x1T_o = get_io(pr, io, "x1T_o", [D, T], F32, out=True)
    xn2T_o = get_io(pr, io, "xn2T_o", [D, T], BF16, out=True)
    x1s = io["x1s"] if (io is not None and "x1s" in io) else pr.dtmp("x1s", [D, T], F32)

    with pr.ctx:
        xnT = [pr.sb(f"xnT{i}", [128, T], BF16) for i in range(NDT)]
        fT = [pr.sb(f"fT{i}", [128, T], BF16) for i in range(NDT)]
        o1 = pr.sb("o1", [1, 128], BF16)
        ocol = pr.sb("ocol", [128, 1], BF16)
        ident = pr.sb("ident", [128, 128], BF16)
        bout = pr.sb("bout", [128, 16], F32)
        acol = pr.sb("acol", [128, 16], F32)
        aps = [pr.sb(f"aps{i}", [128, 16], F32) for i in range(2)]

        pe = pr.sem("pe")
        vm = pr.sem("vm")
        sm = pr.sem("sm")
        st = pr.sem("st")
        ms = {"pe": pe, "vm": vm, "sm": sm, "st": st}
        s_const = Stream(pr, "cst", 1)
        s_xn = Stream(pr, "sxn", 1)
        s_g = Stream(pr, "sg", 4)
        s_tk = Stream(pr, "stk", 2)
        s_w = Stream(pr, "sw", 2)
        s_q = Stream(pr, "sq", 2)
        s_xf = Stream(pr, "sxf", 2)

        if pr.block is None:
            pr.block = pr.ctx.enter_context(nc.Block())
        block = pr.block
        gate_all(block, gate)
        b = {"o1": o1, "ocol": ocol}

        def qc(q):
            dmas = [
                (o1[:], ones1[:], {}),
                (ocol[:], onescol[:], {}),
                (ident[:], ident_d[:], {}),
                (bout[:], bout_d[:], {}),
                (aps[0][:], ap0[:], {}),
            ]
            if not summed:
                dmas.append((aps[1][:], ap1[:], {}))
            s_const.issue(q, 0, dmas)
            for dt in range(NDT):
                s_xn.issue(q, 0, [(xnT[dt][:], xnT_i[128 * dt:128 * (dt + 1), :], {})])
        block.sync(qc)
        const_t = s_const.totals()[0]
        xn_t = s_xn.totals()[0]

        def va(v):
            v.wait_ge(*const_t)
            if summed:
                v.tensor_copy(acol[:], aps[0][:])
            else:
                v.tensor_tensor(acol[:], aps[0][:], aps[1][:], OP.add)
            mark(v, vm)
        block.vector(va)
        a_done = vm.v

        ctx1 = ExitStack()
        xnat = pr.sb("xnat", [128, D], BF16, ctx1)
        gbuf = [pr.sb(f"gbuf{i}", [128, 4096], BF16, ctx1) for i in range(4)]
        dscr = [pr.sb(f"dscr{i}", [128, D], BF16, ctx1) for i in range(2)]
        diag = [pr.sb(f"diag{i}", [128, 128], BF16, ctx1) for i in range(2)]
        idxb = pr.sb("idxb", [128, 32], U32, ctx1)
        fsb = pr.sb("fsb", [128, 32], F32, ctx1)
        dcol = pr.sb("dcol", [128, 32], F32, ctx1)
        sil = pr.sb("sil", [128, 32], F32, ctx1)
        hcol = pr.sb("hcol", [128, 32], F32, ctx1)
        fnat = pr.sb("fnat", [128, D], BF16, ctx1)
        psU = pr.ps("psU", [128, 2048], F32, ctx1)
        psT = pr.ps("psT", [128, 128], BF16, ctx1)

        vmF, smF, peF = vm.v, sm.v, pe.v
        pvN, svN, vvN = 64, 65, 96

        def dpos(k):
            return 1 if k == 0 else 3 * k - 1

        def hpos(j):
            return 3 * j + 3 if j <= 30 else 95

        def gpos(j):
            return 3 * j + 4 if j <= 30 else 96

        def qtkb(q):
            for tb in range(NDT):
                if tb >= 1:
                    q.wait_ge(sm.h, smF + svN * (tb - 1) + 49)
                    q.wait_ge(vm.h, vmF + vvN * (tb - 1) + 96)
                s_tk.issue(q, tb, [(idxb[:], fi_i[tb], {}),
                                   (fsb[:], fs_i[tb], {})])
        block.sync(qtkb)

        def gkb(g):
            for tb in range(NDT):
                tkt = (s_tk.sems[tb % 2].h, 32 * (tb // 2 + 1))
                for k in range(32):
                    gi = 32 * tb + k
                    if k == 0:
                        g.wait_ge(*tkt)
                    if gi >= 4:
                        tbp, kp = (gi - 4) // 32, (gi - 4) % 32
                        g.wait_ge(pe.h, peF + pvN * tbp + 16 + kp + 1)
                    g.indirect_dma_start(
                        out=gbuf[k % 4][:], out_offset=None,
                        in_=ecat[:, :],
                        in_offset=IOA(ap=idxb[:, k:k + 1], axis=0),
                    ).then_inc(s_g.sems[k % 4].h, 16)
                    s_g.sems[k % 4].inc(16)
        block.gpsimd(gkb)

        def txb(t, xv=xn_t, cv=const_t):
            t.wait_ge(*xv)
            t.wait_ge(*cv)
            for tb in range(NDT):
                peb, smb, vmb = peF + pvN * tb, smF + svN * tb, vmF + vvN * tb
                if tb >= 1:
                    t.wait_ge(vm.h, vmb - 4)
                    t.wait_ge(sm.h, smb)
                for dt in range(NDT):
                    if dt >= 1:
                        t.wait_ge(sm.h, smb + dt)
                    t.transpose(psT[:], xnT[dt][:, 128 * tb:128 * (tb + 1)],
                                ident[:]).then_inc(pe.h, 1)
                    pe.inc()
                for k in range(32):
                    t.wait_ge(vm.h, vmb + gpos(k))
                    t.wait_ge(s_g.sems[k % 4].h, 16 * (8 * tb + k // 4 + 1))
                    for tc in range(4):
                        _mm = t.matmul(psU[:, 512 * tc:512 * (tc + 1)],
                                       diag[k % 2][:],
                                       gbuf[k % 4][:, 2048 + 512 * tc:
                                                   2048 + 512 * (tc + 1)],
                                       start=(k == 0), stop=(k == 31))
                    tmark(t, pe, _mm)
                t.wait_ge(sm.h, smb + 49)
                for dt in range(NDT):
                    if dt >= 1:
                        t.wait_ge(sm.h, smb + 49 + dt)
                    t.transpose(psT[:], fnat[:, 128 * dt:128 * (dt + 1)],
                                ident[:]).then_inc(pe.h, 1)
                    pe.inc()
        block.tensor(txb)

        def sxb(s):
            for tb in range(NDT):
                peb, smb, vmb = peF + pvN * tb, smF + svN * tb, vmF + vvN * tb
                for dt in range(NDT):
                    s.wait_ge(pe.h, peb + dt + 1)
                    _c = s.copy(xnat[:, 128 * dt:128 * (dt + 1)], psT[:])
                    _c.then_inc(sm.h, 1)
                    sm.inc()
                for k in range(32):
                    s.wait_ge(vm.h, vmb + dpos(k))
                    _a = s.activation(sil[:, k:k + 1], dcol[:, k:k + 1],
                                      AF.Silu)
                    _a.then_inc(sm.h, 1)
                    sm.inc()
                s.wait_ge(pe.h, peb + 48)
                _c = s.copy(fnat[:], psU[:])
                _c.then_inc(sm.h, 1)
                sm.inc()
                for dt in range(NDT):
                    s.wait_ge(pe.h, peb + 48 + dt + 1)
                    _c = s.copy(fT[dt][:, 128 * tb:128 * (tb + 1)], psT[:])
                    _c.then_inc(sm.h, 1)
                    sm.inc()
        block.scalar(sxb)

        def vdb(v):
            def dot(v, tb, k, smb, vmb):
                v.wait_ge(s_g.sems[k % 4].h, 16 * (8 * tb + k // 4 + 1))
                if k == 0:
                    v.wait_ge(sm.h, smb + 16)
                    if tb >= 1:
                        v.wait_ge(sm.h, smb - svN + 48)
                if k >= 2:
                    v.wait_ge(vm.h, vmb + dpos(k - 2))
                elif tb >= 1:
                    v.wait_ge(vm.h, vmb - 4)
                _d = v.scalar_tensor_tensor(dscr[k % 2][:],
                                            gbuf[k % 4][:, 0:2048],
                                            1.0, xnat[:], OP.mult, OP.mult,
                                            accum_out=dcol[:, k:k + 1])
                _d.then_inc(vm.h, 1)
                vm.inc()

            def hd(v, j, peb, smb, vmb):
                v.wait_ge(sm.h, smb + 16 + j + 1)
                _h = v.scalar_tensor_tensor(hcol[:, j:j + 1], fsb[:, j:j + 1],
                                            0.0, sil[:, j:j + 1],
                                            OP.max, OP.mult)
                _h.then_inc(vm.h, 1)
                hv = vm.inc()
                if j >= 2:
                    v.wait_ge(pe.h, peb + 16 + j - 1)
                v.wait_ge(vm.h, hv)
                _g = v.tensor_scalar(diag[j % 2][:], ident[:],
                                     hcol[:, j:j + 1], None, OP.mult)
                _g.then_inc(vm.h, 1)
                vm.inc()

            for tb in range(NDT):
                peb, smb, vmb = peF + pvN * tb, smF + svN * tb, vmF + vvN * tb
                for k in range(32):
                    dot(v, tb, k, smb, vmb)
                    if k >= 1:
                        hd(v, k - 1, peb, smb, vmb)
                hd(v, 31, peb, smb, vmb)
        block.vector(vdb)

        ctx1.close()
        ffn_done = vm.v
        ffn_sm = sm.v

        ctx2 = ExitStack()
        ytile = [pr.sb(f"yt{i}", [128, 1024], BF16, ctx2) for i in range(NDT)]
        qbuf = [pr.sb(f"qbuf{i}", [128, 1024], BF16, ctx2) for i in range(2)]
        wbuf = [pr.sb(f"wb{i}", [128, NDT * 128], BF16, ctx2) for i in range(2)]
        xf = [pr.sb(f"xf{i}", [128, 1024], F32, ctx2) for i in range(2)]
        t1 = pr.sb("t1", [128, 1024], F32, ctx2)
        t2 = pr.sb("t2", [128, 1024], F32, ctx2)
        psW = pr.ps("psW", [128, 1024], F32, ctx2)

        w_t = {}

        def wload(g, e, buf):
            w_t[e] = s_w.issue(g, buf, [(
                wbuf[buf][:].rearrange("p (dt c) -> p dt c", c=128),
                woutT[:, 128 * e:128 * (e + 1)].rearrange("(dt p) c -> p dt c", p=128),
                {})])

        for half in range(2):
            hs = slice(1024 * half, 1024 * (half + 1))
            for dt in range(NDT):
                def qy(q, dt=dt, half=half, fs_=ffn_sm, vv=vm.v):
                    if half == 0 and dt < 2:
                        q.wait_ge(sm.h, fs_)
                    else:
                        q.wait_ge(vm.h, vv)
                    s_q.issue(q, dt, [(qbuf[dt % 2][:], qT_i[128 * dt:128 * (dt + 1), hs], {})])
                block.sync(qy)
                q_t = (s_q.sems[dt % 2].h, s_q.sems[dt % 2].v)

                def vy(v, dt=dt, qt=q_t, av=a_done):
                    v.wait_ge(*qt)
                    v.wait_ge(vm.h, av)
                    v.tensor_scalar(ytile[dt][:], qbuf[dt % 2][:], acol[:, dt:dt + 1],
                                    None, OP.mult)
                    mark(v, vm)
                block.vector(vy)
            y_done = vm.v

            def g0(g, half=half, yv=y_done):
                g.wait_ge(vm.h, yv)
                wload(g, 0, 0)
                wload(g, 1, 1)
            block.gpsimd(g0)

            for d2 in range(NDT):
                def tw(t, d2=d2, yv=y_done):
                    t.wait_ge(vm.h, yv)
                    t.wait_ge(*w_t[d2])
                    if d2 > 0:
                        t.wait_ge(vm.h, vm.v)
                    for d1 in range(NDT):
                        for tc in range(2):
                            _mm = t.matmul(psW[:, 512 * tc:512 * (tc + 1)],
                                     wbuf[d2 % 2][:, d1 * 128:d1 * 128 + 128],
                                     ytile[d1][:, 512 * tc:512 * (tc + 1)],
                                     start=(d1 == 0), stop=(d1 == NDT - 1))
                    tmark(t, pe, _mm)
                block.tensor(tw)
                mmv = pe.v
                if d2 < NDT - 2:
                    def g2(g, d2=d2, mmv=mmv):
                        g.wait_ge(pe.h, mmv)
                        wload(g, d2 + 2, d2 % 2)
                    block.gpsimd(g2)

                def qx(q, d2=d2, half=half, vv=vm.v):
                    q.wait_ge(vm.h, vv)
                    s_xf.issue(q, d2, [(xf[d2 % 2][:],
                                        xT[128 * d2:128 * (d2 + 1), hs], {})])
                block.sync(qx)
                xf_t = (s_xf.sems[d2 % 2].h, s_xf.sems[d2 % 2].v)

                def vx(v, d2=d2, half=half, mmv=mmv, xt_=xf_t, fs_=ffn_sm):
                    v.wait_ge(pe.h, mmv)
                    v.wait_ge(*xt_)
                    if half == 0:
                        v.wait_ge(sm.h, fs_)
                    v.tensor_scalar(t1[:], psW[:], bout[:, d2:d2 + 1], None, OP.add)
                    v.drain()
                    v.tensor_tensor(t2[:], t1[:], xnT[d2][:, hs], OP.add)
                    v.drain()
                    v.tensor_tensor(t1[:], t2[:], fT[d2][:, hs], OP.add)
                    v.drain()
                    v.tensor_tensor(t2[:], t1[:], xf[d2 % 2][:], OP.add)
                    mark(v, vm)
                block.vector(vx)

                def qo(q, d2=d2, half=half, vv=vm.v):
                    q.wait_ge(vm.h, vv)
                    q.dma_start(out=x1T_o[128 * d2:128 * (d2 + 1), hs],
                                in_=t2[:]).then_inc(st.h, 16)
                    st.inc(16)
                    q.dma_start(out=x1s[128 * d2:128 * (d2 + 1), hs],
                                in_=t2[:]).then_inc(st.h, 16)
                    st.inc(16)
                block.sync(qo)
                stv = st.v

                def vnx(v, sv=stv):
                    v.wait_ge(st.h, sv)
                    mark(v, vm)
                block.vector(vnx)

        ctx2.close()

        emit_ln(pr, b, x1s, xnT, ms, eps=1e-5, tag="ln2")
        xn2_done = vm.v

        def qxn2(q):
            q.wait_ge(vm.h, xn2_done)
            for dt in range(NDT):
                q.dma_start(out=xn2T_o[128 * dt:128 * (dt + 1), :],
                            in_=xnT[dt][:]).then_inc(st.h, 16)
                st.inc(16)
        block.sync(qxn2)

        def qfin(q):
            q.wait_ge(st.h, st.v)
        block.sync(qfin)
        pr.finish = [(pe.h, pe.v), (vm.h, vm.v), (sm.h, sm.v), (st.h, st.v)]
        pr.ctx.pop_all()

    return pr


def l0b_inmaps(prep, l0a_res):
    maps = []
    for c in range(NCORE):
        part = c + 1 if c % 2 == 0 else c - 1
        maps.append({
            "xT": prep["xT"][c],
            "xnT_i": l0a_res[c]["xnT_o"],
            "qT_i": l0a_res[c]["qT_o"],
            "ap0": l0a_res[c]["apart_o"],
            "ap1": l0a_res[part]["apart_o"],
            "fi_i": l0a_res[c]["fi_o"],
            "fs_i": l0a_res[c]["fs_o"],
            "ecat": prep["e_cat"],
            "woutT": prep["woutT"],
            "bout_d": prep["bout"],
            "ident_d": prep["ident"],
            "ones1": prep["ones1"],
            "onescol": prep["onescol"],
        })
    return maps





def build_l1a(pr=None, io=None, gate=None):
    if pr is None:
        pr = Prog("l1a")
    nc = pr.nc
    ag = io is not None and "xn2f_ag" in io
    xn2f_d = io["xn2f_ag"] if ag else get_io(pr, io, "xn2f", [D, S], BF16)
    kvaT_d = get_io(pr, io, "kvaT", [D, 576], F32)
    lnw_d = get_io(pr, io, "lnw_d", [128, 4], F32)
    kvbn_d = get_io(pr, io, "kvbn", [512, 2048], F32)
    kvbv_d = get_io(pr, io, "kvbv", [512, 2048], F32)
    qpT_d = get_io(pr, io, "qpT", [D, 1536], F32)
    ones1 = get_io(pr, io, "ones1", [1, 128], BF16)
    onescol = get_io(pr, io, "onescol", [128, 1], BF16)

    kT_o = get_io(pr, io, "kT_o", [2048, S], BF16, out=True)
    krotT_o = get_io(pr, io, "krotT_o", [64, S], BF16, out=True)
    v_o = get_io(pr, io, "v_o", [S, 2048], BF16, out=True)
    q2T_o = get_io(pr, io, "q2T_o", [1536, S], BF16, out=True)

    with pr.ctx:
        o1 = pr.sb("o1", [1, 128], BF16)
        ocol = pr.sb("ocol", [128, 1], BF16)
        lnw = pr.sb("lnw", [128, 4], F32)
        cn = [pr.sb(f"cn{i}", [128, S], BF16) for i in range(4)]
        RNb = pr.sb("RNb", [128, S], BF16)
        ssq = pr.sb("ssq", [1, S], F32)
        rn16 = pr.sb("rn16", [1, S], BF16)

        pe = pr.sem("pe")
        vm = pr.sem("vm")
        sm = pr.sem("sm")
        st = pr.sem("st")
        s_const = Stream(pr, "cst", 1)
        s_xn = Stream(pr, "sxn", 1)
        s_w = Stream(pr, "sw", 2)

        if pr.block is None:
            pr.block = pr.ctx.enter_context(nc.Block())
        block = pr.block
        gate_all(block, gate)

        def qc(q):
            s_const.issue(q, 0, [
                (o1[:], ones1[:], {}),
                (ocol[:], onescol[:], {}),
                (lnw[:], lnw_d[:], {}),
            ])
        block.sync(qc)
        const_t = s_const.totals()[0]

        ctxA = ExitStack()
        xnf = [pr.sb(f"xnf{i}", [128, S], BF16, ctxA) for i in range(NDT)]
        wbuf = [pr.sb(f"wb{i}", [128, NDT * 128], BF16, ctxA) for i in range(2)]
        evb = pr.sb("evb", [128, 2048], BF16, ctxA)
        csq = evb
        psA = pr.ps("psA", [128, 2048], F32, ctxA)
        psR = pr.ps("psR", [128, 512], F32, ctxA)

        def qxn(q):
            for dt in range(NDT):
                if ag:
                    s_xn.issue(q, 0, [
                        (xnf[dt][:, 0:T], xn2f_d[128 * dt:128 * (dt + 1), :], {}),
                        (xnf[dt][:, T:S],
                         xn2f_d[D + 128 * dt:D + 128 * (dt + 1), :], {}),
                    ])
                else:
                    s_xn.issue(q, 0, [(xnf[dt][:],
                                       xn2f_d[128 * dt:128 * (dt + 1), :], {})])
        block.sync(qxn)
        xn_t = s_xn.totals()[0]

        w_t = {}

        def wload(g, dram, e, buf, ncol=128):
            w_t[(dram is qpT_d, e)] = s_w.issue(g, buf, [(
                wbuf[buf][:, 0:NDT * ncol].rearrange("p (dt c) -> p dt c", c=ncol),
                dram[:, ncol * e:ncol * (e + 1)].rearrange("(dt p) c -> p dt c", p=128),
                {})])

        def gq0(g):
            wload(g, qpT_d, 0, 0)
            wload(g, qpT_d, 1, 1)
        block.gpsimd(gq0)

        for mc in range(12):
            def tq(t, mc=mc, xt=xn_t, ct=const_t):
                t.wait_ge(*xt)
                t.wait_ge(*ct)
                t.wait_ge(*w_t[(True, mc)])
                if mc > 0:
                    t.wait_ge(vm.h, vm.v)
                for th in range(2):
                    for dt in range(NDT):
                        for tc in range(4):
                            _mm = t.matmul(psA[:, 512 * tc:512 * (tc + 1)],
                                     wbuf[mc % 2][:, dt * 128:dt * 128 + 128],
                                     xnf[dt][:, 2048 * th + 512 * tc:
                                              2048 * th + 512 * (tc + 1)],
                                     start=(dt == 0), stop=(dt == NDT - 1))
                    _mm.then_inc(pe.h, 1)
                    pe.inc()
                    t.wait_ge(vm.h, vm.v + th + 1)
            block.tensor(tq)
            if mc < 10:
                def gq(g, mc=mc, pv=pe.v):
                    g.wait_ge(pe.h, pv)
                    wload(g, qpT_d, mc + 2, mc % 2)
                block.gpsimd(gq)

            for th in range(2):
                def vq(v, mc=mc, th=th, pv=pe.v - 1 + th):
                    v.wait_ge(pe.h, pv)
                    v.tensor_scalar(evb[:], psA[:], QSCALE, None, OP.mult)
                    mark(v, vm)
                block.vector(vq)

                def qq(q, mc=mc, th=th, vv=vm.v):
                    q.wait_ge(vm.h, vv)
                    q.dma_start(out=q2T_o[128 * mc:128 * (mc + 1),
                                          2048 * th:2048 * (th + 1)],
                                in_=evb[:]).then_inc(st.h, 16)
                    st.inc(16)
                block.sync(qq)
                stv = st.v

                def vq2(v, sv=stv):
                    v.wait_ge(st.h, sv)
                    mark(v, vm)
                block.vector(vq2)
        q_done = vm.v

        def gc0(g, pv=pe.v):
            g.wait_ge(pe.h, pv)
            wload(g, kvaT_d, 0, 0)
            wload(g, kvaT_d, 1, 1)
        block.gpsimd(gc0)

        def vz(v, qd=q_done):
            v.wait_ge(vm.h, qd)
            v.memset(ssq[:], 0.0)
            mark(v, vm)
        block.vector(vz)

        for cc in range(5):
            ncol = 128 if cc < 4 else 64
            if cc >= 2 and cc < 4:
                pass
            for th in range(2):
                def tc_(t, cc=cc, th=th, ncol=ncol, qd=q_done):
                    t.wait_ge(vm.h, vm.v)
                    t.wait_ge(*w_t[(False, cc)])
                    for dt in range(NDT):
                        for tc in range(4):
                            _mm = t.matmul(psA[0:ncol, 512 * tc:512 * (tc + 1)],
                                     wbuf[cc % 2][:, dt * 128:dt * 128 + ncol],
                                     xnf[dt][:, 2048 * th + 512 * tc:
                                              2048 * th + 512 * (tc + 1)],
                                     start=(dt == 0), stop=(dt == NDT - 1))
                    tmark(t, pe, _mm)
                block.tensor(tc_)
                mmv = pe.v
                if th == 1 and cc < 3:
                    def gc(g, cc=cc, mmv=mmv):
                        g.wait_ge(pe.h, mmv)
                        if cc + 2 < 4:
                            wload(g, kvaT_d, cc + 2, cc % 2)
                        else:
                            w_t[(False, 4)] = s_w.issue(g, cc % 2, [(
                                wbuf[cc % 2][:].rearrange(
                                    "p (dt c) -> p dt c", c=128)[:, :, 0:64],
                                kvaT_d[:, 512:576].rearrange(
                                    "(dt p) c -> p dt c", p=128), {})])
                    block.gpsimd(gc)

                if cc < 4:
                    def sc(s, cc=cc, th=th, mmv=mmv):
                        s.wait_ge(pe.h, mmv)
                        s.copy(cn[cc][:, 2048 * th:2048 * (th + 1)], psA[:])
                        s.drain()
                        s.activation(csq[:], psA[:], AF.Square)
                        mark(s, sm)
                    block.scalar(sc)

                    def tsq(t, sv=sm.v):
                        t.wait_ge(sm.h, sv)
                        for tc in range(4):
                            _mm = t.matmul(psR[0:1, 0:512], ocol[:],
                                     csq[:, 512 * tc:512 * (tc + 1)],
                                     start=True, stop=True)
                            _mm.then_inc(pe.h, 1)
                            pe.inc()
                            t.wait_ge(vm.h, vm.v + tc + 1)
                    block.tensor(tsq)

                    for tc in range(4):
                        def vsq(v, th=th, tc=tc, pv=pe.v - 3 + tc):
                            v.wait_ge(pe.h, pv)
                            v.tensor_tensor(ssq[0:1, 2048 * th + 512 * tc:
                                                2048 * th + 512 * (tc + 1)],
                                            ssq[0:1, 2048 * th + 512 * tc:
                                                2048 * th + 512 * (tc + 1)],
                                            psR[0:1, 0:512], OP.add)
                            mark(v, vm)
                        block.vector(vsq)

                    def vcn(v, sv=sm.v):
                        v.wait_ge(sm.h, sv)
                        mark(v, vm)
                    block.vector(vcn)
                else:
                    def vrot(v, th=th, mmv=mmv):
                        v.wait_ge(pe.h, mmv)
                        v.tensor_copy(evb[0:64, 0:2048], psA[0:64, :])
                        mark(v, vm)
                    block.vector(vrot)

                    def qrot(q, th=th, vv=vm.v):
                        q.wait_ge(vm.h, vv)
                        q.dma_start(out=krotT_o[:, 2048 * th:2048 * (th + 1)],
                                    in_=evb[0:64, 0:2048]).then_inc(st.h, 16)
                        st.inc(16)
                    block.sync(qrot)
                    stv = st.v

                    def vrot2(v, sv=stv):
                        v.wait_ge(st.h, sv)
                        mark(v, vm)
                    block.vector(vrot2)
        c_done = vm.v

        def vrr(v, cd=c_done):
            v.wait_ge(vm.h, cd)
            v.tensor_scalar(ssq[:], ssq[:], 1.0 / KVR_, 1e-6, OP.mult, OP.add)
            mark(v, vm)
        block.vector(vrr)

        def srr(s, vv=vm.v):
            s.wait_ge(vm.h, vv)
            s.activation(ssq[:], ssq[:], AF.Ln)
            s.drain()
            s.activation(ssq[:], ssq[:], AF.Exp, scale=-0.5)
            mark(s, sm)
        block.scalar(srr)

        def vrr2(v, sv=sm.v):
            v.wait_ge(sm.h, sv)
            v.tensor_copy(rn16[:], ssq[:])
            mark(v, vm)
        block.vector(vrr2)

        def trb(t, vv=vm.v):
            t.wait_ge(vm.h, vv)
            for tc in range(8):
                _mm = t.matmul(psA[:, 512 * (tc % 4):512 * (tc % 4 + 1)], o1[:],
                         rn16[:, 512 * tc:512 * (tc + 1)], start=True, stop=True)
                _mm.then_inc(pe.h, 1)
                pe.inc()
                t.wait_ge(sm.h, sm.v + tc + 1)
        block.tensor(trb)

        for tc in range(8):
            def srb(s, tc=tc, pv=pe.v - 7 + tc):
                s.wait_ge(pe.h, pv)
                s.copy(RNb[:, 512 * tc:512 * (tc + 1)],
                       psA[:, 512 * (tc % 4):512 * (tc % 4 + 1)])
                s.drain()
                s.nop().then_inc(sm.h, 1)
                sm.inc()
            block.scalar(srb)
        rb_done = sm.v

        for cc in range(4):
            def vcs(v, cc=cc, rb=rb_done):
                v.wait_ge(sm.h, rb)
                v.tensor_tensor(cn[cc][:], cn[cc][:], RNb[:], OP.mult)
                v.drain()
                v.tensor_scalar(cn[cc][:], cn[cc][:], lnw[:, cc:cc + 1], None,
                                OP.mult)
                mark(v, vm)
            block.vector(vcs)
        cn_done = vm.v

        ctxA.close()

        ctxB = ExitStack()
        kwb = pr.sb("kwb", [128, 4 * 128], BF16, ctxB)
        vwb = pr.sb("vwb", [128, 4 * 2048], BF16, ctxB)
        ev2 = pr.sb("ev2", [128, 2048], BF16, ctxB)
        psK = pr.ps("psK", [128, 2048], F32, ctxB)
        s_kw = Stream(pr, "skw", 2)

        kw_t = {}

        def kwload(g, e, buf):
            kw_t[e] = s_kw.issue(g, buf, [(
                kwb[:, 0:4 * 128].rearrange("p (rt c) -> p rt c", c=128),
                kvbn_d[:, 128 * e:128 * (e + 1)].rearrange("(rt p) c -> p rt c", p=128),
                {})])

        def gk0(g, cd=cn_done):
            g.wait_ge(vm.h, cd)
            kwload(g, 0, 0)
            g.dma_start(out=vwb[:].rearrange("p (rt c) -> p rt c", c=2048),
                        in_=kvbv_d.rearrange("(rt p) c -> p rt c", p=128),
                        ).then_inc(s_kw.sems[1].h, 16)
            s_kw.sems[1].inc(16)
        block.gpsimd(gk0)
        vw_t = (s_kw.sems[1].h, s_kw.sems[1].v)

        for mc in range(16):
            for th in range(2):
                def tk(t, mc=mc, th=th, cd=cn_done):
                    t.wait_ge(vm.h, vm.v if (mc or th) else cd)
                    t.wait_ge(*kw_t[mc])
                    for rt in range(4):
                        for tc in range(4):
                            _mm = t.matmul(psK[:, 512 * tc:512 * (tc + 1)],
                                     kwb[:, rt * 128:rt * 128 + 128],
                                     cn[rt][:, 2048 * th + 512 * tc:
                                             2048 * th + 512 * (tc + 1)],
                                     start=(rt == 0), stop=(rt == 3))
                    tmark(t, pe, _mm)
                block.tensor(tk)
                mmv = pe.v

                def vk(v, mmv=mmv):
                    v.wait_ge(pe.h, mmv)
                    v.tensor_copy(ev2[:], psK[:])
                    mark(v, vm)
                block.vector(vk)

                def qk(q, mc=mc, th=th, vv=vm.v):
                    q.wait_ge(vm.h, vv)
                    q.dma_start(out=kT_o[128 * mc:128 * (mc + 1),
                                         2048 * th:2048 * (th + 1)],
                                in_=ev2[:]).then_inc(st.h, 16)
                    st.inc(16)
                block.sync(qk)
                stv = st.v

                def vk2(v, sv=stv):
                    v.wait_ge(st.h, sv)
                    mark(v, vm)
                block.vector(vk2)
            if mc < 15:
                def gkn(g, mc=mc, pv=pe.v):
                    g.wait_ge(pe.h, pv)
                    kwload(g, mc + 1, 0)
                block.gpsimd(gkn)

        for kc in range(32):
            def tv(t, kc=kc, vt=vw_t):
                t.wait_ge(*vt)
                t.wait_ge(vm.h, vm.v)
                for rt in range(4):
                    for tc in range(4):
                        _mm = t.matmul(psK[:, 512 * tc:512 * (tc + 1)],
                                 cn[rt][:, 128 * kc:128 * (kc + 1)],
                                 vwb[:, rt * 2048 + 512 * tc:
                                     rt * 2048 + 512 * (tc + 1)],
                                 start=(rt == 0), stop=(rt == 3))
                tmark(t, pe, _mm)
            block.tensor(tv)
            mmv = pe.v

            def vv2(v, mmv=mmv):
                v.wait_ge(pe.h, mmv)
                v.tensor_copy(ev2[:], psK[:])
                mark(v, vm)
            block.vector(vv2)

            def qv(q, kc=kc, vv=vm.v):
                q.wait_ge(vm.h, vv)
                q.dma_start(out=v_o[128 * kc:128 * (kc + 1), :],
                            in_=ev2[:]).then_inc(st.h, 16)
                st.inc(16)
            block.sync(qv)
            stv = st.v

            def vv3(v, sv=stv):
                v.wait_ge(st.h, sv)
                mark(v, vm)
            block.vector(vv3)

        ctxB.close()

        def qfin(q):
            q.wait_ge(st.h, st.v)
        block.sync(qfin)
        pr.finish = [(pe.h, pe.v), (vm.h, vm.v), (sm.h, sm.v), (st.h, st.v)]
        pr.ctx.pop_all()

    return pr


KVR_ = 512.0



def build_l1b(pr=None, io=None, gate=None):
    if pr is None:
        pr = Prog("l1b")
    nc = pr.nc
    kT_d = get_io(pr, io, "kT", [1024, S], BF16)
    krotT_d = get_io(pr, io, "krotT", [64, S], BF16)
    v_d = get_io(pr, io, "v", [S, 1024], BF16)
    qT_d = get_io(pr, io, "qT", [1536, S], BF16)
    opT_d = get_io(pr, io, "opT", [1024, D], F32)
    masks_d = get_io(pr, io, "masks", [4, 128, 512], F32)
    ones1 = get_io(pr, io, "ones1", [1, 128], BF16)
    onescol = get_io(pr, io, "onescol", [128, 1], BF16)

    op_o = get_io(pr, io, "op_o", [S, D], BF16, out=True)

    with pr.ctx:
        o1 = pr.sb("o1", [1, 128], BF16)
        ocol = pr.sb("ocol", [128, 1], BF16)
        maskb = pr.sb("maskb", [128, 4 * 512], F32)
        ob = pr.sb("ob", [128, 8 * 2048], BF16)
        krot = pr.sb("krot", [64, S], BF16)
        pvt = [pr.sb(f"pvt{i}", [128, 2048], BF16) for i in range(8)]

        pe = pr.sem("pe")
        vm = pr.sem("vm")
        sm = pr.sem("sm")
        st = pr.sem("st")
        s_const = Stream(pr, "cst", 1)
        s_kh = Stream(pr, "skh", 2)
        s_vh = Stream(pr, "svh", 2)
        s_qh = Stream(pr, "sqh", 2)

        if pr.block is None:
            pr.block = pr.ctx.enter_context(nc.Block())
        block = pr.block
        gate_all(block, gate)

        def qc(q):
            s_const.issue(q, 0, [
                (o1[:], ones1[:], {}),
                (ocol[:], onescol[:], {}),
                (maskb[:].rearrange("p (a c) -> p a c", c=512),
                 masks_d.rearrange("a b c -> b a c"), {}),
                (krot[:], krotT_d[:], {}),
            ])
        block.sync(qc)
        const_t = s_const.totals()[0]

        s_ob = Stream(pr, "sob", 1)

        def gob(g):
            s_ob.issue(g, 0, [(ob[:].rearrange("p (a c) -> p a c", c=2048),
                               opT_d.rearrange("(a p) c -> p a c", p=128), {})])
        block.gpsimd(gob)
        ob_t = s_ob.totals()[0]

        ctx1 = ExitStack()
        kh = [pr.sb(f"kh{i}", [128, S], BF16, ctx1) for i in range(2)]
        vh = [pr.sb(f"vh{i}", [128, 32 * 128], BF16, ctx1) for i in range(2)]
        qhn = [pr.sb(f"qhn{i}", [128, 2048], BF16, ctx1) for i in range(2)]
        qhr = [pr.sb(f"qhr{i}", [64, 2048], BF16, ctx1) for i in range(2)]
        pbuf = [pr.sb(f"pbuf{i}", [128, 512], BF16, ctx1) for i in range(2)]
        zrow = pr.sb("zrow", [1, 512], F32, ctx1)
        rz16 = pr.sb("rz16", [1, 512], BF16, ctx1)
        rzb = pr.sb("rzb", [128, 512], BF16, ctx1)
        onat = pr.sb("onat", [128, 2048], BF16, ctx1)
        psSb = [pr.ps(f"psS{i}", [128, 512], F32, ctx1) for i in range(2)]
        psZ = pr.ps("psZ", [128, 512], F32, ctx1)
        psP = pr.ps("psP", [128, 512], F32, ctx1)
        psO = pr.ps("psO", [128, 2048], F32, ctx1)

        for qhalf in range(2):
            for hl in range(8):
                vm0, sm0, pe0 = vm.v, sm.v, pe.v
                nkts = [4 * (qhalf * 4 + g_) + 4 for g_ in range(4)]
                SC = [0] * 5
                PC = [0] * 5
                for g_ in range(4):
                    SC[g_ + 1] = SC[g_] + nkts[g_] + 1
                    PC[g_ + 1] = PC[g_] + 2 * nkts[g_] + 1

                def expv(g_, kt, sm0=sm0, SC=SC):
                    return sm0 + SC[g_] + kt + 1

                def srzv(g_, sm0=sm0, SC=SC, nkts=nkts):
                    return sm0 + SC[g_] + nkts[g_] + 1

                def tsv(g_, kt, pe0=pe0, PC=PC):
                    return pe0 + PC[g_] + (2 * kt if kt >= 1 else 1)

                def tzv(g_, kt, pe0=pe0, PC=PC, nkts=nkts):
                    nkt = nkts[g_]
                    return pe0 + PC[g_] + (2 * kt + 3 if kt <= nkt - 2
                                           else 2 * nkt)

                def trzv(g_, pe0=pe0, PC=PC, nkts=nkts):
                    return pe0 + PC[g_] + 2 * nkts[g_] + 1

                def qh_(q, hl=hl, qhalf=qhalf, vv=vm0):
                    q.wait_ge(vm.h, vv)
                    s_kh.issue(q, hl, [(kh[hl % 2][:],
                                        kT_d[128 * hl:128 * (hl + 1), :], {})])
                    s_vh.issue(q, hl, [(
                        vh[hl % 2][:].rearrange("p (a c) -> p a c", c=128),
                        v_d[:, 128 * hl:128 * (hl + 1)].rearrange(
                            "(a p) c -> p a c", p=128), {})])
                    s_qh.issue(q, hl, [
                        (qhn[hl % 2][:],
                         qT_d[192 * hl:192 * hl + 128,
                              2048 * qhalf:2048 * (qhalf + 1)], {}),
                        (qhr[hl % 2][0:64, :],
                         qT_d[192 * hl + 128:192 * hl + 192,
                              2048 * qhalf:2048 * (qhalf + 1)], {}),
                    ])
                block.sync(qh_)
                kh_t = (s_kh.sems[hl % 2].h, s_kh.sems[hl % 2].v)
                vh_t = (s_vh.sems[hl % 2].h, s_vh.sems[hl % 2].v)
                qh_t = (s_qh.sems[hl % 2].h, s_qh.sems[hl % 2].v)

                def tb(t, hl=hl, qhalf=qhalf, vm0=vm0, nkts=nkts,
                       kh_t=kh_t, vh_t=vh_t, qh_t=qh_t,
                       expv=expv, tzv=tzv, trzv=trzv):
                    t.wait_ge(*kh_t)
                    t.wait_ge(*qh_t)
                    t.wait_ge(*vh_t)
                    t.wait_ge(*const_t)
                    for g_ in range(4):
                        nkt = nkts[g_]

                        def ts_(kt):
                            psS = psSb[kt % 2]
                            if kt >= 2:
                                t.wait_ge(sm.h, expv(g_, kt - 2))
                            elif kt == 0:
                                t.wait_ge(vm.h, vm0 + 6 * g_)
                            t.matmul(psS[:],
                                     kh[hl % 2][:, 128 * kt:128 * (kt + 1)],
                                     qhn[hl % 2][:, 512 * g_:512 * (g_ + 1)],
                                     start=True, stop=False)
                            _mm = t.matmul(psS[:],
                                           krot[:, 128 * kt:128 * (kt + 1)],
                                           qhr[hl % 2][0:64,
                                               512 * g_:512 * (g_ + 1)],
                                           start=False, stop=True)
                            tmark(t, pe, _mm)

                        def tz_(kt):
                            t.wait_ge(sm.h, expv(g_, kt))
                            t.matmul(psZ[0:1, :], ocol[:], pbuf[kt % 2][:],
                                     start=(kt == 0), stop=(kt == nkt - 1))
                            _mm = t.matmul(psP[:],
                                           vh[hl % 2][:, 128 * kt:128 * (kt + 1)],
                                           pbuf[kt % 2][:],
                                           start=(kt == 0), stop=(kt == nkt - 1))
                            tmark(t, pe, _mm)

                        for kt in range(nkt):
                            ts_(kt)
                            if kt >= 1:
                                tz_(kt - 1)
                        tz_(nkt - 1)
                        t.wait_ge(vm.h, vm0 + 6 * g_ + 5)
                        _mm = t.matmul(psSb[0][:], o1[:], rz16[:],
                                       start=True, stop=True)
                        tmark(t, pe, _mm)
                block.tensor(tb)

                def sb_(s, qhalf=qhalf, vm0=vm0, nkts=nkts,
                        tsv=tsv, tzv=tzv, trzv=trzv):
                    for g_ in range(4):
                        nkt = nkts[g_]
                        qg = qhalf * 4 + g_
                        for kt in range(nkt):
                            j = kt - 4 * qg
                            if j >= 0:
                                s.wait_ge(vm.h, vm0 + 6 * g_ + j + 1)
                            else:
                                s.wait_ge(pe.h, tsv(g_, kt))
                            if kt >= 2:
                                s.wait_ge(pe.h, tzv(g_, kt - 2))
                            _a = s.activation(pbuf[kt % 2][:],
                                              psSb[kt % 2][:], AF.Exp)
                            _a.then_inc(sm.h, 1)
                            sm.inc()
                        s.wait_ge(pe.h, trzv(g_))
                        _c = s.copy(rzb[:], psSb[0][:])
                        _c.then_inc(sm.h, 1)
                        sm.inc()
                block.scalar(sb_)

                def vb_(v, hl=hl, qhalf=qhalf, nkts=nkts,
                        tsv=tsv, tzv=tzv, srzv=srzv):
                    for g_ in range(4):
                        nkt = nkts[g_]
                        qg = qhalf * 4 + g_
                        for j in range(4):
                            kt = 4 * qg + j
                            v.wait_ge(pe.h, tsv(g_, kt))
                            _m = v.tensor_tensor(psSb[kt % 2][:],
                                                 psSb[kt % 2][:],
                                                 maskb[:, 512 * j:512 * (j + 1)],
                                                 OP.add)
                            _m.then_inc(vm.h, 1)
                            vm.inc()
                        v.wait_ge(pe.h, tzv(g_, nkt - 1))
                        v.tensor_copy(zrow[:], psZ[0:1, :])
                        v.drain()
                        v.reciprocal(zrow[:], zrow[:])
                        v.drain()
                        _c = v.tensor_copy(rz16[:], zrow[:])
                        _c.then_inc(vm.h, 1)
                        vm.inc()
                        v.wait_ge(sm.h, srzv(g_))
                        _p = v.tensor_tensor(pvt[hl][:, 512 * g_:512 * (g_ + 1)],
                                             psP[:], rzb[:], OP.mult)
                        _p.then_inc(vm.h, 1)
                        vm.inc()
                block.vector(vb_)

            pv_done = vm.v
            pe1, vm1, st1 = pe.v, vm.v, st.v

            def to_(t, qhalf=qhalf, pv=pv_done, obt=ob_t, vm1=vm1):
                t.wait_ge(vm.h, pv)
                t.wait_ge(*obt)
                for qc_ in range(16):
                    if qc_ > 0:
                        t.wait_ge(vm.h, vm1 + qc_)
                    for hl in range(8):
                        for tc in range(4):
                            _mm = t.matmul(psO[:, 512 * tc:512 * (tc + 1)],
                                           pvt[hl][:, 128 * qc_:128 * (qc_ + 1)],
                                           ob[:, 2048 * hl + 512 * tc:
                                              2048 * hl + 512 * (tc + 1)],
                                           start=(hl == 0), stop=(hl == 7))
                    tmark(t, pe, _mm)
            block.tensor(to_)

            def vo_(v, pe1=pe1, st1=st1):
                for qc_ in range(16):
                    v.wait_ge(pe.h, pe1 + qc_ + 1)
                    if st1 + 16 * qc_ > 0:
                        v.wait_ge(st.h, st1 + 16 * qc_)
                    _c = v.tensor_copy(onat[:], psO[:])
                    _c.then_inc(vm.h, 1)
                    vm.inc()
            block.vector(vo_)

            def qo_(q, qhalf=qhalf, vm1=vm1):
                for qc_ in range(16):
                    q.wait_ge(vm.h, vm1 + qc_ + 1)
                    q.dma_start(out=op_o[2048 * qhalf + 128 * qc_:
                                         2048 * qhalf + 128 * (qc_ + 1), :],
                                in_=onat[:]).then_inc(st.h, 16)
                    st.inc(16)
            block.sync(qo_)

        ctx1.close()

        def qfin(q):
            q.wait_ge(st.h, st.v)
        block.sync(qfin)
        pr.finish = [(pe.h, pe.v), (vm.h, vm.v), (sm.h, sm.v), (st.h, st.v)]
        pr.ctx.pop_all()

    return pr



def build_l1c(pr=None, io=None, gate=None):
    if pr is None:
        pr = Prog("l1c")
    nc = pr.nc
    xn2T_d = get_io(pr, io, "xn2T", [D, T], BF16)
    x1T_d = get_io(pr, io, "x1T", [D, T], F32)
    opa_d = get_io(pr, io, "opa", [T, D], BF16)
    wgut_d = get_io(pr, io, "wgut16", [D, 8 * D], BF16)
    wprt_d = get_io(pr, io, "wprt16", [4 * D, D], BF16)
    selm_d = get_io(pr, io, "selm_d", [32, 2048], BF16)
    indm_d = get_io(pr, io, "indm_d", [128, 254], BF16)
    ident_d = get_io(pr, io, "ident_d", [128, 128], BF16)
    ones1 = get_io(pr, io, "ones1", [1, 128], BF16)
    onescol = get_io(pr, io, "onescol", [128, 1], BF16)
    w16rdy = (io or {}).get("w16_ready", [])

    yT_o = get_io(pr, io, "yT_o", [D, T], F32, out=True)
    h_hbm = pr.dtmp("h_hbm", [4 * D, T], BF16)
    wqpT_hbm = pr.dtmp("wqpT_hbm", [4 * D, D], BF16)

    NH = 4 * D
    with pr.ctx:
        o1 = pr.sb("o1", [1, 128], BF16)
        ocol = pr.sb("ocol", [128, 1], BF16)
        ident = pr.sb("ident", [128, 128], BF16)
        selm = pr.sb("selm", [32, 2048], BF16)
        indm = pr.sb("indm", [128, 254], BF16)
        ssq = pr.sb("ssq", [1, T], F32)
        rn16 = pr.sb("rn16", [1, T], BF16)
        RNb = pr.sb("RNb", [128, T], BF16)
        ctxX = ExitStack()
        xn2T = [pr.sb(f"xn2T{i}", [128, T], BF16, ctxX) for i in range(NDT)]

        s_const = Stream(pr, "cst", 1)
        vq = pr.sem("vq")
        vqn = [0]

        def vchain(inst):
            inst.then_inc(vq.h, 1)
            vqn[0] += 1
            vq.inc()
            return vqn[0]

        if pr.block is None:
            pr.block = pr.ctx.enter_context(nc.Block())
        block = pr.block
        gate_all(block, gate)

        def qc(q):
            s_const.issue(q, 0, [
                (o1[:], ones1[:], {}),
                (ocol[:], onescol[:], {}),
                (ident[:], ident_d[:], {}),
                (selm[:], selm_d[:], {}),
                (indm[:], indm_d[:], {}),
            ])
            for dt in range(NDT):
                s_const.issue(q, 0, [(xn2T[dt][:],
                                      xn2T_d[128 * dt:128 * (dt + 1), :], {})])
        block.sync(qc)
        const_t = s_const.totals()[0]
        xn_t = const_t

        ctxB = ExitStack()
        rawB = [pr.sb(f"rawB{i}", [128, D], BF16, ctxB) for i in range(8)]
        absb = [pr.sb(f"absb{i}", [128, D], BF16, ctxB) for i in range(8)]
        thr16B = [pr.sb(f"thr16B{i}", [8, 1024], BF16, ctxB) for i in range(2)]
        thrBB = [pr.sb(f"thrBB{i}", [128, D], BF16, ctxB) for i in range(2)]
        gtb = pr.sb("gtb", [128, 1024], BF16, ctxB)
        ltb = pr.sb("ltb", [128, 1024], BF16, ctxB)
        subb = pr.sb("subb", [128, 1024], BF16, ctxB)
        wqbB = [pr.sb(f"wqbB{i}", [128, D], BF16, ctxB) for i in range(2)]
        psSum = pr.ps("psSum", [8, 1024], F32, ctxB)
        psThr = [pr.ps(f"psThr{i}", [128, 1024], F32, ctxB) for i in range(2)]

        bA = pr.sem("bA")
        bS = pr.sem("bS")
        bRw = pr.sem("bRw")
        bB = pr.sem("bB")
        bM = pr.sem("bM")
        bC = pr.sem("bC")
        s_wp = Stream(pr, "swp", 2)
        s_sq = Stream(pr, "ssq2", 2)
        wq_done = [(s.h, 16 * 32) for s in s_sq.sems]

        def _wp_slot(ip):
            return s_wp.sems[ip % 2].h, 16 * (ip // 2 + 1)

        def gB(g):
            wait_all(g, w16rdy)
            for ip in range(64):
                if ip >= 2:
                    g.wait_ge(bA.h, ip - 1)
                if ip >= 8:
                    g.wait_ge(bC.h, ip - 7)
                s_wp.issue(g, ip, [(rawB[ip % 8][:],
                           wprt_d[128 * ip:128 * (ip + 1), :], {})])
        block.gpsimd(gB)

        def tB(t):
            t.wait_ge(*const_t)
            for Q in range(16):
                for hc in range(2):
                    t.wait_ge(bA.h, 4 * Q + 4)
                    if 2 * Q + hc >= 1:
                        t.wait_ge(bRw.h, 2 * Q + hc)
                    for r in range(4):
                        for c in range(2):
                            _mm = t.matmul(
                                psSum[:, 512 * c:512 * (c + 1)],
                                indm[:, 126 - 2 * r:134 - 2 * r],
                                absb[(4 * Q + r) % 8][:, 1024 * hc + 512 * c:
                                                      1024 * hc + 512 * (c + 1)],
                                start=(r == 0), stop=(r == 3))
                    tmark(t, bS, _mm)
                for r in range(4):
                    ip = 4 * Q + r
                    for hc in range(2):
                        t.wait_ge(bRw.h, 2 * Q + hc + 1)
                        pm = 2 * ip + 1 if hc == 1 else 2 * ip - 2
                        if pm >= 1:
                            t.wait_ge(bM.h, pm)
                        for c in range(2):
                            _mm = t.matmul(
                                psThr[ip % 2][:, 512 * c:512 * (c + 1)],
                                selm[0:8, 128 * r:128 * (r + 1)],
                                thr16B[hc][:, 512 * c:512 * (c + 1)],
                                start=True, stop=True)
                        tmark(t, bB, _mm)
        block.tensor(tB)

        def sB(s):
            for ip in range(64):
                if ip >= 1:
                    s.wait_ge(bC.h, ip)
                    s_sq.issue(s, ip - 1, [(
                        wqpT_hbm[128 * (ip - 1):128 * ip, :],
                        wqbB[(ip - 1) % 2][:], {})])
                for hc in range(2):
                    s.wait_ge(bB.h, 2 * ip + hc + 1)
                    if ip >= 2:
                        s.wait_ge(bC.h, ip - 1)
                    _cp = s.copy(thrBB[ip % 2][:, 1024 * hc:1024 * (hc + 1)],
                                 psThr[ip % 2][:])
                    _cp.then_inc(bM.h, 1)
                    bM.inc()
            s.wait_ge(bC.h, 64)
            s_sq.issue(s, 63, [(wqpT_hbm[128 * 63:128 * 64, :],
                                wqbB[63 % 2][:], {})])
        block.scalar(sB)

        def vB(v):
            for ip in range(64):
                Q, r = ip // 4, ip % 4
                sh, sv = _wp_slot(ip)
                v.wait_ge(sh, sv)
                if ip >= 8:
                    v.wait_ge(bS.h, 2 * ((ip - 8) // 4) + 2)
                _a = v.scalar_tensor_tensor(absb[ip % 8][:], rawB[ip % 8][:],
                                            -1.0, rawB[ip % 8][:],
                                            OP.mult, OP.max)
                _a.then_inc(bA.h, 1)
                bA.inc()
                if r != 3:
                    continue
                for hc in range(2):
                    v.wait_ge(bS.h, 2 * Q + hc + 1)
                    if Q >= 1:
                        v.wait_ge(bB.h, 2 * (4 * Q - 1) + hc + 1)
                    _r1 = v.tensor_scalar(thr16B[hc][:], psSum[:],
                                          1.0 / 64, None, OP.mult)
                    v.wait_ge(vq.h, vchain(_r1))
                    _r2 = v.tensor_scalar(thr16B[hc][:], thr16B[hc][:],
                                          0.5, 5e-9, OP.mult, OP.max)
                    _r2.then_inc(bRw.h, 1)
                    bRw.inc()
                for r2 in range(4):
                    ip2 = 4 * Q + r2
                    for hc in range(2):
                        v.wait_ge(bM.h, 2 * ip2 + hc + 1)
                        if hc == 0 and ip2 >= 2:
                            sl = s_sq.sems[(ip2 - 2) % 2]
                            v.wait_ge(sl.h, 16 * ((ip2 - 2) // 2 + 1))
                        csl = slice(1024 * hc, 1024 * (hc + 1))
                        v.tensor_tensor(gtb[:], rawB[ip2 % 8][:, csl],
                                        thrBB[ip2 % 2][:, csl], OP.is_gt)
                        _l = v.scalar_tensor_tensor(
                            ltb[:], rawB[ip2 % 8][:, csl], -1.0,
                            thrBB[ip2 % 2][:, csl], OP.mult, OP.is_gt)
                        v.wait_ge(vq.h, vchain(_l))
                        _s = v.tensor_tensor(subb[:], gtb[:], ltb[:],
                                             OP.subtract)
                        v.wait_ge(vq.h, vchain(_s))
                        _w = v.scalar_tensor_tensor(
                            wqbB[ip2 % 2][:, csl], subb[:], 2.0,
                            thrBB[ip2 % 2][:, csl], OP.mult, OP.mult)
                        if hc == 1:
                            _w.then_inc(bC.h, 1)
                            bC.inc()
        block.vector(vB)
        ctxB.close()

        ctxC = ExitStack()
        rawC = [[pr.sb(f"rawC{z}_{i}", [128, 4096], BF16, ctxC)
                 for i in range(2)] for z in range(2)]
        absC = [pr.sb(f"absC{z}", [128, 4096], BF16, ctxC) for z in range(2)]
        thr16C = [pr.sb(f"thr16C{z}", [32, 256], BF16, ctxC) for z in range(2)]
        thrBC = [pr.sb(f"thrBC{z}", [128, 4096], BF16, ctxC) for z in range(2)]
        gtc = pr.sb("gtc", [128, 4096], BF16, ctxC)
        ltc = pr.sb("ltc", [128, 4096], BF16, ctxC)
        wqbC = [[pr.sb(f"wqbC{z}_{i}", [128, 4096], BF16, ctxC)
                 for i in range(2)] for z in range(2)]
        sgb = [pr.sb(f"sgb{i}", [128, 512], F32, ctxC) for i in range(2)]
        hjb = [pr.sb(f"hjb{i}", [128, 512], BF16, ctxC) for i in range(2)]
        hsqb = [pr.sb(f"hsqb{i}", [128, 512], BF16, ctxC) for i in range(2)]
        cSum = pr.ps("cSum", [128, 512], F32, ctxC)
        cThr = [pr.ps(f"cThr{i}", [128, 512], F32, ctxC) for i in range(2)]
        psG = [pr.ps(f"psG{i}", [128, 512], F32, ctxC) for i in range(2)]
        psU = [pr.ps(f"psU{i}", [128, 512], F32, ctxC) for i in range(2)]
        psR = pr.ps("psR", [128, 512], F32, ctxC)

        cA = pr.sem("cA")
        cT = pr.sem("cT")
        cQ = pr.sem("cQ")
        cS = pr.sem("cS")
        cB = pr.sem("cB")
        cM = pr.sem("cM")
        cG = pr.sem("cG")
        cL = pr.sem("cL")
        cH = pr.sem("cH")
        cR = pr.sem("cR")
        cZ = pr.sem("cZ")
        s_wgg = Stream(pr, "swgg", 1)
        s_wgq = Stream(pr, "swgq", 1)
        s_hh = Stream(pr, "shh", 2)

        NJP = 32

        def _wg_slot(jp, z):
            st = s_wgg if z == 0 else s_wgq
            return st.sems[0].h, 16 * (jp + 1)

        def _craw(jp, z):
            c0 = 256 * jp + (NH if z == 1 else 0)
            return wgut_d[:, c0:c0 + 256].rearrange("(a p) c -> p a c", p=128)

        def gC(g):
            wait_all(g, w16rdy)
            g.wait_ge(bC.h, 64)
            wait_all(g, wq_done)
            for jp in range(NJP):
                if jp >= 1:
                    g.wait_ge(cA.h, 2 * (jp - 1) + 0 + 1)
                if jp >= 2:
                    g.wait_ge(cQ.h, 2 * (jp - 2) + 0 + 1)
                s_wgg.issue(g, jp, [(
                    rawC[0][jp % 2][:].rearrange("p (a c) -> p a c", c=256),
                    _craw(jp, 0), {})])
        block.gpsimd(gC)

        def qC(q):
            wait_all(q, w16rdy)
            q.wait_ge(bC.h, 64)
            wait_all(q, wq_done)
            for jp in range(NJP):
                if jp >= 1:
                    q.wait_ge(cA.h, 2 * (jp - 1) + 1 + 1)
                if jp >= 2:
                    q.wait_ge(cQ.h, 2 * (jp - 2) + 1 + 1)
                s_wgq.issue(q, jp, [(
                    rawC[1][jp % 2][:].rearrange("p (a c) -> p a c", c=256),
                    _craw(jp, 1), {})])
        block.sync(qC)

        def tC(t):
            t.wait_ge(*const_t)
            t.wait_ge(*xn_t)

            def quant_mm(jp):
                for z in range(2):
                    q2 = 2 * jp + z
                    t.wait_ge(cA.h, q2 + 1)
                    if q2 >= 1:
                        t.wait_ge(cT.h, q2)
                    for a in range(16):
                        _mm = t.matmul(cSum[0:32, 0:256],
                                       indm[:, 126 - 2 * a:158 - 2 * a],
                                       absC[z][:, 256 * a:256 * (a + 1)],
                                       start=(a == 0), stop=(a == 15))
                    tmark(t, cS, _mm)
                    t.wait_ge(cT.h, q2 + 1)
                    for a in range(16):
                        e = 16 * q2 + a
                        if e >= 2:
                            t.wait_ge(cM.h, e - 1)
                        _mm = t.matmul(cThr[a % 2][:, 0:256],
                                       selm[0:32, 128 * a:128 * (a + 1)],
                                       thr16C[z][:], start=True, stop=True)
                        tmark(t, cB, _mm)

            def gu_mm(jp):
                t.wait_ge(cQ.h, 2 * jp + 2)
                for tc in range(4):
                    for mc in range(2):
                        hh = 8 * jp + 2 * tc + mc
                        if hh >= 2:
                            t.wait_ge(cH.h, hh - 1)
                        for a in range(16):
                            t.matmul(psG[mc][:],
                                     wqbC[0][jp % 2][:, 256 * a + 128 * mc:
                                                     256 * a + 128 * (mc + 1)],
                                     xn2T[a][:, 512 * tc:512 * (tc + 1)],
                                     start=(a == 0), stop=(a == 15))
                        for a in range(16):
                            _mm = t.matmul(psU[mc][:],
                                           wqbC[1][jp % 2][:, 256 * a + 128 * mc:
                                                           256 * a + 128 * (mc + 1)],
                                           xn2T[a][:, 512 * tc:512 * (tc + 1)],
                                           start=(a == 0), stop=(a == 15))
                        tmark(t, cG, _mm)
                    rr = 4 * jp + tc
                    if rr >= 1:
                        t.wait_ge(cZ.h, rr)
                    for mc in range(2):
                        t.wait_ge(cH.h, 8 * jp + 2 * tc + mc + 1)
                        _mm = t.matmul(psR[0:1, :], ocol[:], hsqb[mc][:],
                                       start=(mc == 0), stop=(mc == 1))
                    tmark(t, cR, _mm)

            for jp in range(NJP):
                quant_mm(jp)
                if jp >= 1:
                    gu_mm(jp - 1)
            gu_mm(NJP - 1)
        block.tensor(tC)

        def sC(s):
            def hstore(ph):
                s.wait_ge(cH.h, ph + 1)
                jb, tb2, mb = ph // 8, (ph % 8) // 2, ph % 2
                s_hh.issue(s, ph, [(
                    h_hbm[256 * jb + 128 * mb:256 * jb + 128 * (mb + 1),
                          512 * tb2:512 * (tb2 + 1)], hjb[ph % 2][:], {})])

            def silu_sec(jp):
                for tc in range(4):
                    for mc in range(2):
                        hh = 8 * jp + 2 * tc + mc
                        s.wait_ge(cG.h, hh + 1)
                        if hh >= 2:
                            s.wait_ge(cH.h, hh - 1)
                        _ac = s.activation(sgb[hh % 2][:], psG[mc][:], AF.Silu)
                        _ac.then_inc(cL.h, 1)
                        cL.inc()
                        if hh >= 1:
                            hstore(hh - 1)

            for jp in range(NJP):
                for z in range(2):
                    q2 = 2 * jp + z
                    for a in range(16):
                        e = 16 * q2 + a
                        s.wait_ge(cB.h, e + 1)
                        if a == 0 and jp >= 1:
                            s.wait_ge(cQ.h, 2 * (jp - 1) + z + 1)
                        _cp = s.copy(thrBC[z][:, 256 * a:256 * (a + 1)],
                                     cThr[a % 2][:, 0:256])
                        _cp.then_inc(cM.h, 1)
                        cM.inc()
                if jp >= 1:
                    silu_sec(jp - 1)
            silu_sec(NJP - 1)
            hstore(8 * NJP - 1)
        block.scalar(sC)

        def vC(v):
            def h_sec(jp):
                for tc in range(4):
                    for mc in range(2):
                        hh = 8 * jp + 2 * tc + mc
                        v.wait_ge(cL.h, hh + 1)
                        if hh >= 2:
                            sl = s_hh.sems[hh % 2]
                            v.wait_ge(sl.h, 16 * (hh // 2))
                        _h = v.tensor_tensor(hjb[hh % 2][:], sgb[hh % 2][:],
                                             psU[mc][:], OP.mult)
                        hv = vchain(_h)
                        if 4 * jp + tc >= 1:
                            v.wait_ge(cR.h, 4 * jp + tc)
                        v.wait_ge(vq.h, hv)
                        _q = v.tensor_tensor(hsqb[mc][:], hjb[hh % 2][:],
                                             hjb[hh % 2][:], OP.mult)
                        _q.then_inc(cH.h, 1)
                        cH.inc()
                    rr = 4 * jp + tc
                    v.wait_ge(cR.h, rr + 1)
                    if jp >= 1:
                        v.wait_ge(cZ.h, 4 * (jp - 1) + tc + 1)
                    tsl = slice(512 * tc, 512 * (tc + 1))
                    if jp == 0:
                        _z = v.tensor_copy(ssq[0:1, tsl], psR[0:1, :])
                    else:
                        _z = v.tensor_tensor(ssq[0:1, tsl], ssq[0:1, tsl],
                                             psR[0:1, :], OP.add)
                    _z.then_inc(cZ.h, 1)
                    cZ.inc()

            for jp in range(NJP):
                for z in range(2):
                    q2 = 2 * jp + z
                    sh, sv = _wg_slot(jp, z)
                    v.wait_ge(sh, sv)
                    if jp >= 1:
                        v.wait_ge(cS.h, 2 * (jp - 1) + z + 1)
                    _a = v.scalar_tensor_tensor(absC[z][:], rawC[z][jp % 2][:],
                                                -1.0, rawC[z][jp % 2][:],
                                                OP.mult, OP.max)
                    _a.then_inc(cA.h, 1)
                    cA.inc()
                    v.wait_ge(cS.h, q2 + 1)
                    if jp >= 1:
                        v.wait_ge(cB.h, 16 * (2 * (jp - 1) + z) + 16)
                    _r1 = v.tensor_scalar(thr16C[z][:], cSum[0:32, 0:256],
                                          1.0 / 64, None, OP.mult)
                    v.wait_ge(vq.h, vchain(_r1))
                    _r2 = v.tensor_scalar(thr16C[z][:], thr16C[z][:],
                                          0.5, 5e-9, OP.mult, OP.max)
                    _r2.then_inc(cT.h, 1)
                    cT.inc()
                    v.wait_ge(cM.h, 16 * q2 + 16)
                    if jp >= 2:
                        v.wait_ge(cG.h, 8 * (jp - 2) + 8)
                    v.tensor_tensor(gtc[:], rawC[z][jp % 2][:], thrBC[z][:],
                                    OP.is_gt)
                    _l = v.scalar_tensor_tensor(ltc[:], rawC[z][jp % 2][:],
                                                -1.0, thrBC[z][:],
                                                OP.mult, OP.is_gt)
                    v.wait_ge(vq.h, vchain(_l))
                    _s = v.tensor_tensor(gtc[:], gtc[:], ltc[:], OP.subtract)
                    v.wait_ge(vq.h, vchain(_s))
                    _w = v.scalar_tensor_tensor(wqbC[z][jp % 2][:], gtc[:],
                                                2.0, thrBC[z][:],
                                                OP.mult, OP.mult)
                    _w.then_inc(cQ.h, 1)
                    cQ.inc()
                if jp >= 1:
                    h_sec(jp - 1)
            h_sec(NJP - 1)
        block.vector(vC)
        ctxC.close()
        ctxX.close()

        ctxR = ExitStack()
        psB2 = [pr.ps(f"psB2{i}", [128, 512], F32, ctxR) for i in range(2)]
        CT0, CL0, CR0, CM0 = 2 * NJP, 8 * NJP, 4 * NJP, 32 * NJP

        def vR(v):
            v.wait_ge(cZ.h, 4 * NJP)
            _r = v.tensor_scalar(ssq[:], ssq[:], 1.0 / NH, 1.1920929e-7,
                                 OP.mult, OP.add)
            _r.then_inc(cT.h, 1)
            cT.inc()
            v.wait_ge(cL.h, CL0 + 1)
            _c = v.tensor_copy(rn16[:], ssq[:])
            _c.then_inc(cT.h, 1)
            cT.inc()
        block.vector(vR)

        def sR(s):
            s.wait_ge(cT.h, CT0 + 1)
            s.activation(ssq[:], ssq[:], AF.Ln)
            s.drain()
            _e = s.activation(ssq[:], ssq[:], AF.Exp, scale=-0.5)
            _e.then_inc(cL.h, 1)
            cL.inc()
            for tcc in range(4):
                s.wait_ge(cR.h, CR0 + tcc + 1)
                _cp = s.copy(RNb[:, 512 * tcc:512 * (tcc + 1)],
                             psB2[tcc % 2][:])
                _cp.then_inc(cM.h, 1)
                cM.inc()
        block.scalar(sR)

        def tR(t):
            t.wait_ge(cT.h, CT0 + 2)
            for tcc in range(4):
                if tcc >= 2:
                    t.wait_ge(cM.h, CM0 + tcc - 1)
                _mm = t.matmul(psB2[tcc % 2][:], o1[:],
                               rn16[:, 512 * tcc:512 * (tcc + 1)],
                               start=True, stop=True)
                tmark(t, cR, _mm)
        block.tensor(tR)
        ctxR.close()

        ctxD = ExitStack()
        TD = 1024
        hn = [pr.sb(f"hn{i}", [128, TD], BF16, ctxD) for i in range(64)]
        wpb = [pr.sb(f"wpb{i}", [128, NH], BF16, ctxD) for i in range(2)]
        x1f = [pr.sb(f"x1f{i}", [128, TD], F32, ctxD) for i in range(2)]
        anb = [pr.sb(f"anb{i}", [128, TD], BF16, ctxD) for i in range(2)]
        fo = [pr.sb(f"fo{i}", [128, TD], F32, ctxD) for i in range(2)]
        psO = [pr.ps(f"psO{i}", [128, TD], F32, ctxD) for i in range(2)]
        dH = pr.sem("dH")
        dT = pr.sem("dT")
        dV = pr.sem("dV")
        s_hn = Stream(pr, "shn", 2)
        s_wpbg = Stream(pr, "swbg", 1)
        s_wpbq = Stream(pr, "swbq", 1)
        s_xa = Stream(pr, "sxa", 2)
        s_y = Stream(pr, "sy", 2)

        def sD(s):
            for tc2 in range(2):
                ts_ = slice(TD * tc2, TD * (tc2 + 1))
                for ht in range(64):
                    g2 = 64 * tc2 + ht
                    if ht == 0:
                        s.wait_ge(cM.h, CM0 + 4)
                        if tc2 == 0:
                            wait_all(s, s_hh.totals())
                        else:
                            s.wait_ge(dT.h, 16 * tc2)
                    if g2 >= 2:
                        s.wait_ge(dH.h, g2 - 1)
                    s_hn.issue(s, g2, [(hn[ht][:],
                                h_hbm[128 * ht:128 * (ht + 1), ts_], {})])
                for dc in range(16):
                    pp = 16 * tc2 + dc
                    s.wait_ge(dV.h, pp + 1)
                    s_y.issue(s, pp, [(yT_o[128 * dc:128 * (dc + 1), ts_],
                                       fo[dc % 2][:], {})])
        block.scalar(sD)

        def vD(v):
            for tc2 in range(2):
                ts_ = slice(TD * tc2, TD * (tc2 + 1))
                for ht in range(64):
                    g2 = 64 * tc2 + ht
                    sl = s_hn.sems[g2 % 2]
                    v.wait_ge(sl.h, 16 * (g2 // 2 + 1))
                    if g2 == 0:
                        v.wait_ge(cM.h, CM0 + 4)
                    _m = v.tensor_tensor(hn[ht][:], hn[ht][:], RNb[:, ts_],
                                         OP.mult)
                    _m.then_inc(dH.h, 1)
                    dH.inc()
                for dc in range(16):
                    pp = 16 * tc2 + dc
                    v.wait_ge(dT.h, pp + 1)
                    sl = s_xa.sems[dc % 2]
                    v.wait_ge(sl.h, 32 * (pp // 2 + 1))
                    if pp >= 2:
                        sy = s_y.sems[(pp - 2) % 2]
                        v.wait_ge(sy.h, 16 * ((pp - 2) // 2 + 1))
                    _f = v.tensor_tensor(fo[dc % 2][:], psO[dc % 2][:],
                                         x1f[dc % 2][:], OP.add)
                    _f.then_inc(dV.h, 1)
                    dV.inc()
        block.vector(vD)

        def gD(g):
            g.wait_ge(cM.h, CM0 + 4)
            wait_all(g, wq_done)
            for tc2 in range(2):
                ts_ = slice(TD * tc2, TD * (tc2 + 1))
                for dc in range(16):
                    pp = 16 * tc2 + dc
                    if dc % 2 == 0:
                        if pp >= 2:
                            g.wait_ge(dT.h, pp - 1)
                        s_wpbg.issue(g, 0, [(
                            wpb[0][:].rearrange("p (a c) -> p a c", c=128),
                            wqpT_hbm[:, 128 * dc:128 * (dc + 1)].rearrange(
                                "(a p) c -> p a c", p=128), {})])
                    if pp >= 2:
                        g.wait_ge(dV.h, pp - 1)
                    s_xa.issue(g, dc, [
                        (x1f[dc % 2][:],
                         x1T_d[128 * dc:128 * (dc + 1), ts_], {}),
                        (anb[dc % 2][:].rearrange("p (a c) -> p a c", c=128),
                         opa_d[ts_, 128 * dc:128 * (dc + 1)].rearrange(
                             "(a p) c -> p a c", p=128), {})])
        block.gpsimd(gD)

        def qD(q):
            q.wait_ge(cM.h, CM0 + 4)
            wait_all(q, wq_done)
            for tc2 in range(2):
                for dc in range(1, 16, 2):
                    pp = 16 * tc2 + dc
                    if pp >= 2:
                        q.wait_ge(dT.h, pp - 1)
                    s_wpbq.issue(q, 0, [(
                        wpb[1][:].rearrange("p (a c) -> p a c", c=128),
                        wqpT_hbm[:, 128 * dc:128 * (dc + 1)].rearrange(
                            "(a p) c -> p a c", p=128), {})])
        block.sync(qD)

        def tD(t):
            t.wait_ge(*const_t)
            for tc2 in range(2):
                for dc in range(16):
                    pp = 16 * tc2 + dc
                    if dc == 0:
                        t.wait_ge(dH.h, 64 * tc2 + 64)
                    stw = s_wpbg if dc % 2 == 0 else s_wpbq
                    t.wait_ge(stw.sems[0].h, 16 * (pp // 2 + 1))
                    sa = s_xa.sems[dc % 2]
                    t.wait_ge(sa.h, 32 * (pp // 2 + 1))
                    if pp >= 2:
                        t.wait_ge(dV.h, pp - 1)
                    for ht in range(64):
                        for c in range(2):
                            t.matmul(psO[dc % 2][:, 512 * c:512 * (c + 1)],
                                     wpb[dc % 2][:, 128 * ht:128 * (ht + 1)],
                                     hn[ht][:, 512 * c:512 * (c + 1)],
                                     start=(ht == 0), stop=False)
                    for j in range(8):
                        _mm = t.matmul(psO[dc % 2][:, 128 * j:128 * (j + 1)],
                                       anb[dc % 2][:, 128 * j:128 * (j + 1)],
                                       ident[:], start=False,
                                       stop=(j in (3, 7)))
                    tmark(t, dT, _mm)
        block.tensor(tD)

        def qfin(q):
            wait_all(q, s_y.totals())
        block.sync(qfin)
        pr.finish = (s_y.totals() + s_hh.totals() +
                     [(dT.h, dT.v), (dV.h, dV.v), (dH.h, dH.v),
                      (cM.h, cM.v), (cZ.h, cZ.v), (bC.h, bC.v)])
        ctxD.close()
        pr.ctx.pop_all()

    return pr



RG_PAIR = [[0, 1], [2, 3], [4, 5], [6, 7]]


def build_fused(nphase=5, nocoll=False):
    pr = Prog("fz")
    nc = pr.nc
    io_sh = {
        "ones1": pr.din("ones1", [1, 128], BF16),
        "onescol": pr.din("onescol", [128, 1], BF16),
        "ident_d": pr.din("ident_d", [128, 128], BF16),
    }
    ins = {
        "xT": pr.din("xT", [D, T], F32),
        "winT": pr.din("winT", [D, 3 * D], F32),
        "binp": pr.din("binp", [128, 48], F32),
        "cosT": pr.din("cosT", [512, T], BF16),
        "sinT": pr.din("sinT", [512, T], BF16),
        "wq": pr.din("wq", [D, 512], F32),
        "keysT": pr.din("keysT", [8, 64, 128], F32),
        "iota64": pr.din("iota64", [128, 64], F32),
        "ecat": pr.din("ecat", [16384, 4096], F32),
        "woutT": pr.din("woutT", [D, D], F32),
        "bout_d": pr.din("bout_d", [128, 16], F32),
        "kvaT": pr.din("kvaT", [D, 576], F32),
        "lnw_d": pr.din("lnw_d", [128, 4], F32),
        "kvbn": pr.din("kvbn", [512, 2048], F32),
        "kvbv": pr.din("kvbv", [512, 2048], F32),
        "qpT": pr.din("qpT", [D, 1536], F32),
        "opT": pr.din("opT", [1024, D], F32),
        "masks": pr.din("masks", [4, 128, 512], F32),
        "wguT": pr.din("wguT", [D, 8 * D], F32),
        "wprT": pr.din("wprT", [4 * D, D], F32),
        "selm_d": pr.din("selm_d", [32, 2048], BF16),
        "indm_d": pr.din("indm_d", [128, 254], BF16),
    }
    yT_o = pr.dout("yT_o", [D, T], F32)

    scr = {
        "xnT_s": pr.dtmp("xnT_s", [D, T], BF16),
        "qT_s": pr.dtmp("qT_s", [D, T], BF16),
        "apart_s": pr.dtmp("apart_s", [128, 16], F32),
        "apart_sum": pr.dtmp("apart_sum", [128, 16], F32),
        "fi_s": pr.dtmp("fi_s", [16, 128, 32], U32),
        "fs_s": pr.dtmp("fs_s", [16, 128, 32], F32),
        "x1T_s": pr.dtmp("x1T_s", [D, T], F32),
        "x1s2": pr.dtmp("x1s2", [D, T], F32),
        "xn2T_s": pr.dtmp("xn2T_s", [D, T], BF16),
        "xn2f_ag": pr.dtmp("xn2f_ag", [2 * D, T], BF16),
        "kT_s": pr.dtmp("kT_s", [2048, S], BF16),
        "krot_s": pr.dtmp("krot_s", [64, S], BF16),
        "v_s": pr.dtmp("v_s", [S, 2048], BF16),
        "q2T_s": pr.dtmp("q2T_s", [1536, S], BF16),
        "op_s": pr.dtmp("op_s", [S, D], BF16),
        "opsum_s": pr.dtmp("opsum_s", [T, D], BF16),
    }

    scr["ecat16"] = pr.dtmp("ecat16", [16384, 4096], BF16)
    scr["wguT16"] = pr.dtmp("wguT16", [D, 8 * D], BF16)
    scr["wprT16"] = pr.dtmp("wprT16", [4 * D, D], BF16)
    gbar = pr.sem("gbar")
    ccA = pr.sem("ccA")
    ccG = pr.sem("ccG")
    ccR = pr.sem("ccR")
    cast_s = pr.sem("cast_s")
    cast_w = pr.sem("cast_w")

    pr.block = pr.ctx.enter_context(pr.nc.Block())

    def gcast(g):
        for ch in range(16):
            g.dma_start(out=scr["ecat16"][1024 * ch:1024 * (ch + 1), :],
                        in_=ins["ecat"][1024 * ch:1024 * (ch + 1), :]
                        ).then_inc(cast_s.h, 16)
            cast_s.inc(16)
    pr.block.gpsimd(gcast)
    cast_t = (cast_s.h, cast_s.v)

    def wcast(g):
        for ch in range(4):
            g.dma_start(out=scr["wprT16"][2048 * ch:2048 * (ch + 1), :],
                        in_=ins["wprT"][2048 * ch:2048 * (ch + 1), :]
                        ).then_inc(cast_w.h, 16)
            cast_w.inc(16)
        for ch in range(4):
            g.dma_start(out=scr["wguT16"][512 * ch:512 * (ch + 1), :],
                        in_=ins["wguT"][512 * ch:512 * (ch + 1), :]
                        ).then_inc(cast_w.h, 16)
            cast_w.inc(16)
    cast_wt = (cast_w.h, 16 * 8)

    def phase(fn, io, coll=None, extra_fin=None):
        pr.pctx = ExitStack()
        fn(pr=pr, io=io, gate=(gbar.h, gbar.v) if gbar.v else None)
        pr.pctx.close()
        pr.pctx = None
        block = pr.block

        def qbar(q, fin=pr.finish + (extra_fin or [])):
            wait_all(q, fin)
            q.nop().then_inc(gbar.h, 1)
        block.sync(qbar)
        gbar.inc()

        if coll is not None and nocoll:
            coll = None
        if coll is not None:
            kind, op_, cin, cout, csem = coll

            def gcc(g, gv=gbar.v):
                g.wait_ge(gbar.h, gv)
                g.collective_compute(kind, op_, ins=[cin.opt()], outs=[cout.opt()],
                                     replica_groups=RG_PAIR).then_inc(csem.h, 1)
            block.gpsimd(gcc)
            csem.inc(1)

            def qcc(q, cs=(csem.h, csem.v)):
                q.wait_ge(*cs)
                q.nop().then_inc(gbar.h, 1)
            block.sync(qcc)
            gbar.inc()

    pr.prefix = "a_"
    phase(build_l0a,
          {**io_sh, **{k: ins[k] for k in ["xT", "winT", "binp", "cosT", "sinT",
                                           "wq", "keysT", "iota64"]},
           "xnT_o": scr["xnT_s"], "qT_o": scr["qT_s"], "apart_o": scr["apart_s"],
           "fi_o": scr["fi_s"], "fs_o": scr["fs_s"]},
          coll=("AllReduce", OP.add, scr["apart_s"], scr["apart_sum"], ccA),
          extra_fin=[cast_t])

    if nphase <= 1:
        return pr
    pr.prefix = "b_"
    phase(build_l0b,
          {**io_sh, "xT": ins["xT"], "xnT_i": scr["xnT_s"], "qT_i": scr["qT_s"],
           "ap0": scr["apart_sum"], "fi_i": scr["fi_s"], "fs_i": scr["fs_s"],
           "ecat": scr["ecat16"], "woutT": ins["woutT"], "bout_d": ins["bout_d"],
           "x1T_o": scr["x1T_s"], "xn2T_o": scr["xn2T_s"], "x1s": scr["x1s2"]},
          coll=("AllGather", OP.bypass, scr["xn2T_s"], scr["xn2f_ag"], ccG))
    pr.block.gpsimd(wcast)

    if nphase <= 2:
        return pr
    pr.prefix = "c_"
    phase(build_l1a,
          {**io_sh, "xn2f_ag": scr["xn2f_ag"],
           **{k: ins[k] for k in ["kvaT", "lnw_d", "kvbn", "kvbv", "qpT"]},
           "kT_o": scr["kT_s"], "krotT_o": scr["krot_s"], "v_o": scr["v_s"],
           "q2T_o": scr["q2T_s"]})

    if nphase <= 3:
        return pr
    pr.prefix = "d_"
    phase(build_l1b,
          {**io_sh, "kT": scr["kT_s"][0:1024, :], "krotT": scr["krot_s"],
           "v": scr["v_s"][:, 0:1024], "qT": scr["q2T_s"], "opT": ins["opT"],
           "masks": ins["masks"], "op_o": scr["op_s"]},
          coll=("ReduceScatter", OP.add, scr["op_s"], scr["opsum_s"], ccR))

    if nphase <= 4:
        return pr
    pr.prefix = "e_"
    phase(build_l1c,
          {**io_sh, "xn2T": scr["xn2T_s"], "x1T": scr["x1T_s"],
           "opa": scr["opsum_s"], "wgut16": scr["wguT16"],
           "wprt16": scr["wprT16"], "selm_d": ins["selm_d"],
           "indm_d": ins["indm_d"], "w16_ready": [cast_wt],
           "yT_o": yT_o})

    return pr


def fused_inmaps(prep):
    cos, sin = prep["cosT"], prep["sinT"]
    maps = []
    for c in range(NCORE):
        r = c % 2
        horder = list(range(r * 8, r * 8 + 8)) + list(range((1 - r) * 8,
                                                            (1 - r) * 8 + 8))
        cols = np.concatenate([np.arange(128) + 128 * h for h in horder])
        maps.append({
            "xT": prep["xT"][c],
            "winT": prep["winT"], "binp": prep["binp"],
            "cosT": np.ascontiguousarray(cos[:, r * T:(r + 1) * T]),
            "sinT": np.ascontiguousarray(sin[:, r * T:(r + 1) * T]),
            "wq": prep["wq"], "keysT": prep["keysT"], "iota64": prep["iota64"],
            "ecat": prep["e_cat"], "woutT": prep["woutT"], "bout_d": prep["bout"],
            "kvaT": prep["kvaT"], "lnw_d": prep["lnw"],
            "kvbn": np.ascontiguousarray(prep["kvbT_nope"][:, cols]),
            "kvbv": np.ascontiguousarray(prep["kvbT_v"][:, cols]),
            "qpT": np.ascontiguousarray(prep["qprojT"][:, r * 1536:(r + 1) * 1536]),
            "opT": np.ascontiguousarray(prep["oprojT"][r * 1024:(r + 1) * 1024]),
            "masks": prep["masks"],
            "wguT": prep["wguT"], "wprT": prep["wprT"],
            "selm_d": prep["selm"], "indm_d": prep["indm"],
            "ones1": prep["ones1"], "onescol": prep["onescol"],
            "ident_d": prep["ident"],
        })
    return maps


_PROGS = {}


def _prog(name):
    if name not in _PROGS:
        _PROGS[name] = {"l0a": build_l0a, "l0b": build_l0b, "l1a": build_l1a,
                        "l1b": build_l1b, "l1c": build_l1c}[name]()
    return _PROGS[name]


def _run(name, maps):
    pr = _prog(name)
    res = run_bass_kernel_spmd(pr.nc, maps, core_ids=list(range(NCORE)))
    return res.results


MODE = "fused"


def kernel(**inputs):
    prep = host_prep(inputs)
    if MODE == "fused":
        if "fused" not in _PROGS:
            _PROGS["fused"] = build_fused()
        pr = _PROGS["fused"]
        res = run_bass_kernel_spmd(pr.nc, fused_inmaps(prep),
                                   core_ids=list(range(NCORE)))
        out = np.empty((4, S, D), np.float32)
        for c in range(NCORE):
            b, r = c // 2, c % 2
            out[b, r * T:(r + 1) * T, :] = res.results[c]["yT_o"].T
        return out
    raise RuntimeError("only MODE='fused' is supported")
